# version 04
# baseline (speedup 1.0000x reference)
import contextlib
import numpy as np
import ml_dtypes
import concourse.bass as bass
import concourse.mybir as mybir
from concourse.alu_op_type import AluOpType
from concourse.bass_utils import run_bass_kernel_spmd

F32 = mybir.dt.float32
BF16 = mybir.dt.bfloat16
I32 = mybir.dt.int32
U32 = mybir.dt.uint32
AF = mybir.ActivationFunctionType

D = 2048
T = 2048
NCH = 16
NCORES = 8
IFF = 5504
NIT = IFF // 128
NEG = -1.0e30

ENGINES = ("sync", "tensor", "vector", "scalar", "gpsimd")

LAUNCHES = [["ffn0_a", "attn0", "ffn0_b", "moe_a", "attn1", "moe_b"]]


class Ctx:
    def __init__(self, nc):
        self.nc = nc
        self.q = {e: [] for e in ENGINES}
        self.ctr = {}
        self.sem_names = []
        self.uid = 0

    def name(self, stem):
        self.uid += 1
        return f"{stem}_{self.uid}"

    def sem(self, name):
        if name not in self.ctr:
            self.ctr[name] = 0
            self.sem_names.append(name)
        return name

    def inc(self, name, n=1):
        self.ctr[name] += n
        return self.ctr[name]

    def val(self, name):
        return self.ctr[name]

    def wait(self, eng, sname, value=None):
        if value is None:
            sname, value = sname
        if value <= 0:
            return
        assert value <= self.ctr.get(sname, 0), (eng, sname, value, self.ctr.get(sname, 0))
        self.q[eng].append(lambda E, S: E.wait_ge(S[sname], value))

    def dma(self, eng, out, in_, sname="ld", **kw):
        self.sem(sname)
        tgt = self.inc(sname, 16)
        self.q[eng].append(lambda E, S: E.dma_start(out=out, in_=in_, **kw).then_inc(S[sname], 16))
        return (sname, tgt)

    def dmat(self, eng, out, in_, sname="ld"):
        self.sem(sname)
        tgt = self.inc(sname, 16)
        self.q[eng].append(lambda E, S: E.dma_start_transpose(out, in_).then_inc(S[sname], 16))
        return (sname, tgt)

    def op(self, eng, fn, sname=None, n=1):
        if sname is None:
            self.q[eng].append(lambda E, S: fn(E))
            return None
        tgt = self.inc(sname, n)
        def run(E, S):
            fn(E).then_inc(S[sname], n)
        self.q[eng].append(run)
        return tgt

    def phase(self):
        self.barrier()

    def barrier(self):
        snap = {sn: v for sn, v in self.ctr.items() if v > 0}
        for e in ENGINES:
            for sn, v in snap.items():
                self.wait(e, sn, v)



def emit_ln(C, nc, R, K, out_mbc, out_rbc, SQ, out_rows=None):
    with (nc.psum_tensor(C.name("psln"), [128, 2048], F32) as ps,
          nc.sbuf_tensor(C.name("rows"), [1, 2 * T], F32) as ROWS):
        ONESC, ONES1F = K["ONESC"], K["ONES1F"]
        pe0 = None
        for c in range(NCH):
            for j in range(4):
                pe0 = C.op("tensor", lambda E, c=c, j=j: E.matmul(
                    ps[0:1, j * 512:(j + 1) * 512], ONESC[:],
                    R[:, c * T + j * 512:c * T + (j + 1) * 512],
                    start=(c == 0), stop=(c == NCH - 1)),
                    "pe" if (c == NCH - 1 and j == 3) else None)
        C.wait("vector", "pe", pe0)
        e1 = C.op("vector", lambda E: E.tensor_copy(ROWS[0:1, 0:T], ps[0:1, 0:T]), "dv")
        sq_ms = {}
        pe_ms = {}
        for c in range(NCH):
            buf = SQ[:, (c % 2) * T:(c % 2) * T + T]
            if c >= 2:
                C.wait("scalar", "pe", pe_ms[c - 2])
            sq_ms[c] = C.op("scalar", lambda E, buf=buf, c=c: E.activation(
                buf, R[:, c * T:(c + 1) * T], AF.Square), "ac")
            C.wait("tensor", "ac", sq_ms[c])
            if c == 0:
                C.wait("tensor", "dv", e1)
            for j in range(4):
                pe_ms[c] = C.op("tensor", lambda E, buf=buf, c=c, j=j: E.matmul(
                    ps[0:1, j * 512:(j + 1) * 512], ONESC[:],
                    buf[:, j * 512:(j + 1) * 512],
                    start=(c == 0), stop=(c == NCH - 1)),
                    "pe" if j == 3 else None)
        C.wait("vector", "pe", pe_ms[NCH - 1])
        e2 = C.op("vector", lambda E: E.tensor_copy(ROWS[0:1, T:2 * T], ps[0:1, 0:T]), "dv")
        C.wait("vector", "dv", e2)
        d1 = C.op("vector", lambda E: E.tensor_scalar(ROWS[0:1, 0:T], ROWS[0:1, 0:T],
                  1.0 / D, None, op0=AluOpType.mult), "dv")
        C.wait("vector", "dv", d1)
        d2 = C.op("vector", lambda E: E.tensor_scalar(ROWS[0:1, T:2 * T], ROWS[0:1, T:2 * T],
                  1.0 / D, None, op0=AluOpType.mult), "dv")
        C.wait("vector", "dv", d2)
        d3 = C.op("vector", lambda E: E.tensor_tensor(ps[32:33, 0:T], ROWS[0:1, 0:T],
                  ROWS[0:1, 0:T], op=AluOpType.mult), "dv")
        C.wait("vector", "dv", d3)
        d4 = C.op("vector", lambda E: E.tensor_tensor(ROWS[0:1, T:2 * T], ROWS[0:1, T:2 * T],
                  ps[32:33, 0:T], op=AluOpType.subtract), "dv")
        if os.environ.get("DBG_LN") == "1":
            dbg_r = nc.dram_tensor(C.name("dbg_rows"), [2, T], F32,
                                   kind="ExternalOutput").ap()
            C.wait("sync", "dv", e2)
            dr = C.dma("sync", dbg_r[1:2, :], ROWS[0:1, T:2 * T], "l_dbg2")
            C.wait("sync", "dv", d4)
            dr = C.dma("sync", dbg_r[0:1, :], ROWS[0:1, 0:T], "l_dbg2")
            C.wait("vector", dr)
        C.wait("scalar", "dv", d4)
        a1 = C.op("scalar", lambda E: E.activation(ROWS[0:1, T:2 * T], ROWS[0:1, T:2 * T],
                  AF.Ln, bias=K["EPS"][0:1, :]), "ac")
        C.wait("scalar", "ac", a1)
        a2 = C.op("scalar", lambda E: E.activation(ROWS[0:1, T:2 * T], ROWS[0:1, T:2 * T],
                  AF.Exp, scale=-0.5), "ac")
        C.wait("tensor", "dv", d4)
        for j in range(4):
            pb = C.op("tensor", lambda E, j=j: E.matmul(
                ps[:, j * 512:(j + 1) * 512], ONES1F[:],
                ROWS[0:1, j * 512:(j + 1) * 512], start=True, stop=True),
                "pe" if j == 3 else None)
        C.wait("vector", "pe", pb)
        em = C.op("vector", lambda E: E.tensor_copy(out_mbc[:], ps[:, 0:T]), "dv")
        C.wait("tensor", "ac", a2)
        C.wait("tensor", "dv", em)
        for j in range(4):
            pb2 = C.op("tensor", lambda E, j=j: E.matmul(
                ps[:, j * 512:(j + 1) * 512], ONES1F[:],
                ROWS[0:1, T + j * 512:T + (j + 1) * 512], start=True, stop=True),
                "pe" if j == 3 else None)
        C.wait("vector", "pe", pb2)
        er = C.op("vector", lambda E: E.tensor_copy(out_rbc[:], ps[:, 0:T]), "dv")
        if out_rows is not None:
            C.wait("vector", "dv", er)
            er = C.op("vector", lambda E: E.tensor_copy(out_rows[0:1, 0:T],
                      ROWS[0:1, 0:T]), "dv")
            C.wait("vector", "dv", er)
            er = C.op("vector", lambda E: E.tensor_copy(out_rows[0:1, T:2 * T],
                      ROWS[0:1, T:2 * T]), "dv")
    return er


def emit_xln(C, nc, R, m_bc, r_bc, XLN, dv_ready):
    C.wait("vector", "dv", dv_ready)
    with nc.sbuf_tensor(C.name("xtmp"), [128, 2 * T], F32) as TMP:
        ms = {}
        for c in range(NCH):
            tb = TMP[:, (c % 2) * T:(c % 2) * T + T]
            if c >= 2:
                C.wait("vector", "dv", ms[c - 2])
            t1 = C.op("vector", lambda E, tb=tb, c=c: E.tensor_tensor(
                tb, R[:, c * T:(c + 1) * T], m_bc[:], op=AluOpType.subtract), "dv")
            C.wait("vector", "dv", t1)
            ms[c] = C.op("vector", lambda E, tb=tb, c=c: E.tensor_tensor(
                XLN[:, c * T:(c + 1) * T], tb, r_bc[:], op=AluOpType.mult), "dv")
    return ms[NCH - 1]



def emit_ghost_prep(C, nc, ext, K, state):
    if "idxbf" in state:
        idxg, idxu, idxd = state["idxbf"]
    else:
        idxg = nc.dram_tensor("idxbf_g", [IFF, D], BF16).ap()
        idxu = nc.dram_tensor("idxbf_u", [IFF, D], BF16).ap()
        idxd = nc.dram_tensor("idxbf_d", [D, IFF], BF16).ap()
        state["idxbf"] = (idxg, idxu, idxd)
        kindg = "ExternalOutput" if os.environ.get("DBG_GHOST") == "1" else "Internal"
        state["wimg"] = {
            "g": nc.dram_tensor("wimg_g", [NIT, 128, 16 * 128], BF16, kind=kindg).ap(),
            "u": nc.dram_tensor("wimg_u", [NIT, 128, 16 * 128], BF16, kind=kindg).ap(),
            "d": nc.dram_tensor("wimg_d", [NCH, 128, NIT * 128], BF16, kind=kindg).ap(),
        }
    with nc.sbuf_tensor(C.name("stg"), [128, 2 * IFF], BF16) as STG:
        rounds = []
        for (src, dst, nrow, ncol) in ((ext["L0_ffn_gate_idx"], idxg, NIT, D),
                                       (ext["L0_ffn_up_idx"], idxu, NIT, D),
                                       (ext["L0_ffn_down_idx"], idxd, NCH, IFF)):
            for r in range(nrow):
                rounds.append((src, dst, r, ncol))
        st_ms = {}
        for i, (src, dst, r, ncol) in enumerate(rounds):
            buf = STG[:, (i % 2) * IFF:(i % 2) * IFF + ncol]
            if i >= 2:
                C.wait("gpsimd", st_ms[i - 2])
            ld = C.dma("gpsimd", buf, src[r * 128:(r + 1) * 128, :], f"g_stg{i % 2}")
            C.wait("sync", ld)
            st_ms[i] = C.dma("sync", dst[r * 128:(r + 1) * 128, :], buf, f"l_stg{i % 2}")
    LUTBC = state["LUTBC"]
    with (nc.sbuf_tensor(C.name("lutrow"), [1, 16], F32) as LR,
          nc.psum_tensor(C.name("pslut"), [128, 16], F32) as PSL):
        ld = C.dma("sync", LR[:], ext["L0_ffn_lut"], "l_const")
        C.wait("tensor", ld)
        pm = C.op("tensor", lambda E: E.matmul(PSL[:], K["ONES1F"][:], LR[:],
                  start=True, stop=True), "pe")
        C.wait("vector", "pe", pm)
        lb = C.op("vector", lambda E: E.tensor_copy(LUTBC[:], PSL[:]), "dv")
        C.wait("sync", st_ms[len(rounds) - 1])
        C.wait("sync", st_ms[len(rounds) - 2])
        C.wait("tensor", "dv", lb)
    C.phase()


def emit_ghost_select(C, nc, IDXT, OUT, TBUF, LUTBC, width):
    last = None
    for l in range(16):
        lc = LUTBC[:, l:l + 1]
        if l == 0:
            last = C.op("vector", lambda E, lc=lc: E.tensor_scalar(
                OUT[:, :width], IDXT[:, :width], float(0), lc,
                op0=AluOpType.is_equal, op1=AluOpType.mult), "dv")
        else:
            C.wait("vector", "dv", last)
            t1 = C.op("vector", lambda E, lc=lc, l=l: E.tensor_scalar(
                TBUF[:, :width], IDXT[:, :width], float(l), lc,
                op0=AluOpType.is_equal, op1=AluOpType.mult), "dv")
            C.wait("vector", "dv", t1)
            last = C.op("vector", lambda E: E.tensor_tensor(
                OUT[:, :width], OUT[:, :width], TBUF[:, :width], op=AluOpType.add), "dv")
    return last


import os
FORCE_JIT = os.environ.get("FORCE_JIT") == "1"




def emit_ep(C, nc, state, R, psum_ap, scale, dt, col0, width, buf, final, sname):
    ld, src_ap, dst_ap = state["ep_pend"].pop((dt, col0))
    C.wait("vector", ld)
    s1 = C.op("vector", lambda E: E.scalar_tensor_tensor(
        buf, in0=psum_ap, scalar=scale, in1=buf,
        op0=AluOpType.mult, op1=AluOpType.add), "dv")
    C.wait("sync", "dv", s1)
    stm = C.dma("sync", dst_ap, buf, sname)
    s2 = None
    if final:
        C.wait("vector", "dv", s1)
        s2 = C.op("vector", lambda E: E.tensor_copy(
            R[:, dt * T + col0:dt * T + col0 + width], buf), "dv")
    return stm, (s2 if s2 is not None else s1)


def emit_ep_load(C, nc, state, dt, col0, width, buf, sname):
    src = state["r32_src"]
    dst = state["r32_dst"]
    src_ap = (src[dt * 128:(dt + 1) * 128, col0:col0 + width])
    dst_ap = (dst[dt * 128:(dt + 1) * 128, col0:col0 + width])
    ld = C.dma("sync", buf, src_ap, sname)
    state["ep_pend"][(dt, col0)] = (ld, src_ap, dst_ap)


def emit_ffn0(C, nc, ext, R, K, state):
    state["r32_written"] = True
    first = not state.get("ghost_ready", False) or FORCE_JIT
    if not state.get("ghost_ready", False):
        emit_ghost_prep(C, nc, ext, K, state)
        state["ghost_ready"] = True
    idxg, idxu, idxd = state["idxbf"]
    wim = state["wimg"]
    LUTBC = state["LUTBC"]
    if "h_dram" not in state:
        kindh = "ExternalOutput" if os.environ.get("DBG_GHOST") == "1" else "Internal"
        state["h_dram"] = nc.dram_tensor("h_dram", [IFF, T], BF16, kind=kindh).ap()
    h_dram = state["h_dram"]

    with (nc.sbuf_tensor(C.name("mbc"), [128, T], BF16) as m_bc,
          nc.sbuf_tensor(C.name("rbc"), [128, T], BF16) as r_bc,
          nc.sbuf_tensor(C.name("sqx"), [128, 2 * T], BF16) as SQX,
          nc.sbuf_tensor(C.name("xln"), [128, NCH * T], BF16) as XLN):
        ln_ms = emit_ln(C, nc, R, K, m_bc, r_bc, SQX)
        xln_ms = emit_xln(C, nc, R, m_bc, r_bc, XLN, ln_ms)
        C.wait("tensor", "dv", xln_ms)
        if os.environ.get("DBG_GHOST") == "1":
            dbg_x = nc.dram_tensor(C.name("dbg_xln"), [NCH + 2, 128, T], BF16,
                                   kind="ExternalOutput").ap()
            C.wait("sync", "dv", xln_ms)
            dl = None
            for c in range(NCH):
                dl = C.dma("sync", dbg_x[c], XLN[:, c * T:(c + 1) * T], "l_dbg")
            dl = C.dma("sync", dbg_x[NCH], m_bc[:], "l_dbg")
            dl = C.dma("sync", dbg_x[NCH + 1], r_bc[:], "l_dbg")
            C.wait("sync", dl)
        C.phase()

        with (nc.psum_tensor(C.name("psG"), [128, 2048], F32) as psG,
              nc.psum_tensor(C.name("psU"), [128, 2048], F32) as psU,
              nc.sbuf_tensor(C.name("pg"), [128, 2 * 2048], BF16) as PG,
              nc.sbuf_tensor(C.name("pu"), [128, 2 * 2048], BF16) as PU,
              nc.sbuf_tensor(C.name("idxt"), [128, 2 * 2048], BF16) as IDT,
              nc.sbuf_tensor(C.name("tbuf"), [128, 2048], BF16) as TB,
              nc.sbuf_tensor(C.name("sg"), [128, 2 * 2048], F32) as SG,
              nc.sbuf_tensor(C.name("hh"), [128, 2 * 2048], BF16) as HH):
            silu_ms = {}
            h_ms = {}
            hst_ms = {}
            gate_pe = {}
            up_pe = {}
            pan_ms = {}
            sel_ms = {}
            for t in range(NIT):
                cur = t % 2
                pg = PG[:, cur * 2048:cur * 2048 + 2048]
                pu = PU[:, cur * 2048:cur * 2048 + 2048]
                if first:
                    for mi, (mat, idx_dram, pan, img) in enumerate(
                            (("g", idxg, pg, wim["g"]), ("u", idxu, pu, wim["u"]))):
                        idt = IDT[:, mi * 2048:mi * 2048 + 2048]
                        if t >= 1:
                            C.wait("sync", "dv", sel_ms[(mat, t - 1)])
                        ldt = None
                        for k in range(NCH):
                            ldt = C.dmat("sync", idt[:, k * 128:(k + 1) * 128],
                                         idx_dram[t * 128:(t + 1) * 128, k * 128:(k + 1) * 128],
                                         f"l_idt{mat}")
                        C.wait("vector", ldt)
                        if t >= 2:
                            C.wait("vector", "pe", (gate_pe if mat == "g" else up_pe)[t - 2])
                            C.wait("vector", pan_ms[(mat, t - 2)])
                        sel = emit_ghost_select(C, nc, idt, pan, TB, LUTBC, 2048)
                        sel_ms[(mat, t)] = sel
                        C.wait("sync", "dv", sel)
                        pan_ms[(mat, t)] = C.dma("sync", img[t], pan, f"l_pan{mat}{t % 2}")
                        C.wait("tensor", "dv", sel)
                else:
                    if t >= 2:
                        C.wait("sync", "pe", gate_pe[t - 2])
                        C.wait("sync", "pe", up_pe[t - 2])
                    lg = C.dma("sync", pg, wim["g"][t], f"l_pg{t % 2}")
                    lu = C.dma("sync", pu, wim["u"][t], f"l_pu{t % 2}")
                    C.wait("tensor", lg)
                    C.wait("tensor", lu)
                if t >= 1:
                    C.wait("tensor", "ac", silu_ms[t - 1])
                for k in range(NCH):
                    for j in range(4):
                        gate_pe[t] = C.op("tensor", lambda E, pg=pg, k=k, j=j: E.matmul(
                            psG[:, j * 512:(j + 1) * 512], pg[:, k * 128:(k + 1) * 128],
                            XLN[:, k * T + j * 512:k * T + (j + 1) * 512],
                            start=(k == 0), stop=(k == NCH - 1)),
                            "pe" if (k == NCH - 1 and j == 3) else None)
                C.wait("scalar", "pe", gate_pe[t])
                if t >= 1:
                    C.wait("scalar", "dv", h_ms[t - 1])
                sgb = SG[:, cur * 2048:cur * 2048 + 2048]
                silu_ms[t] = C.op("scalar", lambda E, sgb=sgb: E.activation(
                    sgb, psG[:], AF.Silu), "ac")
                if t >= 1:
                    C.wait("tensor", "dv", h_ms[t - 1])
                for k in range(NCH):
                    for j in range(4):
                        up_pe[t] = C.op("tensor", lambda E, pu=pu, k=k, j=j: E.matmul(
                            psU[:, j * 512:(j + 1) * 512], pu[:, k * 128:(k + 1) * 128],
                            XLN[:, k * T + j * 512:k * T + (j + 1) * 512],
                            start=(k == 0), stop=(k == NCH - 1)),
                            "pe" if (k == NCH - 1 and j == 3) else None)
                C.wait("vector", "pe", up_pe[t])
                C.wait("vector", "ac", silu_ms[t])
                if t >= 2:
                    C.wait("vector", hst_ms[t - 2])
                hb = HH[:, cur * 2048:cur * 2048 + 2048]
                h_ms[t] = C.op("vector", lambda E, hb=hb, sgb=sgb: E.tensor_tensor(
                    hb, sgb, psU[:], op=AluOpType.mult), "dv")
                C.wait("sync", "dv", h_ms[t])
                hst_ms[t] = C.dma("sync", h_dram[t * 128:(t + 1) * 128, :], hb, f"l_hh{t % 2}")
            C.wait("sync", hst_ms[NIT - 1])
            C.wait("sync", hst_ms[NIT - 2])
        C.phase()

    with (nc.psum_tensor(C.name("psD"), [128, 2048], F32) as psD,
          nc.sbuf_tensor(C.name("epb"), [128, 2 * 1024], F32) as EPB,
          nc.sbuf_tensor(C.name("hhalf"), [128, NIT * 1024], BF16) as HHF,
          nc.sbuf_tensor(C.name("wdp"), [128, 2 * NIT * 128], BF16) as WDP,
          nc.sbuf_tensor(C.name("idxt2"), [128, NIT * 128], BF16) as IDT2,
          nc.sbuf_tensor(C.name("tbuf2"), [128, NIT * 128], BF16) as TB2):
        prev_pe = None
        ep_ms = {}
        ep_st = {}
        dpan_ms = {}
        dsel_ms = {}
        dimg_ms = {}
        for half in range(2):
            if prev_pe is not None:
                C.wait("sync", "pe", prev_pe)
            lh = None
            for t in range(NIT):
                lh = C.dma("sync", HHF[:, t * 1024:(t + 1) * 1024],
                           h_dram[t * 128:(t + 1) * 128, half * 1024:half * 1024 + 1024], "l_hhf")
            C.wait("tensor", lh)
            for dt0 in range(NCH):
                dt = half * NCH + dt0
                cur = dt % 2
                wp = WDP[:, cur * NIT * 128:(cur + 1) * NIT * 128]
                if first and half == 0:
                    if dt >= 1:
                        C.wait("sync", "dv", dsel_ms[dt - 1])
                    ldt = None
                    for ic in range(NIT):
                        ldt = C.dmat("sync", IDT2[:, ic * 128:(ic + 1) * 128],
                                     idxd[dt0 * 128:(dt0 + 1) * 128, ic * 128:(ic + 1) * 128],
                                     "l_idtd")
                    C.wait("vector", ldt)
                    if dt >= 2:
                        C.wait("vector", "pe", dpan_ms[dt - 2])
                        C.wait("vector", dimg_ms[dt - 2])
                    sel = emit_ghost_select(C, nc, IDT2, wp, TB2, LUTBC, NIT * 128)
                    dsel_ms[dt] = sel
                    C.wait("sync", "dv", sel)
                    dimg_ms[dt] = C.dma("sync", wim["d"][dt0], wp, f"l_wimd{dt % 2}")
                    C.wait("tensor", "dv", sel)
                else:
                    if dt >= 2:
                        C.wait("sync", "pe", dpan_ms[dt - 2])
                    lw = C.dma("sync", wp, wim["d"][dt0], f"l_wdp{dt % 2}")
                    C.wait("tensor", lw)
                pd = psD[:, cur * 1024:cur * 1024 + 1024]
                ebuf = EPB[:, cur * 1024:cur * 1024 + 1024]
                if dt >= 2:
                    C.wait("tensor", "dv", ep_ms[dt - 2])
                    C.wait("sync", ep_st[dt - 2])
                emit_ep_load(C, nc, state, dt0, half * 1024, 1024, ebuf, f"l_ep{cur}")
                for ic in range(NIT):
                    for j in range(2):
                        dpan_ms[dt] = C.op("tensor", lambda E, wp=wp, ic=ic, j=j, pd=pd: E.matmul(
                            pd[:, j * 512:(j + 1) * 512], wp[:, ic * 128:(ic + 1) * 128],
                            HHF[:, ic * 1024 + j * 512:ic * 1024 + (j + 1) * 512],
                            start=(ic == 0), stop=(ic == NIT - 1)),
                            "pe" if (ic == NIT - 1 and j == 1) else None)
                C.wait("vector", "pe", dpan_ms[dt])
                ep_st[dt], ep_ms[dt] = emit_ep(C, nc, state, R, pd, 0.5, dt0,
                                               half * 1024, 1024, ebuf, True, f"l_eps{cur}")
            C.wait("tensor", "dv", ep_ms[half * NCH + NCH - 1])
            C.wait("tensor", "dv", ep_ms[half * NCH + NCH - 2])
            C.wait("sync", ep_st[half * NCH + NCH - 1])
            C.wait("sync", ep_st[half * NCH + NCH - 2])
            prev_pe = dpan_ms[half * NCH + NCH - 1]


def emit_l1_wcast(C, nc, ext, state):
    wbf = {}
    for nm in ("Wq", "Wk", "Wv", "Wo"):
        wbf[nm] = nc.dram_tensor("wbf1_" + nm, [D, D], BF16).ap()
    state["l1_wbf"] = wbf
    with nc.sbuf_tensor(C.name("wstg"), [128, 2 * D], BF16) as STG:
        st_ms = {}
        i = 0
        for nm in ("Wq", "Wk", "Wv", "Wo"):
            srcw = ext["L1_attn_" + nm]
            for r in range(NCH):
                buf = STG[:, (i % 2) * D:(i % 2) * D + D]
                if i >= 2:
                    C.wait("gpsimd", st_ms[i - 2])
                ld = C.dma("gpsimd", buf, srcw[r * 128:(r + 1) * 128, :], f"g_wst{i % 2}")
                C.wait("sync", ld)
                st_ms[i] = C.dma("sync", wbf[nm][r * 128:(r + 1) * 128, :], buf, f"l_wst{i % 2}")
                i += 1
    state["l1_wcast_ms"] = (st_ms[i - 1], st_ms[i - 2])
    C.phase()


ATTN_PHASES = int(os.environ.get("ATTN_PHASES", "4"))
FLASH_CUT = int(os.environ.get("FLASH_CUT", "7"))


def emit_attn(C, nc, ext, R, K, state, layer):
    state["r32_written"] = True
    H = 16
    NKV = 1 if layer == 0 else H
    kind = "ExternalOutput" if os.environ.get("DBG_ATTN") == "1" else "Internal"
    q_img = nc.dram_tensor(C.name("qimg"), [H, 128, T], BF16, kind=kind).ap()
    k_img = nc.dram_tensor(C.name("kimg"), [NKV, 128, T], BF16, kind=kind).ap()
    v_img = nc.dram_tensor(C.name("vimg"), [NKV, 128, 16 * 128], BF16, kind=kind).ap()
    o_img = nc.dram_tensor(C.name("oimg"), [H, 128, T], BF16, kind=kind).ap()
    if kind == "ExternalOutput":
        state["dbg_names"] = (q_img.tensor.name, k_img.tensor.name,
                              v_img.tensor.name, o_img.tensor.name)
    wbf = state.get("l1_wbf")

    with nc.sbuf_tensor(C.name("ropes"), [128, 4 * T], BF16) as ROPE:
        pre = "rope0_" if layer == 0 else "rope1_"
        g1 = None
        for i, nm in enumerate(("cos_q", "sin_q", "cos_k", "sin_k")):
            g1 = C.dma("gpsimd", ROPE[:, i * T:(i + 1) * T], ext[pre + nm], "g_rope")
        COSQ, SINQ = ROPE[:, 0:T], ROPE[:, T:2 * T]
        COSK, SINK = ROPE[:, 2 * T:3 * T], ROPE[:, 3 * T:4 * T]
        C.wait("vector", g1)

        with (nc.sbuf_tensor(C.name("xln"), [128, NCH * T], BF16) as XLN,
              nc.sbuf_tensor(C.name("wpan"), [128, 16 * 512], BF16) as WPAN,
              nc.sbuf_tensor(C.name("qkb"), [128, 2 * T], BF16) as QKB,
              nc.sbuf_tensor(C.name("rt"), [128, T], BF16) as RT,
              nc.sbuf_tensor(C.name("vtf"), [128, T], F32) as VTF,
              nc.sbuf_tensor(C.name("permt"), [128, 128], BF16) as PERM,
              nc.sbuf_tensor(C.name("identt"), [128, 128], F32) as IDENT):
            lp = C.dma("sync", PERM[:], ext["perm0" if layer == 0 else "perm1"], "l_const")
            lp = C.dma("sync", IDENT[:], ext["identf"], "l_const")
            C.wait("tensor", lp)
            with (nc.sbuf_tensor(C.name("mbc"), [128, T], BF16) as m_bc,
                  nc.sbuf_tensor(C.name("rbc"), [128, T], BF16) as r_bc):
                ln_ms = emit_ln(C, nc, R, K, m_bc, r_bc, XLN[:, 0:2 * T])
                xln_ms = emit_xln(C, nc, R, m_bc, r_bc, XLN, ln_ms)
            C.wait("tensor", "dv", xln_ms)
            C.phase()
            ps_es = contextlib.ExitStack()
            psQ = ps_es.enter_context(nc.psum_tensor(C.name("psq"), [128, 2048], F32))
            psP = ps_es.enter_context(nc.psum_tensor(C.name("psp"), [128, 2048], F32))

            st = {"psQ_free": ("dv", 0), "psP_free": ("dv", 0), "qh_free": ("dv", 0),
                  "ob_free": ("l_obst", 0), "vb_free": ("l_vbst", 0), "vtf_free": ("pe", 0)}

            def wst(eng, key):
                C.wait(eng, st[key])

            def mm_into_psQ(wslice_fn, milestone=True):
                pe_l = None
                for dc in range(NCH):
                    for j in range(4):
                        pe_l = C.op("tensor", lambda E, dc=dc, j=j: E.matmul(
                            psQ[:, j * 512:(j + 1) * 512], wslice_fn(dc),
                            XLN[:, dc * T + j * 512:dc * T + (j + 1) * 512],
                            start=(dc == 0), stop=(dc == NCH - 1)),
                            "pe" if (dc == NCH - 1 and j == 3) else None)
                return pe_l

            def project(wslice_fn, cos, sin, img, img_i):
                wst("tensor", "psQ_free")
                pe_l = mm_into_psQ(wslice_fn)
                qh = QKB[:, 0:T]
                C.wait("vector", "pe", pe_l)
                wst("vector", "qh_free")
                ev = C.op("vector", lambda E, qh=qh: E.tensor_copy(qh, psQ[:]), "dv")
                st["psQ_free"] = ("dv", ev)
                C.wait("tensor", "dv", ev)
                wst("tensor", "psP_free")
                pp = None
                for j in range(4):
                    pp = C.op("tensor", lambda E, qh=qh, j=j: E.matmul(
                        psP[:, j * 512:(j + 1) * 512], PERM[:],
                        qh[:, j * 512:(j + 1) * 512], start=True, stop=True),
                        "pe" if j == 3 else None)
                C.wait("vector", "pe", pp)
                r1 = C.op("vector", lambda E, sin=sin: E.tensor_tensor(
                    RT[:, 0:T], psP[:], sin, op=AluOpType.mult), "dv")
                st["psP_free"] = ("dv", r1)
                C.wait("vector", "dv", r1)
                wst("vector", "ob_free")
                ob = QKB[:, T:2 * T]
                r2 = C.op("vector", lambda E, qh=qh, cos=cos, ob=ob: E.tensor_tensor(
                    ob, qh, cos, op=AluOpType.mult), "dv")
                st["qh_free"] = ("dv", r2)
                C.wait("vector", "dv", r2)
                r3 = C.op("vector", lambda E, ob=ob: E.tensor_tensor(
                    ob, ob, RT[:, 0:T], op=AluOpType.add), "dv")
                C.wait("sync", "dv", r3)
                sm = C.dma("sync", img[img_i], ob, "l_obst")
                st["ob_free"] = sm
                return sm, r3, pp

            def project_v(wslice_fn, img, img_i, next_write_dv):
                wst("tensor", "psQ_free")
                pe_l = mm_into_psQ(wslice_fn)
                C.wait("vector", "pe", pe_l)
                wst("vector", "vtf_free")
                ev = C.op("vector", lambda E: E.tensor_copy(VTF[:], psQ[:]), "dv")
                st["psQ_free"] = ("dv", ev)
                C.wait("tensor", "dv", ev)
                wst("tensor", "psP_free")
                wst("vector", "qh_free")
                wst("vector", "vb_free")
                vb = QKB[:, 0:T]
                tr_pe = {}
                evms = {}
                for tt in range(NCH):
                    slot = psP[:, (tt % 4) * 512:(tt % 4) * 512 + 128]
                    if tt >= 4:
                        C.wait("tensor", "dv", evms[tt - 4])
                    tr_pe[tt] = C.op("tensor", lambda E, slot=slot, tt=tt: E.transpose(
                        slot, VTF[:, tt * 128:(tt + 1) * 128], IDENT[:]), "pe")
                    C.wait("vector", "pe", tr_pe[tt])
                    evms[tt] = C.op("vector", lambda E, slot=slot, tt=tt, vb=vb: E.tensor_copy(
                        vb[:, tt * 128:(tt + 1) * 128], slot), "dv")
                st["psP_free"] = ("dv", evms[NCH - 1])
                st["vtf_free"] = ("pe", tr_pe[NCH - 1])
                C.wait("sync", "dv", evms[NCH - 1])
                sm = C.dma("sync", img[img_i], vb, "l_vbst")
                st["vb_free"] = sm
                st["qh_free"] = sm
                return sm

            if layer == 0:
                WKV = WPAN[:, 16 * 128:16 * 128 + 16 * 256]
                lkv = None
                for c4 in range(4):
                    lkv = C.dma("gpsimd",
                                WKV[:, c4 * 1024:(c4 + 1) * 1024]
                                .rearrange("p (c n) -> p c n", c=4),
                                ext["L0_attn_Wkv"][c4 * 512:(c4 + 1) * 512, :]
                                .rearrange("(c p) n -> p c n", p=128), "g_qpan")
                C.wait("tensor", lkv)
                project(lambda dc: WKV[:, dc * 256:dc * 256 + 128],
                        COSK, SINK, k_img, 0)
                vst = project_v(lambda dc: WKV[:, dc * 256 + 128:dc * 256 + 256],
                                v_img, 0, None)
                prev_pan = 0
                for h in range(H):
                    C.wait("gpsimd", "pe", prev_pan)
                    lqp = None
                    for c4 in range(4):
                        lqp = C.dma("gpsimd",
                                    WPAN[:, c4 * 512:c4 * 512 + 512]
                                    .rearrange("p (c n) -> p c n", c=4),
                                    ext["L0_attn_Wq"][c4 * 512:(c4 + 1) * 512,
                                                      h * 128:(h + 1) * 128]
                                    .rearrange("(c p) n -> p c n", p=128), "g_qpan")
                    C.wait("tensor", lqp)
                    _, _, pp = project(
                        lambda dc: WPAN[:, dc * 128:(dc + 1) * 128],
                        COSQ, SINQ, q_img, h)
                    prev_pan = pp
            else:
                C.wait("sync", state["l1_wcast_ms"][0])
                C.wait("sync", state["l1_wcast_ms"][1])
                prev_pan = 0
                for hg in range(4):
                    for mat, img, what in (("Wq", q_img, "q"), ("Wk", k_img, "k"),
                                           ("Wv", v_img, "v")):
                        C.wait("sync", "pe", prev_pan)
                        lt = None
                        for dc in range(NCH):
                            lt = C.dmat("sync", WPAN[:, dc * 512:(dc + 1) * 512],
                                        wbf[mat][hg * 512:(hg + 1) * 512,
                                                 dc * 128:(dc + 1) * 128], "l_gpan")
                        C.wait("tensor", lt)
                        for hh in range(4):
                            h = hg * 4 + hh
                            wf = (lambda dc, hh=hh: WPAN[:, dc * 512 + hh * 128:
                                                          dc * 512 + hh * 128 + 128])
                            if what == "q":
                                _, _, pp = project(wf, COSQ, SINQ, q_img, h)
                            elif what == "k":
                                _, _, pp = project(wf, COSK, SINK, k_img, h)
                            else:
                                vst = project_v(wf, v_img, h, None)
                        prev_pan = C.val("pe")
            ps_es.close()
        C.phase()
        if ATTN_PHASES < 3:
            return

        with (nc.sbuf_tensor(C.name("qb"), [128, 2 * T], BF16) as QB,
              nc.sbuf_tensor(C.name("kb"), [128, 2 * T], BF16) as KB,
              nc.sbuf_tensor(C.name("vb"), [128, 2 * T], BF16) as VB,
              nc.sbuf_tensor(C.name("psb"), [128, 2048], BF16) as PSB,
              nc.sbuf_tensor(C.name("pts"), [128, 2048], BF16) as PTS,
              nc.sbuf_tensor(C.name("oh"), [128, T], BF16) as OH,
              nc.sbuf_tensor(C.name("caus"), [128, 2048], F32) as CAUSAL,
              nc.sbuf_tensor(C.name("ident2"), [128, 128], BF16) as IDENT2,
              nc.sbuf_tensor(C.name("nm"), [128, 8], F32) as NMRS,
              nc.psum_tensor(C.name("pss"), [128, 2048], F32) as psS,
              nc.psum_tensor(C.name("pst"), [128, 2048], BF16) as psT,
              nc.psum_tensor(C.name("pso"), [128, 512], F32) as psO):
            lc = C.dma("sync", CAUSAL[:], ext["causal"], "l_const")
            lc = C.dma("sync", IDENT2[:], ext["ident"], "l_const")
            C.wait("vector", lc)
            C.wait("tensor", lc)
            last_exp = 0
            last_head_pe = {}
            oh_st = ("l_ohst", 0)
            ld_ms = {}
            for h in range(H):
                cur = h % 2
                if h >= 2:
                    C.wait("sync", "pe", last_head_pe[h - 2])
                ld_q = C.dma("sync", QB[:, cur * T:(cur + 1) * T], q_img[h], f"l_qb{cur}")
                C.wait("tensor", ld_q)
                if layer == 1 or h == 0:
                    kv = h if layer == 1 else 0
                    kcur2 = cur if layer == 1 else 0
                    ld_k = C.dma("sync", KB[:, kcur2 * T:(kcur2 + 1) * T], k_img[kv], f"l_kb{kcur2}")
                    ld_v = C.dma("sync", VB[:, kcur2 * T:(kcur2 + 1) * T], v_img[kv], f"l_vb{kcur2}")
                    C.wait("tensor", ld_k)
                    C.wait("tensor", ld_v)
                kcur = cur if layer == 1 else 0
                qsl = QB[:, cur * T:(cur + 1) * T]
                ksl = KB[:, kcur * T:(kcur + 1) * T]
                vsl = VB[:, kcur * T:(kcur + 1) * T]
                prev_pts_pe = 0
                for qt in range(16):
                    nk = (qt * 128 + 128 + 511) // 512
                    C.wait("tensor", "ac", last_exp)
                    s_pe = None
                    for j in range(nk):
                        s_pe = C.op("tensor", lambda E, j=j, qt=qt, qsl=qsl, ksl=ksl: E.matmul(
                            psS[:, j * 512:(j + 1) * 512],
                            qsl[:, qt * 128:(qt + 1) * 128],
                            ksl[:, j * 512:(j + 1) * 512],
                            start=True, stop=True),
                            "pe" if j == nk - 1 else None)
                    if FLASH_CUT < 2:
                        last_head_pe[h] = s_pe
                        continue
                    v4 = qt - (nk - 1) * 4
                    C.wait("vector", "pe", s_pe)
                    mk = C.op("vector", lambda E, nk=nk, v4=v4: E.tensor_tensor(
                        psS[:, (nk - 1) * 512:nk * 512], psS[:, (nk - 1) * 512:nk * 512],
                        CAUSAL[:, v4 * 512:(v4 + 1) * 512], op=AluOpType.add), "dv")
                    if FLASH_CUT < 3:
                        last_head_pe[h] = s_pe
                        continue
                    C.wait("vector", "dv", mk)
                    C.wait("vector", "ac", last_exp)
                    rm = C.op("vector", lambda E, nk=nk: E.tensor_reduce(
                        NMRS[:, 0:1], psS[:, 0:nk * 512], axis=mybir.AxisListType.X,
                        op=AluOpType.max, negate=True), "dv")
                    if FLASH_CUT < 4:
                        last_head_pe[h] = s_pe
                        continue
                    C.wait("scalar", "dv", rm)
                    ex = C.op("scalar", lambda E, nk=nk: E.activation(
                        PSB[:, 0:nk * 512], psS[:, 0:nk * 512], AF.Exp,
                        bias=NMRS[:, 0:1], accum_out=NMRS[:, 1:2]), "ac")
                    last_exp = ex
                    if FLASH_CUT < 5:
                        last_head_pe[h] = s_pe
                        continue
                    C.wait("vector", "ac", ex)
                    rc = C.op("vector", lambda E: E.reciprocal(NMRS[:, 2:3], NMRS[:, 1:2]), "dv")
                    C.wait("vector", "dv", rc)
                    nrm = C.op("vector", lambda E, nk=nk: E.tensor_scalar(
                        PSB[:, 0:nk * 512], PSB[:, 0:nk * 512], NMRS[:, 2:3], None,
                        op0=AluOpType.mult), "dv")
                    if FLASH_CUT < 6:
                        last_head_pe[h] = s_pe
                        continue
                    C.wait("tensor", "dv", nrm)
                    C.wait("tensor", "pe", prev_pts_pe)
                    njc = nk * 4
                    tr_ms = {}
                    ev_ms = {}
                    for jc in range(njc):
                        slot = psT[:, (jc % 2) * 1024:(jc % 2) * 1024 + 128]
                        if jc >= 2:
                            C.wait("tensor", "dv", ev_ms[jc - 2])
                        tr_ms[jc] = C.op("tensor", lambda E, slot=slot, jc=jc: E.transpose(
                            slot, PSB[:, jc * 128:(jc + 1) * 128], IDENT2[:]), "pe")
                        C.wait("vector", "pe", tr_ms[jc])
                        ev_ms[jc] = C.op("vector", lambda E, slot=slot, jc=jc: E.tensor_copy(
                            PTS[:, jc * 128:(jc + 1) * 128], slot), "dv")
                    if FLASH_CUT < 7:
                        last_head_pe[h] = tr_ms[njc - 1]
                        prev_pts_pe = tr_ms[njc - 1]
                        continue
                    o_pe = None
                    for jc in range(njc):
                        C.wait("tensor", "dv", ev_ms[jc])
                        if jc == 0:
                            C.wait("tensor", "dv", C.val("dv"))
                        o_pe = C.op("tensor", lambda E, jc=jc, vsl=vsl: E.matmul(
                            psO[:, 0:128], vsl[:, jc * 128:(jc + 1) * 128],
                            PTS[:, jc * 128:(jc + 1) * 128],
                            start=(jc == 0), stop=(jc == njc - 1)),
                            "pe" if jc == njc - 1 else None)
                    prev_pts_pe = o_pe
                    C.wait("vector", "pe", o_pe)
                    if qt == 0:
                        C.wait("vector", oh_st)
                    oev = C.op("vector", lambda E, qt=qt: E.tensor_copy(
                        OH[:, qt * 128:(qt + 1) * 128], psO[:, 0:128]), "dv")
                if FLASH_CUT >= 7:
                    last_head_pe[h] = o_pe
                    C.wait("sync", "dv", oev)
                    oh_st = C.dma("sync", o_img[h], OH[:], "l_ohst")
            if FLASH_CUT >= 7:
                C.wait("sync", oh_st)
        C.phase()
        if ATTN_PHASES < 4:
            return

        with (nc.sbuf_tensor(C.name("osb"), [128, NCH * T], BF16) as OSB,
              nc.sbuf_tensor(C.name("wop"), [128, 16 * 512], BF16) as WOP,
              nc.sbuf_tensor(C.name("epb4"), [128, 2 * T], F32) as EPB4,
              nc.psum_tensor(C.name("psa"), [128, 2 * 2048], F32) as psA):
            lo = None
            for h in range(H):
                lo = C.dma("sync", OSB[:, h * T:(h + 1) * T], o_img[h], "l_osb")
            C.wait("tensor", lo)
            ep_ms = {}
            ep_st = {}
            mm_ms = {}
            for dt in range(NCH):
                cur = dt % 2
                if layer == 0:
                    if dt >= 2:
                        C.wait("gpsimd", "pe", mm_ms[dt - 2])
                    lwp = None
                    for c4 in range(4):
                        lwp = C.dma("gpsimd",
                                    WOP[:, (cur * 8 + c4) * 512:(cur * 8 + c4) * 512 + 512]
                                    .rearrange("p (c n) -> p c n", c=4),
                                    ext["L0_attn_Wout"][c4 * 512:(c4 + 1) * 512,
                                                        dt * 128:(dt + 1) * 128]
                                    .rearrange("(c p) n -> p c n", p=128), f"g_wop{cur}")
                    C.wait("tensor", lwp)
                    wsl = lambda ic, cur=cur: WOP[:, cur * 8 * 512 + ic * 512 - ic * 512 + ic * 128:0]                         if False else None
                    base = cur * 8 * 512
                    wsl = (lambda ic, base=base: WOP[:, base + ic * 128:base + ic * 128 + 128])
                else:
                    if dt % 4 == 0:
                        if dt >= 4:
                            C.wait("sync", "pe", mm_ms[dt - 1])
                        lwp = None
                        for ic in range(NCH):
                            lwp = C.dmat("sync", WOP[:, ic * 512:(ic + 1) * 512],
                                         wbf["Wo"][dt * 128:(dt + 4) * 128,
                                                   ic * 128:(ic + 1) * 128], "l_wop")
                        C.wait("tensor", lwp)
                    hh = dt % 4
                    wsl = (lambda ic, hh=hh: WOP[:, ic * 512 + hh * 128:ic * 512 + hh * 128 + 128])
                if dt >= 2:
                    C.wait("tensor", "dv", ep_ms[dt - 2])
                    C.wait("sync", ep_st[dt - 2])
                pa = psA[:, cur * 2048:cur * 2048 + 2048]
                ebuf = EPB4[:, cur * 2048:cur * 2048 + 2048]
                emit_ep_load(C, nc, state, dt, 0, T, ebuf, f"l_ep{cur}")
                for ic in range(NCH):
                    for j in range(4):
                        mm_ms[dt] = C.op("tensor", lambda E, ic=ic, j=j, pa=pa, wsl=wsl: E.matmul(
                            pa[:, j * 512:(j + 1) * 512], wsl(ic),
                            OSB[:, ic * T + j * 512:ic * T + (j + 1) * 512],
                            start=(ic == 0), stop=(ic == NCH - 1)),
                            "pe" if (ic == NCH - 1 and j == 3) else None)
                C.wait("vector", "pe", mm_ms[dt])
                ep_st[dt], ep_ms[dt] = emit_ep(C, nc, state, R, pa, 1.0, dt,
                                               0, T, ebuf, True, f"l_eps{cur}")
            C.wait("tensor", "dv", ep_ms[NCH - 1])
            C.wait("tensor", "dv", ep_ms[NCH - 2])
            C.wait("sync", ep_st[NCH - 1])
            C.wait("sync", ep_st[NCH - 2])


def emit_attn0(C, nc, ext, R, K, state):
    emit_attn(C, nc, ext, R, K, state, 0)


def emit_attn1(C, nc, ext, R, K, state):
    if "l1_wbf" not in state:
        emit_l1_wcast(C, nc, ext, state)
    emit_attn(C, nc, ext, R, K, state, 1)



ECAP = 648
EUSE = 640
NSLOT = 8 * ECAP
I2 = 2816
NI2 = I2 // 128
NI = 11
J2 = 11264
NJ2 = J2 // 128
NJ = 44


def emit_moe(C, nc, ext, R, K, state):
    state["r32_written"] = True
    ap_i = state.get("moe_n", 0)
    state["moe_n"] = ap_i + 1
    if "moe_dram" not in state:
        md = {}
        kd = "ExternalOutput" if os.environ.get("DBG_MOE") == "1" else "Internal"
        md["xln"] = nc.dram_tensor("md_xln", [D, T], BF16, kind=kd).ap()
        md["buf"] = nc.dram_tensor("md_buf", [NSLOT, D], BF16, kind=kd).ap()
        md["obt"] = nc.dram_tensor("md_obt", [D, NSLOT], BF16, kind=kd).ap()
        md["otm"] = nc.dram_tensor("md_otm", [NSLOT, D], BF16, kind=kd).ap()
        md["y"] = nc.dram_tensor("md_y", [T, D], BF16, kind=kd).ap()
        md["hsh"] = nc.dram_tensor("md_hsh", [5632, T], BF16, kind=kd).ap()
        md["rowb"] = nc.dram_tensor("md_rowb", [4, T], F32, kind=kd).ap()
        md["cci"] = nc.dram_tensor("md_cci0", [1, 8], F32)
        md["cco"] = nc.dram_tensor("md_cco0", [8, 8], F32)
        state["moe_dram"] = md
    md = state["moe_dram"]

    with nc.sbuf_tensor(C.name("mrow"), [1, 2 * T], F32) as MR:
        with (nc.sbuf_tensor(C.name("mbc"), [128, T], BF16) as m_bc,
              nc.sbuf_tensor(C.name("rbc"), [128, T], BF16) as r_bc,
              nc.sbuf_tensor(C.name("xln"), [128, NCH * T], BF16) as XLN):
            ln_ms = emit_ln(C, nc, R, K, m_bc, r_bc, XLN[:, 0:2 * T], out_rows=MR)
            xln_ms = emit_xln(C, nc, R, m_bc, r_bc, XLN, ln_ms)
            C.wait("sync", "dv", xln_ms)
            xs = None
            for c in range(NCH):
                xs = C.dma("sync", md["xln"][c * 128:(c + 1) * 128, :],
                           XLN[:, c * T:(c + 1) * T], "l_xst")
            C.wait("sync", xs)
        C.phase()

        with (nc.sbuf_tensor(C.name("rout"), [8, 8 * T], F32) as ROUT,
              nc.sbuf_tensor(C.name("rtr"), [128, 16 * 8], F32) as RTR,
              nc.sbuf_tensor(C.name("r32c"), [128, 2 * T], F32) as R32C,
              nc.sbuf_tensor(C.name("lt"), [128, 16 * 8], F32) as LT,
              nc.sbuf_tensor(C.name("eq"), [128, 4 * 128], F32) as EQ,
              nc.sbuf_tensor(C.name("pk"), [128, 64], F32) as PACK,
              nc.sbuf_tensor(C.name("cols"), [128, 128], F32) as COLS,
              nc.sbuf_tensor(C.name("row1"), [1, T], F32) as ROW1,
              nc.sbuf_tensor(C.name("sm8"), [128, 16], F32) as SM8,
              nc.sbuf_tensor(C.name("idf"), [128, 128], F32) as IDF,
              nc.psum_tensor(C.name("pslg"), [128, 2048], F32) as psL,
              nc.psum_tensor(C.name("pscs"), [128, 512], F32) as psC):
            Q = {"lg": 0, "prod": 0,
                 "mbc8": 1, "c1": 1, "pos1": 1,
                 "rbc8": 2, "c2": 2,
                 "m1": 3, "m2": 4,
                 "ones8": 5, "v1": 5,
                 "pos2": 6, "v2": 7}
            def q(name, c0=0, w=T):
                t = Q[name]
                return ROUT[0:8, t * T + c0:t * T + c0 + w]
            g1 = C.op("gpsimd", lambda E: E.iota(SM8[:, 0:1], pattern=[[0, 1]],
                      base=0, channel_multiplier=1, allow_small_or_imprecise_dtypes=True), None)
            g2 = C.op("gpsimd", lambda E: E.iota(LT[:, 0:8], pattern=[[1, 8]],
                      base=0, channel_multiplier=0, allow_small_or_imprecise_dtypes=True), None)
            g3 = C.op("gpsimd", lambda E: E.memset(q("ones8"), 1.0), "gp")
            lrm = C.dma("sync", SM8[0:8, 1:2], ext["rank_mask"], "l_const")
            lid = C.dma("sync", IDF[:], ext["identf"], "l_const")
            IOTA8R = LT[:, 0:8]
            C.wait("vector", "gp", g3)
            ionb = C.op("vector", lambda E: E.tensor_copy(COLS[:, 120:128], LT[:, 0:8]), "dv")
            IOTA8ROW = COLS[:, 120:128]
            IOTA8COL = SM8[:, 0:1]
            RMASK = SM8[0:8, 1:2]
            lr = None
            for c in range(NCH):
                lr = C.dma("sync", RTR[:, c * 8:(c + 1) * 8],
                           ext["L1_ffn_router"][c * 128:(c + 1) * 128, :], "l_const")
            C.wait("tensor", "l_const", C.val("l_const"))
            C.wait("tensor", "dv", ionb)
            rsrc = state["r32_src"]
            lg_pe = None
            cs_pe = None
            ch_pe = {}
            for c in range(NCH):
                sl = c % 2
                if c >= 2:
                    C.wait("sync", "pe", ch_pe[c - 2])
                lc = C.dma("sync", R32C[:, sl * T:sl * T + T],
                           rsrc[c * 128:(c + 1) * 128, :], f"l_r32{sl}")
                C.wait("tensor", lc)
                for j in range(4):
                    lg_pe = C.op("tensor", lambda E, c=c, j=j, sl=sl: E.matmul(
                        psL[0:8, j * 512:(j + 1) * 512], RTR[:, c * 8:(c + 1) * 8],
                        R32C[:, sl * T + j * 512:sl * T + (j + 1) * 512],
                        start=(c == 0), stop=(c == NCH - 1)),
                        "pe" if (c == NCH - 1 and j == 3) else None)
                cs_pe = C.op("tensor", lambda E, c=c: E.matmul(
                    psC[0:8, 0:1], RTR[:, c * 8:(c + 1) * 8], K["ONESCF"][:],
                    start=(c == 0), stop=(c == NCH - 1)),
                    "pe" if c == NCH - 1 else None)
                ch_pe[c] = cs_pe if cs_pe is not None else lg_pe
                if ch_pe[c] is None:
                    ch_pe[c] = C.op("tensor", lambda E, c=c, sl=sl: E.matmul(
                        psC[32:40, 256:257], RTR[:, c * 8:(c + 1) * 8],
                        K["ONESCF"][:], start=True, stop=True), "pe")
            C.wait("vector", "pe", lg_pe)
            C.wait("vector", "pe", cs_pe)
            e1 = C.op("vector", lambda E: E.tensor_copy(q("lg"), psL[0:8, 0:T]), "dv")
            C.wait("vector", "dv", e1)
            e2 = C.op("vector", lambda E: E.tensor_copy(SM8[0:8, 2:3], psC[0:8, 0:1]), "dv")
            C.wait("vector", "dv", e2)
            e3 = C.op("vector", lambda E: E.tensor_scalar(SM8[0:8, 3:4], SM8[0:8, 2:3],
                      -1.0, None, op0=AluOpType.mult), "dv")
            CSRN = SM8[0:8, 3:4]
            C.wait("tensor", "dv", e3)
            for j in range(4):
                mb = C.op("tensor", lambda E, j=j: E.matmul(
                    psL[0:8, j * 512:(j + 1) * 512], K["ONES1F"][0:1, 0:8],
                    MR[0:1, j * 512:(j + 1) * 512], start=True, stop=True),
                    "pe" if j == 3 else None)
            C.wait("vector", "pe", mb)
            e4 = C.op("vector", lambda E: E.tensor_copy(q("mbc8"), psL[0:8, 0:T]), "dv")
            C.wait("tensor", "dv", e4)
            for j in range(4):
                rb2 = C.op("tensor", lambda E, j=j: E.matmul(
                    psL[0:8, j * 512:(j + 1) * 512], K["ONES1F"][0:1, 0:8],
                    MR[0:1, T + j * 512:T + (j + 1) * 512], start=True, stop=True),
                    "pe" if j == 3 else None)
            C.wait("vector", "pe", rb2)
            e5 = C.op("vector", lambda E: E.tensor_copy(q("rbc8"), psL[0:8, 0:T]), "dv")
            C.wait("vector", "dv", e5)
            f1 = C.op("vector", lambda E: E.scalar_tensor_tensor(
                q("lg"), in0=q("mbc8"), scalar=CSRN, in1=q("lg"),
                op0=AluOpType.mult, op1=AluOpType.add), "dv")
            C.wait("vector", "dv", f1)
            f2 = C.op("vector", lambda E: E.tensor_tensor(
                q("lg"), q("lg"), q("rbc8"), op=AluOpType.mult), "dv")
            if os.environ.get("DBG_MOE") == "1":
                dbg_lg = nc.dram_tensor(C.name("dbg_lg"), [8, T], F32,
                                        kind="ExternalOutput").ap()
                C.wait("sync", "dv", f2)
                dlg = C.dma("sync", dbg_lg, q("lg"), "l_dbg3")
                C.wait("tensor", dlg)
            C.wait("tensor", "dv", f2)
            trms = {}
            for tt in range(16):
                slot = psL[:, (tt % 4) * 512:(tt % 4) * 512 + 8]
                if tt >= 4:
                    C.wait("tensor", "dv", trms[tt - 4])
                tp = C.op("tensor", lambda E, slot=slot, tt=tt: E.transpose(
                    slot, q("lg", tt * 128, 128), IDF[0:8, 0:8]), "pe")
                C.wait("vector", "pe", tp)
                trms[tt] = C.op("vector", lambda E, slot=slot, tt=tt: E.tensor_copy(
                    LT[:, tt * 8:(tt + 1) * 8], slot), "dv")
            last = trms[15]
            eqtr = {}
            for tt in range(16):
                lt = LT[:, tt * 8:(tt + 1) * 8]
                C.wait("vector", "dv", last)
                a = C.op("vector", lambda E, lt=lt: E.tensor_reduce(
                    SM8[:, 4:5], lt, axis=mybir.AxisListType.X, op=AluOpType.max), "dv")
                C.wait("vector", "dv", a)
                b = C.op("vector", lambda E, lt=lt: E.tensor_scalar(
                    EQ[:, 0:8], lt, SM8[:, 4:5], None, op0=AluOpType.is_equal), "dv")
                C.wait("vector", "dv", b)
                c2_ = C.op("vector", lambda E, lt=lt: E.scalar_tensor_tensor(
                    EQ[:, 8:16], in0=EQ[:, 0:8], scalar=NEG, in1=lt,
                    op0=AluOpType.mult, op1=AluOpType.add), "dv")
                C.wait("vector", "dv", c2_)
                dd = C.op("vector", lambda E: E.tensor_reduce(
                    SM8[:, 5:6], EQ[:, 8:16], axis=mybir.AxisListType.X,
                    op=AluOpType.max), "dv")
                C.wait("vector", "dv", dd)
                ee = C.op("vector", lambda E: E.tensor_scalar(
                    EQ[:, 16:24], EQ[:, 8:16], SM8[:, 5:6], None,
                    op0=AluOpType.is_equal), "dv")
                C.wait("vector", "dv", ee)
                ff = C.op("vector", lambda E: E.tensor_tensor(
                    EQ[:, 24:32], EQ[:, 0:8], IOTA8ROW, op=AluOpType.mult), "dv")
                C.wait("vector", "dv", ff)
                gg = C.op("vector", lambda E, tt=tt: E.tensor_reduce(
                    PACK[:, tt:tt + 1], EQ[:, 24:32], axis=mybir.AxisListType.X,
                    op=AluOpType.add), "dv")
                C.wait("vector", "dv", gg)
                hh = C.op("vector", lambda E: E.tensor_tensor(
                    EQ[:, 24:32], EQ[:, 16:24], IOTA8ROW, op=AluOpType.mult), "dv")
                C.wait("vector", "dv", hh)
                ii = C.op("vector", lambda E, tt=tt: E.tensor_reduce(
                    PACK[:, 16 + tt:17 + tt], EQ[:, 24:32], axis=mybir.AxisListType.X,
                    op=AluOpType.add), "dv")
                C.wait("vector", "dv", ii)
                jj = C.op("vector", lambda E: E.tensor_tensor(
                    SM8[:, 6:7], SM8[:, 5:6], SM8[:, 4:5], op=AluOpType.subtract), "dv")
                C.wait("scalar", "dv", jj)
                kk = C.op("scalar", lambda E, tt=tt: E.activation(
                    PACK[:, 48 + tt:49 + tt], SM8[:, 6:7], AF.Sigmoid), "ac")
                C.wait("vector", "ac", kk)
                ll = C.op("vector", lambda E, tt=tt: E.tensor_scalar(
                    PACK[:, 32 + tt:33 + tt], PACK[:, 48 + tt:49 + tt], -1.0, 1.0,
                    op0=AluOpType.mult, op1=AluOpType.add), "dv")
                C.wait("tensor", "dv", ll)
                t1 = C.op("tensor", lambda E, tt=tt: E.transpose(
                    psL[0:8, 0:128], EQ[:, 0:8], IDF[:]), "pe")
                C.wait("vector", "pe", t1)
                mm1 = C.op("vector", lambda E, tt=tt: E.tensor_copy(
                    q("m1", tt * 128, 128), psL[0:8, 0:128]), "dv")
                C.wait("tensor", "dv", mm1)
                t2 = C.op("tensor", lambda E, tt=tt: E.transpose(
                    psL[0:8, 512:640], EQ[:, 16:24], IDF[:]), "pe")
                C.wait("vector", "pe", t2)
                last = C.op("vector", lambda E, tt=tt: E.tensor_copy(
                    q("m2", tt * 128, 128), psL[0:8, 512:640]), "dv")
            if os.environ.get("DBG_MOE") == "1":
                dbg_lt = nc.dram_tensor(C.name("dbg_lt"), [128, 16 * 8], F32,
                                        kind="ExternalOutput").ap()
                dbg_pk = nc.dram_tensor(C.name("dbg_pk"), [128, 64], F32,
                                        kind="ExternalOutput").ap()
                dbg_m = nc.dram_tensor(C.name("dbg_m"), [8, 2 * T], F32,
                                       kind="ExternalOutput").ap()
                C.wait("sync", "dv", last)
                C.dma("sync", dbg_lt, LT[:, 0:128], "l_dbg3")
                C.dma("sync", dbg_pk, PACK[:], "l_dbg3")
                C.dma("sync", dbg_m[:, 0:T], q("m1"), "l_dbg3")
                dmm_ = C.dma("sync", dbg_m[:, T:2 * T], q("m2"), "l_dbg3")
                C.wait("vector", dmm_)
            C.wait("vector", "dv", last)
            s1 = C.op("vector", lambda E: E.tensor_tensor_scan(
                q("c1"), q("ones8"), q("m1"), 0.0,
                op0=AluOpType.mult, op1=AluOpType.add), "dv")
            C.wait("vector", "dv", s1)
            s2 = C.op("vector", lambda E: E.tensor_tensor_scan(
                q("c2"), q("ones8"), q("m2"), 0.0,
                op0=AluOpType.mult, op1=AluOpType.add), "dv")
            C.wait("vector", "dv", s2)
            cn = C.op("vector", lambda E: E.tensor_tensor(
                SM8[0:8, 7:8], q("c1", T - 1, 1), q("c2", T - 1, 1),
                op=AluOpType.add), "dv")
            C.wait("gpsimd", "dv", cn)
            ci = C.dma("gpsimd", md["cci"].ap().rearrange("a b -> b a"),
                       SM8[0:8, 7:8], "g_cc")
            C.wait("gpsimd", ci)
            ccs = C.sem(f"cc{ap_i}")
            C.ctr[ccs] = 1
            def _cc(E, S):
                E.collective_compute(
                    "AllGather", AluOpType.bypass,
                    replica_groups=[list(range(NCORES))],
                    ins=[md["cci"].ap().opt()], outs=[md["cco"].ap().opt()],
                ).then_inc(S[ccs])
            C.q["gpsimd"].append(_cc)
            C.wait("gpsimd", ccs, 1)
            co = C.dma("gpsimd", COLS[0:8, 0:8], md["cco"].ap(), "g_cc")
            C.wait("tensor", co)
            of = C.op("tensor", lambda E: E.matmul(
                psC[0:8, 0:1], COLS[0:8, 0:8], RMASK[0:8, :],
                start=True, stop=True), "pe")
            C.wait("vector", "pe", of)
            ov = C.op("vector", lambda E: E.tensor_copy(SM8[0:8, 8:9], psC[0:8, 0:1]), "dv")
            OFF = SM8[0:8, 8:9]
            C.wait("vector", "dv", ov)
            p1 = C.op("vector", lambda E: E.tensor_tensor(
                q("pos1"), q("c1"), q("m1"), op=AluOpType.subtract), "dv")
            C.wait("vector", "dv", p1)
            p2 = C.op("vector", lambda E: E.tensor_tensor(
                q("c2"), q("c2"), q("m2"), op=AluOpType.subtract), "dv")
            C.wait("vector", "dv", p2)
            p3 = C.op("vector", lambda E: E.tensor_tensor(
                q("pos1"), q("pos1"), q("c2"), op=AluOpType.add), "dv")
            C.wait("vector", "dv", p3)
            p4 = C.op("vector", lambda E: E.tensor_tensor(
                q("pos2"), q("pos1"), q("m1"), op=AluOpType.add), "dv")
            C.wait("vector", "dv", p4)
            v1 = C.op("vector", lambda E: E.tensor_scalar(
                q("v1"), q("pos1"), OFF, 5120.0,
                op0=AluOpType.add, op1=AluOpType.is_lt), "dv")
            C.wait("vector", "dv", v1)
            v2 = C.op("vector", lambda E: E.tensor_scalar(
                q("v2"), q("pos2"), OFF, 5120.0,
                op0=AluOpType.add, op1=AluOpType.is_lt), "dv")
            C.wait("vector", "dv", v2)
            cl1 = C.op("vector", lambda E: E.tensor_scalar(
                q("pos1"), q("pos1"), float(ECAP - 1), None, op0=AluOpType.min), "dv")
            C.wait("vector", "dv", cl1)
            cl2 = C.op("vector", lambda E: E.tensor_scalar(
                q("pos2"), q("pos2"), float(ECAP - 1), None, op0=AluOpType.min), "dv")
            C.wait("vector", "dv", cl2)
            rowq = [("pos1", "m1", 0), ("pos2", "m2", 1), ("v1", "m1", 2), ("v2", "m2", 3)]
            bms = {}
            for nm, msk, ri in rowq:
                pr = C.op("vector", lambda E, nm=nm, msk=msk: E.tensor_tensor(
                    q("prod"), q(nm), q(msk), op=AluOpType.mult), "dv")
                C.wait("tensor", "dv", pr)
                for j in range(4):
                    rm2 = C.op("tensor", lambda E, j=j: E.matmul(
                        psL[0:1, j * 512:(j + 1) * 512],
                        K["ONESCF"][0:8, :], q("prod", j * 512, 512),
                        start=True, stop=True), "pe" if j == 3 else None)
                C.wait("vector", "pe", rm2)
                ev = C.op("vector", lambda E: E.tensor_copy(ROW1[0:1, :], psL[0:1, 0:T]), "dv")
                C.wait("sync", "dv", ev)
                bms[ri] = C.dma("sync", md["rowb"][ri:ri + 1, :], ROW1[0:1, :], "l_rbc")
                C.wait("vector", bms[ri])
            for ri in range(4):
                lb = C.dma("sync", LT[0:16, 0:128],
                           md["rowb"][ri:ri + 1, :].rearrange("o (a b) -> (o a) b", a=16),
                           "l_rbc")
                C.wait("tensor", lb)
                tp = C.op("tensor", lambda E: E.transpose(
                    psL[:, 0:16], LT[0:16, 0:128], IDF[0:16, 0:16]), "pe")
                C.wait("vector", "pe", tp)
                cv = C.op("vector", lambda E, ri=ri: E.tensor_copy(
                    COLS[:, 16 + ri * 16:32 + ri * 16], psL[:, 0:16]), "dv")
                C.wait("sync", "dv", cv)
            POS1C, POS2C = COLS[:, 16:32], COLS[:, 32:48]
            V1C, V2C = COLS[:, 48:64], COLS[:, 64:80]
            dv1 = C.op("vector", lambda E: E.tensor_scalar(
                COLS[:, 80:96], PACK[:, 0:16], float(ECAP), None,
                op0=AluOpType.mult), "dv")
            C.wait("vector", "dv", dv1)
            dv2 = C.op("vector", lambda E: E.tensor_tensor(
                COLS[:, 80:96], COLS[:, 80:96], POS1C, op=AluOpType.add), "dv")
            C.wait("vector", "dv", dv2)
            dv3 = C.op("vector", lambda E: E.tensor_scalar(
                COLS[:, 96:112], PACK[:, 16:32], float(ECAP), None,
                op0=AluOpType.mult), "dv")
            C.wait("vector", "dv", dv3)
            dv4 = C.op("vector", lambda E: E.tensor_tensor(
                COLS[:, 96:112], COLS[:, 96:112], POS2C, op=AluOpType.add), "dv")
            C.wait("vector", "dv", dv4)
            g1_ = C.op("vector", lambda E: E.tensor_tensor(
                PACK[:, 32:48], PACK[:, 32:48], V1C, op=AluOpType.mult), "dv")
            C.wait("vector", "dv", g1_)
            g2_ = C.op("vector", lambda E: E.tensor_tensor(
                PACK[:, 48:64], PACK[:, 48:64], V2C, op=AluOpType.mult), "dv")
            C.wait("gpsimd", "dv", dv4)
            du = C.dma("gpsimd", state["GDU"][:, 0:32], COLS[:, 80:112], "g_du")
            C.wait("sync", du)
            C.wait("vector", "dv", g2_)
            gd = C.op("vector", lambda E: E.tensor_copy(state["GD"][:, 0:32],
                      PACK[:, 32:64]), "dv")
            C.wait("sync", "dv", gd)
        C.phase()

        for _sn in ("g_sca0", "g_sca1", "g_scb0", "g_scb1", "g_ga0", "g_ga1"):
            C.sem(_sn)
        with nc.sbuf_tensor(C.name("xt"), [128, 2 * T], BF16) as XT:
            if os.environ.get("MOE_ZBUF") == "1":
                zf = C.op("gpsimd", lambda E: E.memset(XT[:, 0:T], 0.0), "gp")
                C.wait("sync", "gp", zf)
                zz = None
                for rr_ in range(41):
                    n_ = min(128, NSLOT - rr_ * 128)
                    zz = C.dma("sync", md["buf"][rr_ * 128:rr_ * 128 + n_, :],
                               XT[0:n_, 0:T], "l_zb")
                C.wait("sync", zz)
                C.wait("gpsimd", zz)
            sc_ms = {("a", 0): ("g_sca0", 0), ("a", 1): ("g_sca1", 0),
                     ("b", 0): ("g_scb0", 0), ("b", 1): ("g_scb1", 0)}
            for tt in range(16):
                sl = tt % 2
                if tt >= 2:
                    C.wait("sync", sc_ms[("a", sl)])
                    C.wait("sync", sc_ms[("b", sl)])
                xl = C.dmat("sync", XT[:, sl * T:sl * T + T],
                            md["xln"][:, tt * 128:(tt + 1) * 128], f"l_xt{sl}")
                C.wait("gpsimd", xl)
                def sc(E, S, sl=sl, tt=tt, which=0):
                    pass
                ia = C.inc(f"g_sca{sl}", 16)
                C.q["gpsimd"].append(lambda E, S, sl=sl, tt=tt, ia=ia: E.indirect_dma_start(
                    out=md["buf"], out_offset=bass.IndirectOffsetOnAxis(
                        ap=state["GDU"][:, tt:tt + 1], axis=0),
                    in_=XT[:, sl * T:sl * T + T], in_offset=None,
                ).then_inc(S[f"g_sca{sl}"], 16))
                sc_ms[("a", sl)] = (f"g_sca{sl}", ia)
                ib = C.inc(f"g_scb{sl}", 16)
                C.q["gpsimd"].append(lambda E, S, sl=sl, tt=tt, ib=ib: E.indirect_dma_start(
                    out=md["buf"], out_offset=bass.IndirectOffsetOnAxis(
                        ap=state["GDU"][:, 16 + tt:17 + tt], axis=0),
                    in_=XT[:, sl * T:sl * T + T], in_offset=None,
                ).then_inc(S[f"g_scb{sl}"], 16))
                sc_ms[("b", sl)] = (f"g_scb{sl}", ib)
            for k, v in sc_ms.items():
                C.wait("sync", v)
                C.wait("gpsimd", v)
            fl = C.dma("gpsimd", XT[:, 0:64], md["buf"][0:128, 0:64], "g_du")
            C.wait("sync", fl)
            C.wait("gpsimd", fl)
            C.wait("tensor", fl)
        C.phase()

        with (nc.sbuf_tensor(C.name("bt"), [128, 16 * EUSE], BF16) as BT,
              nc.sbuf_tensor(C.name("he"), [128, NI * EUSE], BF16) as HE,
              nc.sbuf_tensor(C.name("wep"), [128, 4 * 16 * 128], BF16) as WEP,
              nc.sbuf_tensor(C.name("sge"), [128, 2 * EUSE], F32) as SGE,
              nc.sbuf_tensor(C.name("obt"), [128, 2 * EUSE], BF16) as OBT,
              nc.psum_tensor(C.name("psga"), [128, 1024], F32) as psA,
              nc.psum_tensor(C.name("psgb"), [128, 1024], F32) as psB,
              nc.psum_tensor(C.name("psdd"), [128, 2048], F32) as psD2):
            wug = ext["L1_ffn_w_up_gate"]
            wdn = ext["L1_ffn_w_down"]
            state["m4_pan"] = {}
            prev_bt_pe = None
            for e in range(8):
                if prev_bt_pe is not None:
                    C.wait("sync", "pe", prev_bt_pe)
                bl = None
                for dc in range(NCH):
                    bl = C.dmat("sync", BT[:, dc * EUSE:(dc + 1) * EUSE],
                                md["buf"][e * ECAP:e * ECAP + EUSE,
                                          dc * 128:(dc + 1) * 128], "l_bt")
                C.wait("tensor", bl)
                silu_ms = None
                h_ms = None
                for i in range(NI):
                    for which, i2t in (("g", i), ("u", NI + i)):
                        slw = (0 if which == "g" else 1) * 2 + i % 2
                        wp = WEP[:, slw * 16 * 128:(slw + 1) * 16 * 128]
                        gk = (which, i % 2)
                        if gk in state["m4_pan"]:
                            C.wait("gpsimd", "pe", state["m4_pan"][gk])
                        lw = None
                        for c4 in range(4):
                            lw = C.dma("gpsimd",
                                       wp[:, c4 * 512:(c4 + 1) * 512]
                                       .rearrange("p (c n) -> p c n", c=4),
                                       wug[e, c4 * 512:(c4 + 1) * 512,
                                           i2t * 128:(i2t + 1) * 128]
                                       .rearrange("(c p) n -> p c n", p=128),
                                       f"g_we{slw}", )
                        C.wait("tensor", lw)
                        ps = psA if which == "g" else psB
                        if which == "g" and silu_ms is not None:
                            C.wait("tensor", "ac", silu_ms)
                        if which == "u" and h_ms is not None:
                            C.wait("tensor", "dv", h_ms)
                        for dc in range(NCH):
                            pe_l = C.op("tensor", lambda E, wp=wp, dc=dc, ps=ps: E.matmul(
                                ps[:, 0:512], wp[:, dc * 128:(dc + 1) * 128],
                                BT[:, dc * EUSE:dc * EUSE + 512],
                                start=(dc == 0), stop=(dc == NCH - 1)), None)
                            pe_l = C.op("tensor", lambda E, wp=wp, dc=dc, ps=ps: E.matmul(
                                ps[:, 512:512 + (EUSE - 512)],
                                wp[:, dc * 128:(dc + 1) * 128],
                                BT[:, dc * EUSE + 512:dc * EUSE + EUSE],
                                start=(dc == 0), stop=(dc == NCH - 1)),
                                "pe" if dc == NCH - 1 else None)
                        if which == "g":
                            gate_pe = pe_l
                        else:
                            up_pe = pe_l
                        state["m4_pan"][(which, i % 2)] = pe_l
                    C.wait("scalar", "pe", gate_pe)
                    if h_ms is not None:
                        C.wait("scalar", "dv", h_ms)
                    sg = SGE[:, (i % 2) * EUSE:(i % 2) * EUSE + EUSE]
                    silu_ms = C.op("scalar", lambda E, sg=sg: E.activation(
                        sg, psA[:, 0:EUSE], AF.Silu), "ac")
                    C.wait("vector", "ac", silu_ms)
                    C.wait("vector", "pe", up_pe)
                    h_ms = C.op("vector", lambda E, sg=sg, i=i: E.tensor_tensor(
                        HE[:, i * EUSE:(i + 1) * EUSE], sg, psB[:, 0:EUSE],
                        op=AluOpType.mult), "dv")
                C.wait("tensor", "dv", h_ms)
                ob_st = {}
                dmm = {}
                gu_done = C.val("pe")
                for dt in range(NCH):
                    cur = dt % 2
                    wp = WEP[:, cur * 16 * 128:cur * 16 * 128 + NI * 128]
                    if dt >= 2:
                        C.wait("gpsimd", "pe", dmm[dt - 2])
                    else:
                        C.wait("gpsimd", "pe", gu_done)
                    state["m4_pan"][("g", cur)] = gu_done
                    lw = None
                    for c4 in range(3):
                        nch4 = 4 if c4 < 2 else 3
                        lw = C.dma("gpsimd",
                                   wp[:, c4 * 512:c4 * 512 + nch4 * 128]
                                   .rearrange("p (c n) -> p c n", c=nch4),
                                   wdn[e, c4 * 512:c4 * 512 + nch4 * 128,
                                       dt * 128:(dt + 1) * 128]
                                   .rearrange("(c p) n -> p c n", p=128),
                                   f"g_wd{cur}")
                    C.wait("tensor", lw)
                    pd = psD2[:, cur * 1024:cur * 1024 + 1024]
                    if dt >= 2:
                        C.wait("tensor", "dv", dmm_ev[dt - 2])
                    for ic in range(NI):
                        dmm[dt] = C.op("tensor", lambda E, wp=wp, ic=ic, pd=pd: E.matmul(
                            pd[:, 0:512], wp[:, ic * 128:(ic + 1) * 128],
                            HE[:, ic * EUSE:ic * EUSE + 512],
                            start=(ic == 0), stop=(ic == NI - 1)), None)
                        dmm[dt] = C.op("tensor", lambda E, wp=wp, ic=ic, pd=pd: E.matmul(
                            pd[:, 512:EUSE], wp[:, ic * 128:(ic + 1) * 128],
                            HE[:, ic * EUSE + 512:(ic + 1) * EUSE],
                            start=(ic == 0), stop=(ic == NI - 1)),
                            "pe" if ic == NI - 1 else None)
                    C.wait("vector", "pe", dmm[dt])
                    if dt >= 2:
                        C.wait("vector", ob_st[dt - 2])
                    if dt == 0:
                        dmm_ev = {}
                    ob = OBT[:, cur * EUSE:(cur + 1) * EUSE]
                    dmm_ev[dt] = C.op("vector", lambda E, pd=pd, ob=ob: E.tensor_copy(
                        ob, pd[:, 0:EUSE]), "dv")
                    C.wait("sync", "dv", dmm_ev[dt])
                    ob_st[dt] = C.dma("sync", md["obt"][dt * 128:(dt + 1) * 128,
                                      e * ECAP:e * ECAP + EUSE], ob, f"l_ob{cur}")
                C.wait("sync", ob_st[NCH - 1])
                C.wait("sync", ob_st[NCH - 2])
                prev_bt_pe = dmm[NCH - 1]
                state["m4_pan"][("g", 0)] = dmm[NCH - 2]
                state["m4_pan"][("g", 1)] = dmm[NCH - 1]
        C.phase()

        mes = contextlib.ExitStack()
        with mes:
            YG = mes.enter_context(nc.sbuf_tensor(C.name("yg"), [128, T], BF16))
            TMY = mes.enter_context(nc.sbuf_tensor(C.name("tmy"), [128, T], BF16))
            YB = mes.enter_context(nc.sbuf_tensor(C.name("yb"), [128, 2 * T], BF16))
            OTB = mes.enter_context(nc.sbuf_tensor(C.name("ot"), [128, T], BF16))
            zm = C.op("gpsimd", lambda E: E.memset(TMY[0:8, :], 0.0), "gp")
            C.wait("sync", "gp", zm)
            zs = None
            for e in range(8):
                zs = C.dma("sync", md["otm"][e * ECAP + EUSE:(e + 1) * ECAP, :],
                           TMY[0:ECAP - EUSE, :], "l_zs")
            ot_st = {}
            i = 0
            for e in range(8):
                for kk in range(EUSE // 128):
                    sl = i % 2
                    if i >= 1:
                        C.wait("sync", ot_st[i - 1])
                    tl = C.dmat("sync", OTB[:],
                                md["obt"][:, e * ECAP + kk * 128:e * ECAP + (kk + 1) * 128],
                                f"l_otl{sl}")
                    C.wait("sync", tl)
                    ot_st[i] = C.dma("sync",
                                     md["otm"][e * ECAP + kk * 128:e * ECAP + (kk + 1) * 128, :],
                                     OTB[:], f"l_ots{sl}")
                    i += 1
            C.wait("sync", ot_st[i - 1])
            C.wait("sync", ot_st[i - 2])
            C.wait("sync", zs)
            ot_done = (ot_st[i - 1], ot_st[i - 2])
            C.wait("gpsimd", ot_done[0])
            C.wait("gpsimd", ot_done[1])
            C.wait("gpsimd", zs)
            yst = {}
            cm_ms = {}
            for tt in range(16):
                sl = tt % 2
                ga = C.inc(f"g_ga{sl}", 16)
                if tt >= 1:
                    C.wait("gpsimd", "dv", cm_ms[tt - 1])
                C.q["gpsimd"].append(lambda E, S, sl=sl, tt=tt: E.indirect_dma_start(
                    out=YG[:], out_offset=None,
                    in_=md["otm"], in_offset=bass.IndirectOffsetOnAxis(
                        ap=state["GDU"][:, tt:tt + 1], axis=0),
                ).then_inc(S[f"g_ga{sl}"], 16))
                C.wait("vector", (f"g_ga{sl}", ga))
                t1 = C.op("vector", lambda E, sl=sl, tt=tt: E.tensor_scalar(
                    TMY[:], YG[:], state["GD"][:, tt:tt + 1], None,
                    op0=AluOpType.mult), "dv")
                C.wait("gpsimd", "dv", t1)
                gb = C.inc(f"g_ga{sl}", 16)
                C.q["gpsimd"].append(lambda E, S, sl=sl, tt=tt: E.indirect_dma_start(
                    out=YG[:], out_offset=None,
                    in_=md["otm"], in_offset=bass.IndirectOffsetOnAxis(
                        ap=state["GDU"][:, 16 + tt:17 + tt], axis=0),
                ).then_inc(S[f"g_ga{sl}"], 16))
                C.wait("vector", (f"g_ga{sl}", gb))
                if tt >= 2:
                    C.wait("vector", yst[tt - 2])
                cm_ms[tt] = C.op("vector", lambda E, sl=sl, tt=tt: E.scalar_tensor_tensor(
                    YB[:, sl * T:sl * T + T], in0=YG[:],
                    scalar=state["GD"][:, 16 + tt:17 + tt], in1=TMY[:],
                    op0=AluOpType.mult, op1=AluOpType.add), "dv")
                C.wait("sync", "dv", cm_ms[tt])
                yst[tt] = C.dma("sync", md["y"][tt * 128:(tt + 1) * 128, :],
                                YB[:, sl * T:sl * T + T], f"l_y{sl}")
            C.wait("sync", yst[15])
            C.wait("sync", yst[14])
            XL2 = mes.enter_context(nc.sbuf_tensor(C.name("xln2"), [128, NCH * T], BF16))
            WSP = mes.enter_context(nc.sbuf_tensor(C.name("wsp"), [128, 4 * 16 * 128], BF16))
            SGS = mes.enter_context(nc.sbuf_tensor(C.name("sgs"), [128, 2 * T], F32))
            HBS = mes.enter_context(nc.sbuf_tensor(C.name("hbs"), [128, 2 * T], BF16))
            psG2 = mes.enter_context(nc.psum_tensor(C.name("psg2"), [128, 2048], F32))
            psU2 = mes.enter_context(nc.psum_tensor(C.name("psu2"), [128, 2048], F32))
            xl = None
            for c in range(NCH):
                xl = C.dma("sync", XL2[:, c * T:(c + 1) * T],
                           md["xln"][c * 128:(c + 1) * 128, :], "l_xst")
            C.wait("tensor", xl)
            swu = ext["L1_ffn_shared_w_up_gate"]
            silu_ms = {}
            h_ms = {}
            hst_ms = {}
            gate_pe = {}
            up_pe = {}
            for t in range(NJ):
                cur = t % 2
                for which, jt in (("g", t), ("u", NJ + t)):
                    slw = (0 if which == "g" else 1) * 2 + cur
                    wp = WSP[:, slw * 16 * 128:(slw + 1) * 16 * 128]
                    if t >= 2:
                        C.wait("gpsimd", "pe", (gate_pe if which == "g" else up_pe)[t - 2])
                    lw = None
                    for c4 in range(4):
                        lw = C.dma("gpsimd",
                                   wp[:, c4 * 512:(c4 + 1) * 512]
                                   .rearrange("p (c n) -> p c n", c=4),
                                   swu[c4 * 512:(c4 + 1) * 512, jt * 128:(jt + 1) * 128]
                                   .rearrange("(c p) n -> p c n", p=128),
                                   f"g_ws{slw}")
                    C.wait("tensor", lw)
                    ps = psG2 if which == "g" else psU2
                    if which == "g" and t >= 1:
                        C.wait("tensor", "ac", silu_ms[t - 1])
                    if which == "u" and t >= 1:
                        C.wait("tensor", "dv", h_ms[t - 1])
                    pe_l = None
                    for dc in range(NCH):
                        for j in range(4):
                            pe_l = C.op("tensor", lambda E, wp=wp, dc=dc, j=j, ps=ps: E.matmul(
                                ps[:, j * 512:(j + 1) * 512], wp[:, dc * 128:(dc + 1) * 128],
                                XL2[:, dc * T + j * 512:dc * T + (j + 1) * 512],
                                start=(dc == 0), stop=(dc == NCH - 1)),
                                "pe" if (dc == NCH - 1 and j == 3) else None)
                    if which == "g":
                        gate_pe[t] = pe_l
                    else:
                        up_pe[t] = pe_l
                C.wait("scalar", "pe", gate_pe[t])
                if t >= 1:
                    C.wait("scalar", "dv", h_ms[t - 1])
                sgb = SGS[:, cur * T:(cur + 1) * T]
                silu_ms[t] = C.op("scalar", lambda E, sgb=sgb: E.activation(
                    sgb, psG2[:], AF.Silu), "ac")
                C.wait("vector", "pe", up_pe[t])
                C.wait("vector", "ac", silu_ms[t])
                if t >= 2:
                    C.wait("vector", hst_ms[t - 2])
                hb = HBS[:, cur * T:(cur + 1) * T]
                h_ms[t] = C.op("vector", lambda E, hb=hb, sgb=sgb: E.tensor_tensor(
                    hb, sgb, psU2[:], op=AluOpType.mult), "dv")
                C.wait("sync", "dv", h_ms[t])
                hst_ms[t] = C.dma("sync", md["hsh"][t * 128:(t + 1) * 128, :], hb,
                                  f"l_hs{cur}")
            C.wait("sync", hst_ms[NJ - 1])
            C.wait("sync", hst_ms[NJ - 2])

        C.phase()
        with (nc.psum_tensor(C.name("psd3"), [128, 2048], F32) as psD3,
              nc.sbuf_tensor(C.name("epb3"), [128, 2 * 1024], F32) as EPB3,
              nc.sbuf_tensor(C.name("hhs"), [128, NJ * 1024], BF16) as HHS,
              nc.sbuf_tensor(C.name("wds"), [128, 2 * NJ * 128], BF16) as WDS):
            swd = ext["L1_ffn_shared_w_down"]
            prev_pe = None
            ep_ms = {}
            ep_st = {}
            dmm2 = {}
            for half in range(2):
                if prev_pe is not None:
                    C.wait("sync", "pe", prev_pe)
                lh = None
                for t in range(NJ):
                    lh = C.dma("sync", HHS[:, t * 1024:(t + 1) * 1024],
                               md["hsh"][t * 128:(t + 1) * 128,
                                         half * 1024:half * 1024 + 1024], "l_hh2")
                C.wait("tensor", lh)
                for dt0 in range(NCH):
                    dt = half * NCH + dt0
                    cur = dt % 2
                    wp = WDS[:, cur * NJ * 128:(cur + 1) * NJ * 128]
                    if dt >= 2:
                        C.wait("gpsimd", "pe", dmm2[dt - 2])
                    lw = None
                    for c4 in range(11):
                        lw = C.dma("gpsimd",
                                   wp[:, c4 * 512:(c4 + 1) * 512]
                                   .rearrange("p (c n) -> p c n", c=4),
                                   swd[c4 * 512:(c4 + 1) * 512, dt0 * 128:(dt0 + 1) * 128]
                                   .rearrange("(c p) n -> p c n", p=128),
                                   f"g_wd2{cur}")
                    C.wait("tensor", lw)
                    pd = psD3[:, cur * 1024:cur * 1024 + 1024]
                    ebuf = EPB3[:, cur * 1024:cur * 1024 + 1024]
                    if dt >= 2:
                        C.wait("tensor", "dv", ep_ms[dt - 2])
                        C.wait("sync", ep_st[dt - 2])
                    emit_ep_load(C, nc, state, dt0, half * 1024, 1024, ebuf, f"l_ep{cur}")
                    for jc in range(NJ):
                        for j in range(2):
                            dmm2[dt] = C.op("tensor", lambda E, wp=wp, jc=jc, j=j, pd=pd: E.matmul(
                                pd[:, j * 512:(j + 1) * 512], wp[:, jc * 128:(jc + 1) * 128],
                                HHS[:, jc * 1024 + j * 512:jc * 1024 + (j + 1) * 512],
                                start=(jc == 0), stop=(jc == NJ - 1)),
                                "pe" if (jc == NJ - 1 and j == 1) else None)
                    C.wait("vector", "pe", dmm2[dt])
                    ep_st[dt], ep_ms[dt] = emit_ep(C, nc, state, R, pd, 0.5, dt0,
                                                   half * 1024, 1024, ebuf, False,
                                                   f"l_eps{cur}")
                C.wait("tensor", "dv", ep_ms[half * NCH + NCH - 1])
                C.wait("tensor", "dv", ep_ms[half * NCH + NCH - 2])
                C.wait("sync", ep_st[half * NCH + NCH - 1])
                C.wait("sync", ep_st[half * NCH + NCH - 2])
                prev_pe = dmm2[half * NCH + NCH - 1]
        C.phase()
        state["r32_src"] = state["r32_dst"]
        state["r32_dst"] = (state["r32b"] if state["r32_dst"] is state["r32a"]
                            else state["r32a"])
        with (nc.sbuf_tensor(C.name("yt"), [128, 2 * T], BF16) as YT,
              nc.sbuf_tensor(C.name("epb5"), [128, 2 * T], F32) as EPB5):
            ep_ms = {}
            ep_st = {}
            for dc in range(NCH):
                cur = dc % 2
                if dc >= 2:
                    C.wait("sync", "dv", ep_ms[dc - 2])
                    C.wait("sync", ep_st[dc - 2])
                yl = C.dmat("sync", YT[:, cur * T:(cur + 1) * T],
                            md["y"][:, dc * 128:(dc + 1) * 128], f"l_yt{cur}")
                ebuf = EPB5[:, cur * T:(cur + 1) * T]
                emit_ep_load(C, nc, state, dc, 0, T, ebuf, f"l_ep{cur}")
                C.wait("vector", yl)
                ep_st[dc], ep_ms[dc] = emit_ep(C, nc, state, R,
                                               YT[:, cur * T:(cur + 1) * T], 0.5, dc,
                                               0, T, ebuf, True, f"l_eps{cur}")
            C.wait("tensor", "dv", ep_ms[NCH - 1])
            C.wait("tensor", "dv", ep_ms[NCH - 2])
            C.wait("sync", ep_st[NCH - 1])
            C.wait("sync", ep_st[NCH - 2])



def build_program(sublayers):
    nc = bass.Bass("TRN2", target_bir_lowering=False)
    ext = {}

    def ein(name, shape, dt):
        ext[name] = nc.dram_tensor(name, shape, dt, kind="ExternalInput").ap()

    ein("x_in", [D, T], F32)
    need = set(sublayers)
    if {"ffn0_a", "ffn0_b"} & need:
        ein("L0_ffn_lut", [1, 16], F32)
        ein("L0_ffn_gate_idx", [IFF, D], I32)
        ein("L0_ffn_up_idx", [IFF, D], I32)
        ein("L0_ffn_down_idx", [D, IFF], I32)
    if "attn0" in need:
        ein("L0_attn_Wq", [D, D], F32)
        ein("L0_attn_Wkv", [D, 256], F32)
        ein("L0_attn_Wout", [D, D], F32)
        for nm in ("rope0_cos_q", "rope0_sin_q", "rope0_cos_k", "rope0_sin_k"):
            ein(nm, [128, T], F32)
    if "attn1" in need:
        for nm in ("L1_attn_Wq", "L1_attn_Wk", "L1_attn_Wv", "L1_attn_Wo"):
            ein(nm, [D, D], F32)
        for nm in ("rope1_cos_q", "rope1_sin_q", "rope1_cos_k", "rope1_sin_k"):
            ein(nm, [128, T], F32)
    if {"moe_a", "moe_b"} & need:
        if "identf" not in ext:
            ein("identf", [128, 128], F32)
        ein("L1_ffn_router", [D, 8], F32)
        ein("L1_ffn_w_up_gate", [8, D, 2816], F32)
        ein("L1_ffn_w_down", [8, 1408, D], F32)
        ein("L1_ffn_shared_w_up_gate", [D, 11264], F32)
        ein("L1_ffn_shared_w_down", [5632, D], F32)
        ein("rank_mask", [8, 1], F32)
    if {"attn0", "attn1"} & need:
        ein("ident", [128, 128], BF16)
        ein("perm0", [128, 128], BF16)
        ein("perm1", [128, 128], BF16)
        ein("causal", [128, 2048], F32)
        if "identf" not in ext:
            ein("identf", [128, 128], F32)

    x_out = nc.dram_tensor("x_out", [D, T], F32, kind="ExternalOutput").ap()

    C = Ctx(nc)
    for s in ("pe", "dv", "ac", "gp"):
        C.sem(s)

    es = contextlib.ExitStack()
    with es:
        R = es.enter_context(nc.sbuf_tensor("R", [128, NCH * T], BF16))
        K = {}
        K["ONES1"] = es.enter_context(nc.sbuf_tensor("ones1", [1, 128], BF16))
        K["ONESC"] = es.enter_context(nc.sbuf_tensor("onesc", [128, 1], BF16))
        K["ONES1F"] = es.enter_context(nc.sbuf_tensor("ones1f", [1, 128], F32))
        K["ONESCF"] = es.enter_context(nc.sbuf_tensor("onescf", [128, 1], F32))
        K["EPS"] = es.enter_context(nc.sbuf_tensor("epsc", [128, 1], F32))
        state = {}
        if {"ffn0_a", "ffn0_b"} & need:
            state["LUTBC"] = es.enter_context(nc.sbuf_tensor("lutbc", [128, 16], F32))
        if {"moe_a", "moe_b"} & need:
            state["GD"] = es.enter_context(nc.sbuf_tensor("gdcols", [128, 32], F32))
            state["GDU"] = es.enter_context(nc.sbuf_tensor("gducols", [128, 32], U32))

        C.op("gpsimd", lambda E: E.memset(K["ONES1"][:], 1.0))
        C.op("gpsimd", lambda E: E.memset(K["ONESC"][:], 1.0))
        C.op("gpsimd", lambda E: E.memset(K["ONES1F"][:], 1.0))
        C.op("gpsimd", lambda E: E.memset(K["EPS"][:], 1e-5))
        g_ones = C.op("gpsimd", lambda E: E.memset(K["ONESCF"][:], 1.0), "gp")

        kr = "ExternalOutput" if os.environ.get("DBG_MOE") == "1" else "Internal"
        state["r32a"] = nc.dram_tensor("r32a", [D, T], F32, kind=kr).ap()
        state["r32b"] = nc.dram_tensor("r32b", [D, T], F32, kind=kr).ap()
        state["r32_src"] = ext["x_in"]
        state["r32_dst"] = state["r32a"]
        state["ep_pend"] = {}
        ld_x = None
        for c in range(NCH):
            ld_x = C.dma("gpsimd", R[:, c * T:(c + 1) * T],
                         ext["x_in"][c * 128:(c + 1) * 128, :], "g_xin")
        for e in ("tensor", "vector", "scalar"):
            C.wait(e, ld_x)
            C.wait(e, "gp", g_ones)
        C.wait("gpsimd", "gp", g_ones)
        C.barrier()

        for name in sublayers:
            if name in ("ffn0_a", "ffn0_b"):
                emit_ffn0(C, nc, ext, R, K, state)
            elif name == "attn0":
                emit_attn0(C, nc, ext, R, K, state)
            elif name == "attn1":
                emit_attn1(C, nc, ext, R, K, state)
            elif name in ("moe_a", "moe_b"):
                emit_moe(C, nc, ext, R, K, state)
            else:
                raise ValueError(name)
            if state.pop("r32_written", False):
                old_src = state["r32_src"]
                state["r32_src"] = state["r32_dst"]
                state["r32_dst"] = (state["r32b"] if state["r32_dst"] is state["r32a"]
                                    else state["r32a"])
            C.barrier()

        with nc.sbuf_tensor("xo", [128, 2 * T], F32) as XO:
            st_ms = {}
            rsrc = state["r32_src"]
            for c in range(NCH):
                buf = XO[:, (c % 2) * T:(c % 2) * T + T]
                if c >= 2:
                    C.wait("sync", st_ms[c - 2])
                lb = C.dma("sync", buf, rsrc[c * 128:(c + 1) * 128, :], f"xo{c % 2}")
                C.wait("sync", lb)
                st_ms[c] = C.dma("sync", x_out[c * 128:(c + 1) * 128, :], buf, f"st{c % 2}")
            C.wait("sync", st_ms[NCH - 1])
            C.wait("sync", st_ms[NCH - 2])

        S = {}
        for sn in C.sem_names:
            S[sn] = es.enter_context(nc.semaphore(sn))
        block = es.enter_context(nc.Block())
        for eng in ENGINES:
            ops = C.q[eng]
            def replay(E, ops=ops):
                for f in ops:
                    f(E, S)
            getattr(block, eng)(replay)
    return nc



def _rope_tables():
    t = np.arange(T, dtype=np.float64)
    out = {}
    inv = 1.0 / 10000.0 ** (np.arange(0, 128, 2, dtype=np.float64) / 128)
    fr = t[:, None] * inv[None, :]
    pos = np.concatenate([fr, fr], axis=1)
    sc = 128 ** -0.5
    cos, sin = np.cos(pos), np.sin(pos)
    out["rope0_cos_q"] = (cos.T * sc).astype(np.float32)
    out["rope0_sin_q"] = (sin.T * sc).astype(np.float32)
    out["rope0_cos_k"] = cos.T.astype(np.float32)
    out["rope0_sin_k"] = sin.T.astype(np.float32)
    th = 10000.0 ** (-2.0 * np.arange(64, dtype=np.float64) / 128)
    ang = t[:, None] * th[None, :]
    c1, s1 = np.cos(ang), np.sin(ang)
    cexp = np.repeat(c1, 2, axis=1)
    sexp = np.repeat(s1, 2, axis=1).copy()
    sexp[:, 0::2] *= -1.0
    sc1 = 128 ** -0.25
    out["rope1_cos_q"] = (cexp.T * (128 ** -0.5)).astype(np.float32)
    out["rope1_sin_q"] = (sexp.T * (128 ** -0.5)).astype(np.float32)
    out["rope1_cos_k"] = cexp.T.astype(np.float32)
    out["rope1_sin_k"] = sexp.T.astype(np.float32)
    return out


def _causal():
    m = np.full((128, 2048), NEG, np.float32)
    p = np.arange(128)[:, None]
    c = np.arange(512)[None, :]
    for v in range(4):
        m[:, v * 512:(v + 1) * 512] = np.where(c <= 128 * v + p, 0.0, NEG)
    return m


def _perms():
    p0 = np.zeros((128, 128), np.float32)
    for p in range(64):
        p0[p + 64, p] = -1.0
        p0[p, p + 64] = 1.0
    p1 = np.zeros((128, 128), np.float32)
    for i in range(64):
        p1[2 * i + 1, 2 * i] = 1.0
        p1[2 * i, 2 * i + 1] = 1.0
    return (p0.astype(ml_dtypes.bfloat16), p1.astype(ml_dtypes.bfloat16))


def kernel(**inputs):
    x = np.asarray(inputs["L0_attn_x"])
    consts = _rope_tables()
    perm0, perm1 = _perms()
    ident = np.eye(128).astype(ml_dtypes.bfloat16)
    causal = _causal()

    all_subs = [s for launch in LAUNCHES for s in launch]
    cur = [np.ascontiguousarray(x[c].T) for c in range(NCORES)]

    for launch in LAUNCHES:
        nc = build_program(launch)
        need = set(launch)
        in_maps = []
        for c in range(NCORES):
            im = {"x_in": cur[c]}
            if {"ffn0_a", "ffn0_b"} & need:
                im["L0_ffn_lut"] = np.asarray(inputs["L0_ffn_lut"]).reshape(1, 16).astype(np.float32)
                im["L0_ffn_gate_idx"] = np.asarray(inputs["L0_ffn_gate_idx"])
                im["L0_ffn_up_idx"] = np.asarray(inputs["L0_ffn_up_idx"])
                im["L0_ffn_down_idx"] = np.asarray(inputs["L0_ffn_down_idx"])
            if "attn0" in need:
                im["L0_attn_Wq"] = np.asarray(inputs["L0_attn_Wq"])
                im["L0_attn_Wkv"] = np.asarray(inputs["L0_attn_Wkv"])
                im["L0_attn_Wout"] = np.asarray(inputs["L0_attn_Wout"])
                for nm in ("rope0_cos_q", "rope0_sin_q", "rope0_cos_k", "rope0_sin_k"):
                    im[nm] = consts[nm]
            if "attn1" in need:
                for nm in ("L1_attn_Wq", "L1_attn_Wk", "L1_attn_Wv", "L1_attn_Wo"):
                    im[nm] = np.asarray(inputs[nm])
                for nm in ("rope1_cos_q", "rope1_sin_q", "rope1_cos_k", "rope1_sin_k"):
                    im[nm] = consts[nm]
            if {"moe_a", "moe_b"} & need:
                for nm in ("L1_ffn_router", "L1_ffn_w_up_gate", "L1_ffn_w_down",
                           "L1_ffn_shared_w_up_gate", "L1_ffn_shared_w_down"):
                    im[nm] = np.asarray(inputs[nm])
                im["rank_mask"] = (np.arange(8) < c).astype(np.float32).reshape(8, 1)
                im["identf"] = np.eye(128, dtype=np.float32)
            if {"attn0", "attn1"} & need:
                im["ident"] = ident
                im["perm0"] = perm0
                im["perm1"] = perm1
                im["causal"] = causal
                im["identf"] = np.eye(128, dtype=np.float32)
            in_maps.append(im)
        res = run_bass_kernel_spmd(nc, in_maps, core_ids=list(range(NCORES)))
        if any(os.environ.get(k) == "1" for k in ("DBG_ATTN", "DBG_GHOST", "DBG_LN", "DBG_MOE")):
            kernel.dbg = res.results
        cur = [np.asarray(r["x_out"]) for r in res.results]

    out = np.stack([c.T for c in cur], axis=0).astype(np.float32)
    return np.ascontiguousarray(out)


# version 06 (final)
# speedup vs baseline: 1.1213x; 1.1213x over previous
import contextlib
import numpy as np
import ml_dtypes
import concourse.bass as bass
import concourse.mybir as mybir
from concourse.alu_op_type import AluOpType
from concourse.bass_utils import run_bass_kernel_spmd

F32 = mybir.dt.float32
BF16 = mybir.dt.bfloat16
I32 = mybir.dt.int32
U32 = mybir.dt.uint32
AF = mybir.ActivationFunctionType

D = 2048
T = 2048
NCH = 16
NCORES = 8
IFF = 5504
NIT = IFF // 128
NEG = -1.0e30

ENGINES = ("sync", "tensor", "vector", "scalar", "gpsimd")

LAUNCHES = [["ffn0_a", "attn0", "ffn0_b", "moe_a", "attn1", "moe_b"]]


class Ctx:
    def __init__(self, nc):
        self.nc = nc
        self.q = {e: [] for e in ENGINES}
        self.ctr = {}
        self.sem_names = []
        self.uid = 0

    def name(self, stem):
        self.uid += 1
        return f"{stem}_{self.uid}"

    def sem(self, name):
        if name not in self.ctr:
            self.ctr[name] = 0
            self.sem_names.append(name)
        return name

    def inc(self, name, n=1):
        self.ctr[name] += n
        return self.ctr[name]

    def val(self, name):
        return self.ctr[name]

    def wait(self, eng, sname, value=None):
        if value is None:
            sname, value = sname
        if value <= 0:
            return
        assert value <= self.ctr.get(sname, 0), (eng, sname, value, self.ctr.get(sname, 0))
        self.q[eng].append(lambda E, S: E.wait_ge(S[sname], value))

    def dma(self, eng, out, in_, sname="ld", **kw):
        self.sem(sname)
        tgt = self.inc(sname, 16)
        self.q[eng].append(lambda E, S: E.dma_start(out=out, in_=in_, **kw).then_inc(S[sname], 16))
        return (sname, tgt)

    def dmat(self, eng, out, in_, sname="ld"):
        self.sem(sname)
        tgt = self.inc(sname, 16)
        self.q[eng].append(lambda E, S: E.dma_start_transpose(out, in_).then_inc(S[sname], 16))
        return (sname, tgt)

    def op(self, eng, fn, sname=None, n=1):
        if sname is None:
            self.q[eng].append(lambda E, S: fn(E))
            return None
        tgt = self.inc(sname, n)
        def run(E, S):
            fn(E).then_inc(S[sname], n)
        self.q[eng].append(run)
        return tgt

    def phase(self):
        self.barrier()

    def barrier(self):
        snap = {sn: v for sn, v in self.ctr.items() if v > 0}
        for e in ENGINES:
            for sn, v in snap.items():
                self.wait(e, sn, v)



def emit_ln(C, nc, R, K, out_mbc, out_rbc, SQ, out_rows=None):
    with (nc.psum_tensor(C.name("psln"), [128, 2048], F32) as ps,
          nc.sbuf_tensor(C.name("rows"), [1, 2 * T], F32) as ROWS):
        ONESC, ONES1F = K["ONESC"], K["ONES1F"]
        pe0 = None
        for c in range(NCH):
            for j in range(4):
                pe0 = C.op("tensor", lambda E, c=c, j=j: E.matmul(
                    ps[0:1, j * 512:(j + 1) * 512], ONESC[:],
                    R[:, c * T + j * 512:c * T + (j + 1) * 512],
                    start=(c == 0), stop=(c == NCH - 1)),
                    "pe" if (c == NCH - 1 and j == 3) else None)
        C.wait("vector", "pe", pe0)
        e1 = C.op("vector", lambda E: E.tensor_copy(ROWS[0:1, 0:T], ps[0:1, 0:T]), "dv")
        sq_ms = {}
        pe_ms = {}
        for c in range(NCH):
            buf = SQ[:, (c % 2) * T:(c % 2) * T + T]
            if c >= 2:
                C.wait("scalar", "pe", pe_ms[c - 2])
            sq_ms[c] = C.op("scalar", lambda E, buf=buf, c=c: E.activation(
                buf, R[:, c * T:(c + 1) * T], AF.Square), "ac")
            C.wait("tensor", "ac", sq_ms[c])
            if c == 0:
                C.wait("tensor", "dv", e1)
            for j in range(4):
                pe_ms[c] = C.op("tensor", lambda E, buf=buf, c=c, j=j: E.matmul(
                    ps[0:1, j * 512:(j + 1) * 512], ONESC[:],
                    buf[:, j * 512:(j + 1) * 512],
                    start=(c == 0), stop=(c == NCH - 1)),
                    "pe" if j == 3 else None)
        C.wait("vector", "pe", pe_ms[NCH - 1])
        e2 = C.op("vector", lambda E: E.tensor_copy(ROWS[0:1, T:2 * T], ps[0:1, 0:T]), "dv")
        C.wait("vector", "dv", e2)
        d1 = C.op("vector", lambda E: E.tensor_scalar(ROWS[0:1, 0:T], ROWS[0:1, 0:T],
                  1.0 / D, None, op0=AluOpType.mult), "dv")
        C.wait("vector", "dv", d1)
        d2 = C.op("vector", lambda E: E.tensor_scalar(ROWS[0:1, T:2 * T], ROWS[0:1, T:2 * T],
                  1.0 / D, None, op0=AluOpType.mult), "dv")
        C.wait("vector", "dv", d2)
        d3 = C.op("vector", lambda E: E.tensor_tensor(ps[32:33, 0:T], ROWS[0:1, 0:T],
                  ROWS[0:1, 0:T], op=AluOpType.mult), "dv")
        C.wait("vector", "dv", d3)
        d4 = C.op("vector", lambda E: E.tensor_tensor(ROWS[0:1, T:2 * T], ROWS[0:1, T:2 * T],
                  ps[32:33, 0:T], op=AluOpType.subtract), "dv")
        if os.environ.get("DBG_LN") == "1":
            dbg_r = nc.dram_tensor(C.name("dbg_rows"), [2, T], F32,
                                   kind="ExternalOutput").ap()
            C.wait("sync", "dv", e2)
            dr = C.dma("sync", dbg_r[1:2, :], ROWS[0:1, T:2 * T], "l_dbg2")
            C.wait("sync", "dv", d4)
            dr = C.dma("sync", dbg_r[0:1, :], ROWS[0:1, 0:T], "l_dbg2")
            C.wait("vector", dr)
        C.wait("scalar", "dv", d4)
        a1 = C.op("scalar", lambda E: E.activation(ROWS[0:1, T:2 * T], ROWS[0:1, T:2 * T],
                  AF.Ln, bias=K["EPS"][0:1, :]), "ac")
        C.wait("scalar", "ac", a1)
        a2 = C.op("scalar", lambda E: E.activation(ROWS[0:1, T:2 * T], ROWS[0:1, T:2 * T],
                  AF.Exp, scale=-0.5), "ac")
        C.wait("tensor", "dv", d4)
        for j in range(4):
            pb = C.op("tensor", lambda E, j=j: E.matmul(
                ps[:, j * 512:(j + 1) * 512], ONES1F[:],
                ROWS[0:1, j * 512:(j + 1) * 512], start=True, stop=True),
                "pe" if j == 3 else None)
        C.wait("vector", "pe", pb)
        em = C.op("vector", lambda E: E.tensor_copy(out_mbc[:], ps[:, 0:T]), "dv")
        C.wait("tensor", "ac", a2)
        C.wait("tensor", "dv", em)
        for j in range(4):
            pb2 = C.op("tensor", lambda E, j=j: E.matmul(
                ps[:, j * 512:(j + 1) * 512], ONES1F[:],
                ROWS[0:1, T + j * 512:T + (j + 1) * 512], start=True, stop=True),
                "pe" if j == 3 else None)
        C.wait("vector", "pe", pb2)
        er = C.op("vector", lambda E: E.tensor_copy(out_rbc[:], ps[:, 0:T]), "dv")
        if out_rows is not None:
            C.wait("vector", "dv", er)
            er = C.op("vector", lambda E: E.tensor_copy(out_rows[0:1, 0:T],
                      ROWS[0:1, 0:T]), "dv")
            C.wait("vector", "dv", er)
            er = C.op("vector", lambda E: E.tensor_copy(out_rows[0:1, T:2 * T],
                      ROWS[0:1, T:2 * T]), "dv")
    return er


def emit_xln(C, nc, R, m_bc, r_bc, XLN, dv_ready):
    C.wait("vector", "dv", dv_ready)
    with nc.sbuf_tensor(C.name("xtmp"), [128, 2 * T], F32) as TMP:
        ms = {}
        for c in range(NCH):
            tb = TMP[:, (c % 2) * T:(c % 2) * T + T]
            if c >= 2:
                C.wait("vector", "dv", ms[c - 2])
            t1 = C.op("vector", lambda E, tb=tb, c=c: E.tensor_tensor(
                tb, R[:, c * T:(c + 1) * T], m_bc[:], op=AluOpType.subtract), "dv")
            C.wait("vector", "dv", t1)
            ms[c] = C.op("vector", lambda E, tb=tb, c=c: E.tensor_tensor(
                XLN[:, c * T:(c + 1) * T], tb, r_bc[:], op=AluOpType.mult), "dv")
    return ms[NCH - 1]



def emit_ghost_prep(C, nc, ext, K, state):
    if "idxbf" in state:
        idxg, idxu, idxd = state["idxbf"]
    else:
        idxg = nc.dram_tensor("idxbf_g", [IFF, D], BF16).ap()
        idxu = nc.dram_tensor("idxbf_u", [IFF, D], BF16).ap()
        idxd = nc.dram_tensor("idxbf_d", [D, IFF], BF16).ap()
        state["idxbf"] = (idxg, idxu, idxd)
        kindg = "ExternalOutput" if os.environ.get("DBG_GHOST") == "1" else "Internal"
        state["wimg"] = {
            "g": nc.dram_tensor("wimg_g", [NIT, 128, 16 * 128], BF16, kind=kindg).ap(),
            "u": nc.dram_tensor("wimg_u", [NIT, 128, 16 * 128], BF16, kind=kindg).ap(),
            "d": nc.dram_tensor("wimg_d", [NCH, 128, NIT * 128], BF16, kind=kindg).ap(),
        }
    with nc.sbuf_tensor(C.name("stg"), [128, 2 * IFF], BF16) as STG:
        rounds = []
        for (src, dst, nrow, ncol) in ((ext["L0_ffn_gate_idx"], idxg, NIT, D),
                                       (ext["L0_ffn_up_idx"], idxu, NIT, D),
                                       (ext["L0_ffn_down_idx"], idxd, NCH, IFF)):
            for r in range(nrow):
                rounds.append((src, dst, r, ncol))
        st_ms = {}
        for i, (src, dst, r, ncol) in enumerate(rounds):
            buf = STG[:, (i % 2) * IFF:(i % 2) * IFF + ncol]
            if i >= 2:
                C.wait("gpsimd", st_ms[i - 2])
            ld = C.dma("gpsimd", buf, src[r * 128:(r + 1) * 128, :], f"g_stg{i % 2}")
            C.wait("sync", ld)
            st_ms[i] = C.dma("sync", dst[r * 128:(r + 1) * 128, :], buf, f"l_stg{i % 2}")
    LUTBC = state["LUTBC"]
    with (nc.sbuf_tensor(C.name("lutrow"), [1, 16], F32) as LR,
          nc.psum_tensor(C.name("pslut"), [128, 16], F32) as PSL):
        ld = C.dma("sync", LR[:], ext["L0_ffn_lut"], "l_const")
        C.wait("tensor", ld)
        pm = C.op("tensor", lambda E: E.matmul(PSL[:], K["ONES1F"][:], LR[:],
                  start=True, stop=True), "pe")
        C.wait("vector", "pe", pm)
        lb = C.op("vector", lambda E: E.tensor_copy(LUTBC[:], PSL[:]), "dv")
        C.wait("sync", st_ms[len(rounds) - 1])
        C.wait("sync", st_ms[len(rounds) - 2])
        C.wait("tensor", "dv", lb)
    C.phase()


def emit_ghost_select(C, nc, IDXT, OUT, TBUF, LUTBC, width):
    last = None
    for l in range(16):
        lc = LUTBC[:, l:l + 1]
        if l == 0:
            last = C.op("vector", lambda E, lc=lc: E.tensor_scalar(
                OUT[:, :width], IDXT[:, :width], float(0), lc,
                op0=AluOpType.is_equal, op1=AluOpType.mult), "dv")
        else:
            C.wait("vector", "dv", last)
            t1 = C.op("vector", lambda E, lc=lc, l=l: E.tensor_scalar(
                TBUF[:, :width], IDXT[:, :width], float(l), lc,
                op0=AluOpType.is_equal, op1=AluOpType.mult), "dv")
            C.wait("vector", "dv", t1)
            last = C.op("vector", lambda E: E.tensor_tensor(
                OUT[:, :width], OUT[:, :width], TBUF[:, :width], op=AluOpType.add), "dv")
    return last


import os
FORCE_JIT = os.environ.get("FORCE_JIT") == "1"




def emit_ep(C, nc, state, R, psum_ap, scale, dt, col0, width, buf, final, sname):
    ld, src_ap, dst_ap = state["ep_pend"].pop((dt, col0))
    C.wait("vector", ld)
    s1 = C.op("vector", lambda E: E.scalar_tensor_tensor(
        buf, in0=psum_ap, scalar=scale, in1=buf,
        op0=AluOpType.mult, op1=AluOpType.add), "dv")
    C.wait("sync", "dv", s1)
    stm = C.dma("sync", dst_ap, buf, sname)
    s2 = None
    if final:
        C.wait("vector", "dv", s1)
        s2 = C.op("vector", lambda E: E.tensor_copy(
            R[:, dt * T + col0:dt * T + col0 + width], buf), "dv")
    return stm, (s2 if s2 is not None else s1)


def emit_ep_load(C, nc, state, dt, col0, width, buf, sname):
    src = state["r32_src"]
    dst = state["r32_dst"]
    src_ap = (src[dt * 128:(dt + 1) * 128, col0:col0 + width])
    dst_ap = (dst[dt * 128:(dt + 1) * 128, col0:col0 + width])
    ld = C.dma("sync", buf, src_ap, sname)
    state["ep_pend"][(dt, col0)] = (ld, src_ap, dst_ap)


def emit_ffn0(C, nc, ext, R, K, state):
    state["r32_written"] = True
    first = not state.get("ghost_ready", False) or FORCE_JIT
    if not state.get("ghost_ready", False):
        emit_ghost_prep(C, nc, ext, K, state)
        state["ghost_ready"] = True
    idxg, idxu, idxd = state["idxbf"]
    wim = state["wimg"]
    LUTBC = state["LUTBC"]
    if "h_dram" not in state:
        kindh = "ExternalOutput" if os.environ.get("DBG_GHOST") == "1" else "Internal"
        state["h_dram"] = nc.dram_tensor("h_dram", [IFF, T], BF16, kind=kindh).ap()
    h_dram = state["h_dram"]

    with (nc.sbuf_tensor(C.name("mbc"), [128, T], BF16) as m_bc,
          nc.sbuf_tensor(C.name("rbc"), [128, T], BF16) as r_bc,
          nc.sbuf_tensor(C.name("sqx"), [128, 2 * T], BF16) as SQX,
          nc.sbuf_tensor(C.name("xln"), [128, NCH * T], BF16) as XLN):
        ln_ms = emit_ln(C, nc, R, K, m_bc, r_bc, SQX)
        xln_ms = emit_xln(C, nc, R, m_bc, r_bc, XLN, ln_ms)
        C.wait("tensor", "dv", xln_ms)
        if os.environ.get("DBG_GHOST") == "1":
            dbg_x = nc.dram_tensor(C.name("dbg_xln"), [NCH + 2, 128, T], BF16,
                                   kind="ExternalOutput").ap()
            C.wait("sync", "dv", xln_ms)
            dl = None
            for c in range(NCH):
                dl = C.dma("sync", dbg_x[c], XLN[:, c * T:(c + 1) * T], "l_dbg")
            dl = C.dma("sync", dbg_x[NCH], m_bc[:], "l_dbg")
            dl = C.dma("sync", dbg_x[NCH + 1], r_bc[:], "l_dbg")
            C.wait("sync", dl)
        C.phase()

        with (nc.psum_tensor(C.name("psG"), [128, 2048], F32) as psG,
              nc.psum_tensor(C.name("psU"), [128, 2048], F32) as psU,
              nc.sbuf_tensor(C.name("pg"), [128, 2 * 2048], BF16) as PG,
              nc.sbuf_tensor(C.name("pu"), [128, 2 * 2048], BF16) as PU,
              nc.sbuf_tensor(C.name("idxt"), [128, 2 * 2048], BF16) as IDT,
              nc.sbuf_tensor(C.name("tbuf"), [128, 2048], BF16) as TB,
              nc.sbuf_tensor(C.name("sg"), [128, 2 * 2048], F32) as SG,
              nc.sbuf_tensor(C.name("hh"), [128, 2 * 2048], BF16) as HH):
            silu_ms = {}
            h_ms = {}
            hst_ms = {}
            gate_pe = {}
            up_pe = {}
            pan_ms = {}
            sel_ms = {}
            for t in range(NIT):
                cur = t % 2
                pg = PG[:, cur * 2048:cur * 2048 + 2048]
                pu = PU[:, cur * 2048:cur * 2048 + 2048]
                if first:
                    for mi, (mat, idx_dram, pan, img) in enumerate(
                            (("g", idxg, pg, wim["g"]), ("u", idxu, pu, wim["u"]))):
                        idt = IDT[:, mi * 2048:mi * 2048 + 2048]
                        if t >= 1:
                            C.wait("sync", "dv", sel_ms[(mat, t - 1)])
                        ldt = None
                        for k in range(NCH):
                            ldt = C.dmat("sync", idt[:, k * 128:(k + 1) * 128],
                                         idx_dram[t * 128:(t + 1) * 128, k * 128:(k + 1) * 128],
                                         f"l_idt{mat}")
                        C.wait("vector", ldt)
                        if t >= 2:
                            C.wait("vector", "pe", (gate_pe if mat == "g" else up_pe)[t - 2])
                            C.wait("vector", pan_ms[(mat, t - 2)])
                        sel = emit_ghost_select(C, nc, idt, pan, TB, LUTBC, 2048)
                        sel_ms[(mat, t)] = sel
                        C.wait("sync", "dv", sel)
                        pan_ms[(mat, t)] = C.dma("sync", img[t], pan, f"l_pan{mat}{t % 2}")
                        C.wait("tensor", "dv", sel)
                else:
                    if t >= 2:
                        C.wait("sync", "pe", gate_pe[t - 2])
                        C.wait("sync", "pe", up_pe[t - 2])
                    lg = C.dma("sync", pg, wim["g"][t], f"l_pg{t % 2}")
                    lu = C.dma("sync", pu, wim["u"][t], f"l_pu{t % 2}")
                    C.wait("tensor", lg)
                    C.wait("tensor", lu)
                if t >= 1:
                    C.wait("tensor", "ac", silu_ms[t - 1])
                for k in range(NCH):
                    for j in range(4):
                        gate_pe[t] = C.op("tensor", lambda E, pg=pg, k=k, j=j: E.matmul(
                            psG[:, j * 512:(j + 1) * 512], pg[:, k * 128:(k + 1) * 128],
                            XLN[:, k * T + j * 512:k * T + (j + 1) * 512],
                            start=(k == 0), stop=(k == NCH - 1)),
                            "pe" if (k == NCH - 1 and j == 3) else None)
                C.wait("scalar", "pe", gate_pe[t])
                if t >= 1:
                    C.wait("scalar", "dv", h_ms[t - 1])
                sgb = SG[:, cur * 2048:cur * 2048 + 2048]
                silu_ms[t] = C.op("scalar", lambda E, sgb=sgb: E.activation(
                    sgb, psG[:], AF.Silu), "ac")
                if t >= 1:
                    C.wait("tensor", "dv", h_ms[t - 1])
                for k in range(NCH):
                    for j in range(4):
                        up_pe[t] = C.op("tensor", lambda E, pu=pu, k=k, j=j: E.matmul(
                            psU[:, j * 512:(j + 1) * 512], pu[:, k * 128:(k + 1) * 128],
                            XLN[:, k * T + j * 512:k * T + (j + 1) * 512],
                            start=(k == 0), stop=(k == NCH - 1)),
                            "pe" if (k == NCH - 1 and j == 3) else None)
                C.wait("vector", "pe", up_pe[t])
                C.wait("vector", "ac", silu_ms[t])
                if t >= 2:
                    C.wait("vector", hst_ms[t - 2])
                hb = HH[:, cur * 2048:cur * 2048 + 2048]
                h_ms[t] = C.op("vector", lambda E, hb=hb, sgb=sgb: E.tensor_tensor(
                    hb, sgb, psU[:], op=AluOpType.mult), "dv")
                C.wait("sync", "dv", h_ms[t])
                hst_ms[t] = C.dma("sync", h_dram[t * 128:(t + 1) * 128, :], hb, f"l_hh{t % 2}")
            C.wait("sync", hst_ms[NIT - 1])
            C.wait("sync", hst_ms[NIT - 2])
        C.phase()

    with (nc.psum_tensor(C.name("psD"), [128, 2048], F32) as psD,
          nc.sbuf_tensor(C.name("epb"), [128, 2 * 1024], F32) as EPB,
          nc.sbuf_tensor(C.name("hhalf"), [128, NIT * 1024], BF16) as HHF,
          nc.sbuf_tensor(C.name("wdp"), [128, 2 * NIT * 128], BF16) as WDP,
          nc.sbuf_tensor(C.name("idxt2"), [128, 2 * NIT * 128], BF16) as IDT2,
          nc.sbuf_tensor(C.name("tbuf2"), [128, NIT * 64], BF16) as TB2):
        prev_pe = None
        ep_ms = {}
        ep_st = {}
        dpan_ms = {}
        dsel_ms = {}
        dimg_ms = {}
        ldt_ms = {}

        def emit_idx_loads(dtl):
            sl = dtl % 2
            last = None
            for ic in range(NIT):
                last = C.dmat("sync", IDT2[:, (sl * NIT + ic) * 128:(sl * NIT + ic + 1) * 128],
                              idxd[dtl * 128:(dtl + 1) * 128, ic * 128:(ic + 1) * 128],
                              f"l_idtd{sl}")
            return last
        for half in range(2):
            if prev_pe is not None:
                C.wait("sync", "pe", prev_pe)
            lh = None
            for t in range(NIT):
                lh = C.dma("sync", HHF[:, t * 1024:(t + 1) * 1024],
                           h_dram[t * 128:(t + 1) * 128, half * 1024:half * 1024 + 1024], "l_hhf")
            C.wait("tensor", lh)
            for dt0 in range(NCH):
                dt = half * NCH + dt0
                cur = dt % 2
                wp = WDP[:, cur * NIT * 128:(cur + 1) * NIT * 128]
                if first and half == 0:
                    if dt == 0:
                        ldt_ms[0] = emit_idx_loads(0)
                    C.wait("vector", ldt_ms[dt])
                    if dt >= 2:
                        C.wait("vector", "pe", dpan_ms[dt - 2])
                        C.wait("vector", dimg_ms[dt - 2])
                    sel = None
                    for hw in range(2):
                        if sel is not None:
                            C.wait("vector", "dv", sel)
                        hw0 = cur * NIT * 128 + hw * NIT * 64
                        sel = emit_ghost_select(C, nc, IDT2[:, hw0:hw0 + NIT * 64],
                                                wp[:, hw * NIT * 64:(hw + 1) * NIT * 64],
                                                TB2, LUTBC, NIT * 64)
                    dsel_ms[dt] = sel
                    if dt + 1 < NCH:
                        if dt >= 1:
                            C.wait("sync", "dv", dsel_ms[dt - 1])
                        ldt_ms[dt + 1] = emit_idx_loads(dt + 1)
                    C.wait("sync", "dv", sel)
                    dimg_ms[dt] = C.dma("sync", wim["d"][dt0], wp, f"l_wimd{dt % 2}")
                    C.wait("tensor", "dv", sel)
                else:
                    if dt >= 2:
                        C.wait("sync", "pe", dpan_ms[dt - 2])
                    lw = C.dma("sync", wp, wim["d"][dt0], f"l_wdp{dt % 2}")
                    C.wait("tensor", lw)
                pd = psD[:, cur * 1024:cur * 1024 + 1024]
                ebuf = EPB[:, cur * 1024:cur * 1024 + 1024]
                if dt >= 2:
                    C.wait("tensor", "dv", ep_ms[dt - 2])
                    C.wait("sync", ep_st[dt - 2])
                emit_ep_load(C, nc, state, dt0, half * 1024, 1024, ebuf, f"l_ep{cur}")
                for ic in range(NIT):
                    for j in range(2):
                        dpan_ms[dt] = C.op("tensor", lambda E, wp=wp, ic=ic, j=j, pd=pd: E.matmul(
                            pd[:, j * 512:(j + 1) * 512], wp[:, ic * 128:(ic + 1) * 128],
                            HHF[:, ic * 1024 + j * 512:ic * 1024 + (j + 1) * 512],
                            start=(ic == 0), stop=(ic == NIT - 1)),
                            "pe" if (ic == NIT - 1 and j == 1) else None)
                C.wait("vector", "pe", dpan_ms[dt])
                ep_st[dt], ep_ms[dt] = emit_ep(C, nc, state, R, pd, 0.5, dt0,
                                               half * 1024, 1024, ebuf, True, f"l_eps{cur}")
            C.wait("tensor", "dv", ep_ms[half * NCH + NCH - 1])
            C.wait("tensor", "dv", ep_ms[half * NCH + NCH - 2])
            C.wait("sync", ep_st[half * NCH + NCH - 1])
            C.wait("sync", ep_st[half * NCH + NCH - 2])
            prev_pe = dpan_ms[half * NCH + NCH - 1]


def emit_l1_wcast(C, nc, ext, state):
    wbf = {}
    for nm in ("Wq", "Wk", "Wv", "Wo"):
        wbf[nm] = nc.dram_tensor("wbf1_" + nm, [D, D], BF16).ap()
    state["l1_wbf"] = wbf
    with nc.sbuf_tensor(C.name("wstg"), [128, 2 * D], BF16) as STG:
        st_ms = {}
        i = 0
        for nm in ("Wq", "Wk", "Wv", "Wo"):
            srcw = ext["L1_attn_" + nm]
            for r in range(NCH):
                buf = STG[:, (i % 2) * D:(i % 2) * D + D]
                if i >= 2:
                    C.wait("gpsimd", st_ms[i - 2])
                ld = C.dma("gpsimd", buf, srcw[r * 128:(r + 1) * 128, :], f"g_wst{i % 2}")
                C.wait("sync", ld)
                st_ms[i] = C.dma("sync", wbf[nm][r * 128:(r + 1) * 128, :], buf, f"l_wst{i % 2}")
                i += 1
    state["l1_wcast_ms"] = (st_ms[i - 1], st_ms[i - 2])
    C.phase()


ATTN_PHASES = int(os.environ.get("ATTN_PHASES", "4"))
FLASH_CUT = int(os.environ.get("FLASH_CUT", "7"))


def emit_attn(C, nc, ext, R, K, state, layer):
    state["r32_written"] = True
    H = 16
    NKV = 1 if layer == 0 else H
    kind = "ExternalOutput" if os.environ.get("DBG_ATTN") == "1" else "Internal"
    q_img = nc.dram_tensor(C.name("qimg"), [H, 128, T], BF16, kind=kind).ap()
    k_img = nc.dram_tensor(C.name("kimg"), [NKV, 128, T], BF16, kind=kind).ap()
    v_img = nc.dram_tensor(C.name("vimg"), [NKV, 128, 16 * 128], BF16, kind=kind).ap()
    o_img = nc.dram_tensor(C.name("oimg"), [H, 128, T], BF16, kind=kind).ap()
    if kind == "ExternalOutput":
        state["dbg_names"] = (q_img.tensor.name, k_img.tensor.name,
                              v_img.tensor.name, o_img.tensor.name)
    wbf = state.get("l1_wbf")

    with nc.sbuf_tensor(C.name("ropes"), [128, 4 * T], BF16) as ROPE:
        pre = "rope0_" if layer == 0 else "rope1_"
        g1 = None
        for i, nm in enumerate(("cos_q", "sin_q", "cos_k", "sin_k")):
            g1 = C.dma("gpsimd", ROPE[:, i * T:(i + 1) * T], ext[pre + nm], "g_rope")
        COSQ, SINQ = ROPE[:, 0:T], ROPE[:, T:2 * T]
        COSK, SINK = ROPE[:, 2 * T:3 * T], ROPE[:, 3 * T:4 * T]
        C.wait("vector", g1)

        with (nc.sbuf_tensor(C.name("xln"), [128, NCH * T], BF16) as XLN,
              nc.sbuf_tensor(C.name("wpan"), [128, 16 * 512], BF16) as WPAN,
              nc.sbuf_tensor(C.name("qkb"), [128, 2 * T], BF16) as QKB,
              nc.sbuf_tensor(C.name("rt"), [128, T], BF16) as RT,
              nc.sbuf_tensor(C.name("vtf"), [128, T], F32) as VTF,
              nc.sbuf_tensor(C.name("permt"), [128, 128], BF16) as PERM,
              nc.sbuf_tensor(C.name("identt"), [128, 128], F32) as IDENT):
            lp = C.dma("sync", PERM[:], ext["perm0" if layer == 0 else "perm1"], "l_const")
            lp = C.dma("sync", IDENT[:], ext["identf"], "l_const")
            C.wait("tensor", lp)
            with (nc.sbuf_tensor(C.name("mbc"), [128, T], BF16) as m_bc,
                  nc.sbuf_tensor(C.name("rbc"), [128, T], BF16) as r_bc):
                ln_ms = emit_ln(C, nc, R, K, m_bc, r_bc, XLN[:, 0:2 * T])
                xln_ms = emit_xln(C, nc, R, m_bc, r_bc, XLN, ln_ms)
            C.wait("tensor", "dv", xln_ms)
            C.phase()
            ps_es = contextlib.ExitStack()
            psQ = ps_es.enter_context(nc.psum_tensor(C.name("psq"), [128, 2048], F32))
            psP = ps_es.enter_context(nc.psum_tensor(C.name("psp"), [128, 2048], F32))

            st = {"psQ_free": ("dv", 0), "psP_free": ("dv", 0), "qh_free": ("dv", 0),
                  "ob_free": ("l_obst", 0), "vb_free": ("l_vbst", 0), "vtf_free": ("pe", 0)}

            def wst(eng, key):
                C.wait(eng, st[key])

            def mm_into_psQ(wslice_fn, milestone=True):
                pe_l = None
                for dc in range(NCH):
                    for j in range(4):
                        pe_l = C.op("tensor", lambda E, dc=dc, j=j: E.matmul(
                            psQ[:, j * 512:(j + 1) * 512], wslice_fn(dc),
                            XLN[:, dc * T + j * 512:dc * T + (j + 1) * 512],
                            start=(dc == 0), stop=(dc == NCH - 1)),
                            "pe" if (dc == NCH - 1 and j == 3) else None)
                return pe_l

            def project(wslice_fn, cos, sin, img, img_i):
                wst("tensor", "psQ_free")
                pe_l = mm_into_psQ(wslice_fn)
                qh = QKB[:, 0:T]
                C.wait("vector", "pe", pe_l)
                wst("vector", "qh_free")
                ev = C.op("vector", lambda E, qh=qh: E.tensor_copy(qh, psQ[:]), "dv")
                st["psQ_free"] = ("dv", ev)
                C.wait("tensor", "dv", ev)
                wst("tensor", "psP_free")
                pp = None
                for j in range(4):
                    pp = C.op("tensor", lambda E, qh=qh, j=j: E.matmul(
                        psP[:, j * 512:(j + 1) * 512], PERM[:],
                        qh[:, j * 512:(j + 1) * 512], start=True, stop=True),
                        "pe" if j == 3 else None)
                C.wait("vector", "pe", pp)
                r1 = C.op("vector", lambda E, sin=sin: E.tensor_tensor(
                    RT[:, 0:T], psP[:], sin, op=AluOpType.mult), "dv")
                st["psP_free"] = ("dv", r1)
                C.wait("vector", "dv", r1)
                wst("vector", "ob_free")
                ob = QKB[:, T:2 * T]
                r2 = C.op("vector", lambda E, qh=qh, cos=cos, ob=ob: E.tensor_tensor(
                    ob, qh, cos, op=AluOpType.mult), "dv")
                st["qh_free"] = ("dv", r2)
                C.wait("vector", "dv", r2)
                r3 = C.op("vector", lambda E, ob=ob: E.tensor_tensor(
                    ob, ob, RT[:, 0:T], op=AluOpType.add), "dv")
                C.wait("sync", "dv", r3)
                sm = C.dma("sync", img[img_i], ob, "l_obst")
                st["ob_free"] = sm
                return sm, r3, pp

            def project_v(wslice_fn, img, img_i, next_write_dv):
                wst("tensor", "psQ_free")
                pe_l = mm_into_psQ(wslice_fn)
                C.wait("vector", "pe", pe_l)
                wst("vector", "vtf_free")
                ev = C.op("vector", lambda E: E.tensor_copy(VTF[:], psQ[:]), "dv")
                st["psQ_free"] = ("dv", ev)
                C.wait("tensor", "dv", ev)
                wst("tensor", "psP_free")
                wst("vector", "qh_free")
                wst("vector", "vb_free")
                vb = QKB[:, 0:T]
                tr_pe = {}
                evms = {}
                for tt in range(NCH):
                    slot = psP[:, (tt % 4) * 512:(tt % 4) * 512 + 128]
                    if tt >= 4:
                        C.wait("tensor", "dv", evms[tt - 4])
                    tr_pe[tt] = C.op("tensor", lambda E, slot=slot, tt=tt: E.transpose(
                        slot, VTF[:, tt * 128:(tt + 1) * 128], IDENT[:]), "pe")
                    C.wait("vector", "pe", tr_pe[tt])
                    evms[tt] = C.op("vector", lambda E, slot=slot, tt=tt, vb=vb: E.tensor_copy(
                        vb[:, tt * 128:(tt + 1) * 128], slot), "dv")
                st["psP_free"] = ("dv", evms[NCH - 1])
                st["vtf_free"] = ("pe", tr_pe[NCH - 1])
                C.wait("sync", "dv", evms[NCH - 1])
                sm = C.dma("sync", img[img_i], vb, "l_vbst")
                st["vb_free"] = sm
                st["qh_free"] = sm
                return sm

            if layer == 0:
                WKV = WPAN[:, 16 * 128:16 * 128 + 16 * 256]
                lkv = None
                for c4 in range(4):
                    lkv = C.dma("gpsimd",
                                WKV[:, c4 * 1024:(c4 + 1) * 1024]
                                .rearrange("p (c n) -> p c n", c=4),
                                ext["L0_attn_Wkv"][c4 * 512:(c4 + 1) * 512, :]
                                .rearrange("(c p) n -> p c n", p=128), "g_qpan")
                C.wait("tensor", lkv)
                project(lambda dc: WKV[:, dc * 256:dc * 256 + 128],
                        COSK, SINK, k_img, 0)
                vst = project_v(lambda dc: WKV[:, dc * 256 + 128:dc * 256 + 256],
                                v_img, 0, None)
                prev_pan = 0
                for h in range(H):
                    C.wait("gpsimd", "pe", prev_pan)
                    lqp = None
                    for c4 in range(4):
                        lqp = C.dma("gpsimd",
                                    WPAN[:, c4 * 512:c4 * 512 + 512]
                                    .rearrange("p (c n) -> p c n", c=4),
                                    ext["L0_attn_Wq"][c4 * 512:(c4 + 1) * 512,
                                                      h * 128:(h + 1) * 128]
                                    .rearrange("(c p) n -> p c n", p=128), "g_qpan")
                    C.wait("tensor", lqp)
                    _, _, pp = project(
                        lambda dc: WPAN[:, dc * 128:(dc + 1) * 128],
                        COSQ, SINQ, q_img, h)
                    prev_pan = pp
            else:
                C.wait("sync", state["l1_wcast_ms"][0])
                C.wait("sync", state["l1_wcast_ms"][1])
                prev_pan = 0
                for hg in range(4):
                    for mat, img, what in (("Wq", q_img, "q"), ("Wk", k_img, "k"),
                                           ("Wv", v_img, "v")):
                        C.wait("sync", "pe", prev_pan)
                        lt = None
                        for dc in range(NCH):
                            lt = C.dmat("sync", WPAN[:, dc * 512:(dc + 1) * 512],
                                        wbf[mat][hg * 512:(hg + 1) * 512,
                                                 dc * 128:(dc + 1) * 128], "l_gpan")
                        C.wait("tensor", lt)
                        for hh in range(4):
                            h = hg * 4 + hh
                            wf = (lambda dc, hh=hh: WPAN[:, dc * 512 + hh * 128:
                                                          dc * 512 + hh * 128 + 128])
                            if what == "q":
                                _, _, pp = project(wf, COSQ, SINQ, q_img, h)
                            elif what == "k":
                                _, _, pp = project(wf, COSK, SINK, k_img, h)
                            else:
                                vst = project_v(wf, v_img, h, None)
                        prev_pan = C.val("pe")
            ps_es.close()
        C.phase()
        if ATTN_PHASES < 3:
            return

        with (nc.sbuf_tensor(C.name("qb"), [128, 2 * T], BF16) as QB,
              nc.sbuf_tensor(C.name("kb"), [128, 2 * T], BF16) as KB,
              nc.sbuf_tensor(C.name("vb"), [128, 2 * T], BF16) as VB,
              nc.sbuf_tensor(C.name("psb"), [128, 2048], BF16) as PSB,
              nc.sbuf_tensor(C.name("pts"), [128, 2048], BF16) as PTS,
              nc.sbuf_tensor(C.name("oh"), [128, T], BF16) as OH,
              nc.sbuf_tensor(C.name("caus"), [128, 2048], F32) as CAUSAL,
              nc.sbuf_tensor(C.name("ident2"), [128, 128], BF16) as IDENT2,
              nc.sbuf_tensor(C.name("nm"), [128, 8], F32) as NMRS,
              nc.psum_tensor(C.name("pss"), [128, 2048], F32) as psS,
              nc.psum_tensor(C.name("pst"), [128, 2048], BF16) as psT,
              nc.psum_tensor(C.name("pso"), [128, 512], F32) as psO):
            lc = C.dma("sync", CAUSAL[:], ext["causal"], "l_const")
            lc = C.dma("sync", IDENT2[:], ext["ident"], "l_const")
            C.wait("vector", lc)
            C.wait("tensor", lc)
            last_exp = 0
            last_head_pe = {}
            oh_st = ("l_ohst", 0)
            ld_ms = {}
            for h in range(H):
                cur = h % 2
                if h >= 2:
                    C.wait("sync", "pe", last_head_pe[h - 2])
                ld_q = C.dma("sync", QB[:, cur * T:(cur + 1) * T], q_img[h], f"l_qb{cur}")
                C.wait("tensor", ld_q)
                if layer == 1 or h == 0:
                    kv = h if layer == 1 else 0
                    kcur2 = cur if layer == 1 else 0
                    ld_k = C.dma("sync", KB[:, kcur2 * T:(kcur2 + 1) * T], k_img[kv], f"l_kb{kcur2}")
                    ld_v = C.dma("sync", VB[:, kcur2 * T:(kcur2 + 1) * T], v_img[kv], f"l_vb{kcur2}")
                    C.wait("tensor", ld_k)
                    C.wait("tensor", ld_v)
                kcur = cur if layer == 1 else 0
                qsl = QB[:, cur * T:(cur + 1) * T]
                ksl = KB[:, kcur * T:(kcur + 1) * T]
                vsl = VB[:, kcur * T:(kcur + 1) * T]
                prev_pts_pe = 0
                for qt in range(16):
                    nk = (qt * 128 + 128 + 511) // 512
                    C.wait("tensor", "ac", last_exp)
                    s_pe = None
                    for j in range(nk):
                        s_pe = C.op("tensor", lambda E, j=j, qt=qt, qsl=qsl, ksl=ksl: E.matmul(
                            psS[:, j * 512:(j + 1) * 512],
                            qsl[:, qt * 128:(qt + 1) * 128],
                            ksl[:, j * 512:(j + 1) * 512],
                            start=True, stop=True),
                            "pe" if j == nk - 1 else None)
                    if FLASH_CUT < 2:
                        last_head_pe[h] = s_pe
                        continue
                    v4 = qt - (nk - 1) * 4
                    C.wait("vector", "pe", s_pe)
                    mk = C.op("vector", lambda E, nk=nk, v4=v4: E.tensor_tensor(
                        psS[:, (nk - 1) * 512:nk * 512], psS[:, (nk - 1) * 512:nk * 512],
                        CAUSAL[:, v4 * 512:(v4 + 1) * 512], op=AluOpType.add), "dv")
                    if FLASH_CUT < 3:
                        last_head_pe[h] = s_pe
                        continue
                    C.wait("vector", "dv", mk)
                    C.wait("vector", "ac", last_exp)
                    rm = C.op("vector", lambda E, nk=nk: E.tensor_reduce(
                        NMRS[:, 0:1], psS[:, 0:nk * 512], axis=mybir.AxisListType.X,
                        op=AluOpType.max, negate=True), "dv")
                    if FLASH_CUT < 4:
                        last_head_pe[h] = s_pe
                        continue
                    C.wait("scalar", "dv", rm)
                    ex = C.op("scalar", lambda E, nk=nk: E.activation(
                        PSB[:, 0:nk * 512], psS[:, 0:nk * 512], AF.Exp,
                        bias=NMRS[:, 0:1], accum_out=NMRS[:, 1:2]), "ac")
                    last_exp = ex
                    if FLASH_CUT < 5:
                        last_head_pe[h] = s_pe
                        continue
                    C.wait("vector", "ac", ex)
                    rc = C.op("vector", lambda E: E.reciprocal(NMRS[:, 2:3], NMRS[:, 1:2]), "dv")
                    C.wait("vector", "dv", rc)
                    nrm = C.op("vector", lambda E, nk=nk: E.tensor_scalar(
                        PSB[:, 0:nk * 512], PSB[:, 0:nk * 512], NMRS[:, 2:3], None,
                        op0=AluOpType.mult), "dv")
                    if FLASH_CUT < 6:
                        last_head_pe[h] = s_pe
                        continue
                    C.wait("tensor", "dv", nrm)
                    C.wait("tensor", "pe", prev_pts_pe)
                    njc = nk * 4
                    tr_ms = {}
                    ev_ms = {}
                    for jc in range(njc):
                        slot = psT[:, (jc % 2) * 1024:(jc % 2) * 1024 + 128]
                        if jc >= 2:
                            C.wait("tensor", "dv", ev_ms[jc - 2])
                        tr_ms[jc] = C.op("tensor", lambda E, slot=slot, jc=jc: E.transpose(
                            slot, PSB[:, jc * 128:(jc + 1) * 128], IDENT2[:]), "pe")
                        C.wait("vector", "pe", tr_ms[jc])
                        ev_ms[jc] = C.op("vector", lambda E, slot=slot, jc=jc: E.tensor_copy(
                            PTS[:, jc * 128:(jc + 1) * 128], slot), "dv")
                    if FLASH_CUT < 7:
                        last_head_pe[h] = tr_ms[njc - 1]
                        prev_pts_pe = tr_ms[njc - 1]
                        continue
                    o_pe = None
                    for jc in range(njc):
                        C.wait("tensor", "dv", ev_ms[jc])
                        if jc == 0:
                            C.wait("tensor", "dv", C.val("dv"))
                        o_pe = C.op("tensor", lambda E, jc=jc, vsl=vsl: E.matmul(
                            psO[:, 0:128], vsl[:, jc * 128:(jc + 1) * 128],
                            PTS[:, jc * 128:(jc + 1) * 128],
                            start=(jc == 0), stop=(jc == njc - 1)),
                            "pe" if jc == njc - 1 else None)
                    prev_pts_pe = o_pe
                    C.wait("vector", "pe", o_pe)
                    if qt == 0:
                        C.wait("vector", oh_st)
                    oev = C.op("vector", lambda E, qt=qt: E.tensor_copy(
                        OH[:, qt * 128:(qt + 1) * 128], psO[:, 0:128]), "dv")
                if FLASH_CUT >= 7:
                    last_head_pe[h] = o_pe
                    C.wait("sync", "dv", oev)
                    oh_st = C.dma("sync", o_img[h], OH[:], "l_ohst")
            if FLASH_CUT >= 7:
                C.wait("sync", oh_st)
        C.phase()
        if ATTN_PHASES < 4:
            return

        with (nc.sbuf_tensor(C.name("osb"), [128, NCH * T], BF16) as OSB,
              nc.sbuf_tensor(C.name("wop"), [128, 16 * 512], BF16) as WOP,
              nc.sbuf_tensor(C.name("epb4"), [128, 2 * T], F32) as EPB4,
              nc.psum_tensor(C.name("psa"), [128, 2 * 2048], F32) as psA):
            lo = None
            for h in range(H):
                lo = C.dma("sync", OSB[:, h * T:(h + 1) * T], o_img[h], "l_osb")
            C.wait("tensor", lo)
            ep_ms = {}
            ep_st = {}
            mm_ms = {}
            for dt in range(NCH):
                cur = dt % 2
                if layer == 0:
                    if dt >= 2:
                        C.wait("gpsimd", "pe", mm_ms[dt - 2])
                    lwp = None
                    for c4 in range(4):
                        lwp = C.dma("gpsimd",
                                    WOP[:, (cur * 8 + c4) * 512:(cur * 8 + c4) * 512 + 512]
                                    .rearrange("p (c n) -> p c n", c=4),
                                    ext["L0_attn_Wout"][c4 * 512:(c4 + 1) * 512,
                                                        dt * 128:(dt + 1) * 128]
                                    .rearrange("(c p) n -> p c n", p=128), f"g_wop{cur}")
                    C.wait("tensor", lwp)
                    wsl = lambda ic, cur=cur: WOP[:, cur * 8 * 512 + ic * 512 - ic * 512 + ic * 128:0]                         if False else None
                    base = cur * 8 * 512
                    wsl = (lambda ic, base=base: WOP[:, base + ic * 128:base + ic * 128 + 128])
                else:
                    if dt % 4 == 0:
                        if dt >= 4:
                            C.wait("sync", "pe", mm_ms[dt - 1])
                        lwp = None
                        for ic in range(NCH):
                            lwp = C.dmat("sync", WOP[:, ic * 512:(ic + 1) * 512],
                                         wbf["Wo"][dt * 128:(dt + 4) * 128,
                                                   ic * 128:(ic + 1) * 128], "l_wop")
                        C.wait("tensor", lwp)
                    hh = dt % 4
                    wsl = (lambda ic, hh=hh: WOP[:, ic * 512 + hh * 128:ic * 512 + hh * 128 + 128])
                if dt >= 2:
                    C.wait("tensor", "dv", ep_ms[dt - 2])
                    C.wait("sync", ep_st[dt - 2])
                pa = psA[:, cur * 2048:cur * 2048 + 2048]
                ebuf = EPB4[:, cur * 2048:cur * 2048 + 2048]
                emit_ep_load(C, nc, state, dt, 0, T, ebuf, f"l_ep{cur}")
                for ic in range(NCH):
                    for j in range(4):
                        mm_ms[dt] = C.op("tensor", lambda E, ic=ic, j=j, pa=pa, wsl=wsl: E.matmul(
                            pa[:, j * 512:(j + 1) * 512], wsl(ic),
                            OSB[:, ic * T + j * 512:ic * T + (j + 1) * 512],
                            start=(ic == 0), stop=(ic == NCH - 1)),
                            "pe" if (ic == NCH - 1 and j == 3) else None)
                C.wait("vector", "pe", mm_ms[dt])
                ep_st[dt], ep_ms[dt] = emit_ep(C, nc, state, R, pa, 1.0, dt,
                                               0, T, ebuf, True, f"l_eps{cur}")
            C.wait("tensor", "dv", ep_ms[NCH - 1])
            C.wait("tensor", "dv", ep_ms[NCH - 2])
            C.wait("sync", ep_st[NCH - 1])
            C.wait("sync", ep_st[NCH - 2])


def emit_attn0(C, nc, ext, R, K, state):
    emit_attn(C, nc, ext, R, K, state, 0)


def emit_attn1(C, nc, ext, R, K, state):
    if "l1_wbf" not in state:
        emit_l1_wcast(C, nc, ext, state)
    emit_attn(C, nc, ext, R, K, state, 1)



ECAP = 648
EUSE = 640
NSLOT = 8 * ECAP
I2 = 2816
NI2 = I2 // 128
NI = 11
J2 = 11264
NJ2 = J2 // 128
NJ = 44


def emit_moe(C, nc, ext, R, K, state):
    state["r32_written"] = True
    ap_i = state.get("moe_n", 0)
    state["moe_n"] = ap_i + 1
    if "moe_dram" not in state:
        md = {}
        kd = "ExternalOutput" if os.environ.get("DBG_MOE") == "1" else "Internal"
        md["xln"] = nc.dram_tensor("md_xln", [D, T], BF16, kind=kd).ap()
        md["buf"] = nc.dram_tensor("md_buf", [NSLOT, D], BF16, kind=kd).ap()
        md["obt"] = nc.dram_tensor("md_obt", [D, NSLOT], BF16, kind=kd).ap()
        md["otm"] = nc.dram_tensor("md_otm", [NSLOT, D], BF16, kind=kd).ap()
        md["y"] = nc.dram_tensor("md_y", [T, D], BF16, kind=kd).ap()
        md["hsh"] = nc.dram_tensor("md_hsh", [5632, T], BF16, kind=kd).ap()
        md["rowb"] = nc.dram_tensor("md_rowb", [4, T], F32, kind=kd).ap()
        md["cci"] = nc.dram_tensor("md_cci0", [1, 8], F32)
        md["cco"] = nc.dram_tensor("md_cco0", [8, 8], F32)
        state["moe_dram"] = md
    md = state["moe_dram"]

    with nc.sbuf_tensor(C.name("mrow"), [1, 2 * T], F32) as MR:
        with (nc.sbuf_tensor(C.name("mbc"), [128, T], BF16) as m_bc,
              nc.sbuf_tensor(C.name("rbc"), [128, T], BF16) as r_bc,
              nc.sbuf_tensor(C.name("xln"), [128, NCH * T], BF16) as XLN):
            ln_ms = emit_ln(C, nc, R, K, m_bc, r_bc, XLN[:, 0:2 * T], out_rows=MR)
            xln_ms = emit_xln(C, nc, R, m_bc, r_bc, XLN, ln_ms)
            C.wait("sync", "dv", xln_ms)
            xs = None
            for c in range(NCH):
                xs = C.dma("sync", md["xln"][c * 128:(c + 1) * 128, :],
                           XLN[:, c * T:(c + 1) * T], "l_xst")
            C.wait("sync", xs)
        C.phase()

        with (nc.sbuf_tensor(C.name("rout"), [8, 8 * T], F32) as ROUT,
              nc.sbuf_tensor(C.name("rtr"), [128, 16 * 8], F32) as RTR,
              nc.sbuf_tensor(C.name("r32c"), [128, 2 * T], F32) as R32C,
              nc.sbuf_tensor(C.name("lt"), [128, 16 * 8], F32) as LT,
              nc.sbuf_tensor(C.name("eq"), [128, 4 * 128], F32) as EQ,
              nc.sbuf_tensor(C.name("pk"), [128, 64], F32) as PACK,
              nc.sbuf_tensor(C.name("cols"), [128, 128], F32) as COLS,
              nc.sbuf_tensor(C.name("row1"), [1, T], F32) as ROW1,
              nc.sbuf_tensor(C.name("sm8"), [128, 16], F32) as SM8,
              nc.sbuf_tensor(C.name("idf"), [128, 128], F32) as IDF,
              nc.psum_tensor(C.name("pslg"), [128, 2048], F32) as psL,
              nc.psum_tensor(C.name("pscs"), [128, 512], F32) as psC):
            Q = {"lg": 0, "prod": 0,
                 "mbc8": 1, "c1": 1, "pos1": 1,
                 "rbc8": 2, "c2": 2,
                 "m1": 3, "m2": 4,
                 "ones8": 5, "v1": 5,
                 "pos2": 6, "v2": 7}
            def q(name, c0=0, w=T):
                t = Q[name]
                return ROUT[0:8, t * T + c0:t * T + c0 + w]
            g1 = C.op("gpsimd", lambda E: E.iota(SM8[:, 0:1], pattern=[[0, 1]],
                      base=0, channel_multiplier=1, allow_small_or_imprecise_dtypes=True), None)
            g2 = C.op("gpsimd", lambda E: E.iota(LT[:, 0:8], pattern=[[1, 8]],
                      base=0, channel_multiplier=0, allow_small_or_imprecise_dtypes=True), None)
            g3 = C.op("gpsimd", lambda E: E.memset(q("ones8"), 1.0), "gp")
            lrm = C.dma("sync", SM8[0:8, 1:2], ext["rank_mask"], "l_const")
            lid = C.dma("sync", IDF[:], ext["identf"], "l_const")
            IOTA8R = LT[:, 0:8]
            C.wait("vector", "gp", g3)
            ionb = C.op("vector", lambda E: E.tensor_copy(COLS[:, 120:128], LT[:, 0:8]), "dv")
            IOTA8ROW = COLS[:, 120:128]
            IOTA8COL = SM8[:, 0:1]
            RMASK = SM8[0:8, 1:2]
            lr = None
            for c in range(NCH):
                lr = C.dma("sync", RTR[:, c * 8:(c + 1) * 8],
                           ext["L1_ffn_router"][c * 128:(c + 1) * 128, :], "l_const")
            C.wait("tensor", "l_const", C.val("l_const"))
            C.wait("tensor", "dv", ionb)
            rsrc = state["r32_src"]
            lg_pe = None
            cs_pe = None
            ch_pe = {}
            for c in range(NCH):
                sl = c % 2
                if c >= 2:
                    C.wait("sync", "pe", ch_pe[c - 2])
                lc = C.dma("sync", R32C[:, sl * T:sl * T + T],
                           rsrc[c * 128:(c + 1) * 128, :], f"l_r32{sl}")
                C.wait("tensor", lc)
                for j in range(4):
                    lg_pe = C.op("tensor", lambda E, c=c, j=j, sl=sl: E.matmul(
                        psL[0:8, j * 512:(j + 1) * 512], RTR[:, c * 8:(c + 1) * 8],
                        R32C[:, sl * T + j * 512:sl * T + (j + 1) * 512],
                        start=(c == 0), stop=(c == NCH - 1)),
                        "pe" if (c == NCH - 1 and j == 3) else None)
                cs_pe = C.op("tensor", lambda E, c=c: E.matmul(
                    psC[0:8, 0:1], RTR[:, c * 8:(c + 1) * 8], K["ONESCF"][:],
                    start=(c == 0), stop=(c == NCH - 1)),
                    "pe" if c == NCH - 1 else None)
                ch_pe[c] = cs_pe if cs_pe is not None else lg_pe
                if ch_pe[c] is None:
                    ch_pe[c] = C.op("tensor", lambda E, c=c, sl=sl: E.matmul(
                        psC[32:40, 256:257], RTR[:, c * 8:(c + 1) * 8],
                        K["ONESCF"][:], start=True, stop=True), "pe")
            C.wait("vector", "pe", lg_pe)
            C.wait("vector", "pe", cs_pe)
            e1 = C.op("vector", lambda E: E.tensor_copy(q("lg"), psL[0:8, 0:T]), "dv")
            C.wait("vector", "dv", e1)
            e2 = C.op("vector", lambda E: E.tensor_copy(SM8[0:8, 2:3], psC[0:8, 0:1]), "dv")
            C.wait("vector", "dv", e2)
            e3 = C.op("vector", lambda E: E.tensor_scalar(SM8[0:8, 3:4], SM8[0:8, 2:3],
                      -1.0, None, op0=AluOpType.mult), "dv")
            CSRN = SM8[0:8, 3:4]
            C.wait("tensor", "dv", e3)
            for j in range(4):
                mb = C.op("tensor", lambda E, j=j: E.matmul(
                    psL[0:8, j * 512:(j + 1) * 512], K["ONES1F"][0:1, 0:8],
                    MR[0:1, j * 512:(j + 1) * 512], start=True, stop=True),
                    "pe" if j == 3 else None)
            C.wait("vector", "pe", mb)
            e4 = C.op("vector", lambda E: E.tensor_copy(q("mbc8"), psL[0:8, 0:T]), "dv")
            C.wait("tensor", "dv", e4)
            for j in range(4):
                rb2 = C.op("tensor", lambda E, j=j: E.matmul(
                    psL[0:8, j * 512:(j + 1) * 512], K["ONES1F"][0:1, 0:8],
                    MR[0:1, T + j * 512:T + (j + 1) * 512], start=True, stop=True),
                    "pe" if j == 3 else None)
            C.wait("vector", "pe", rb2)
            e5 = C.op("vector", lambda E: E.tensor_copy(q("rbc8"), psL[0:8, 0:T]), "dv")
            C.wait("vector", "dv", e5)
            f1 = C.op("vector", lambda E: E.scalar_tensor_tensor(
                q("lg"), in0=q("mbc8"), scalar=CSRN, in1=q("lg"),
                op0=AluOpType.mult, op1=AluOpType.add), "dv")
            C.wait("vector", "dv", f1)
            f2 = C.op("vector", lambda E: E.tensor_tensor(
                q("lg"), q("lg"), q("rbc8"), op=AluOpType.mult), "dv")
            if os.environ.get("DBG_MOE") == "1":
                dbg_lg = nc.dram_tensor(C.name("dbg_lg"), [8, T], F32,
                                        kind="ExternalOutput").ap()
                C.wait("sync", "dv", f2)
                dlg = C.dma("sync", dbg_lg, q("lg"), "l_dbg3")
                C.wait("tensor", dlg)
            C.wait("tensor", "dv", f2)
            trms = {}
            for tt in range(16):
                slot = psL[:, (tt % 4) * 512:(tt % 4) * 512 + 8]
                if tt >= 4:
                    C.wait("tensor", "dv", trms[tt - 4])
                tp = C.op("tensor", lambda E, slot=slot, tt=tt: E.transpose(
                    slot, q("lg", tt * 128, 128), IDF[0:8, 0:8]), "pe")
                C.wait("vector", "pe", tp)
                trms[tt] = C.op("vector", lambda E, slot=slot, tt=tt: E.tensor_copy(
                    LT[:, tt * 8:(tt + 1) * 8], slot), "dv")
            last = trms[15]
            eqtr = {}
            for tt in range(16):
                lt = LT[:, tt * 8:(tt + 1) * 8]
                C.wait("vector", "dv", last)
                a = C.op("vector", lambda E, lt=lt: E.tensor_reduce(
                    SM8[:, 4:5], lt, axis=mybir.AxisListType.X, op=AluOpType.max), "dv")
                C.wait("vector", "dv", a)
                b = C.op("vector", lambda E, lt=lt: E.tensor_scalar(
                    EQ[:, 0:8], lt, SM8[:, 4:5], None, op0=AluOpType.is_equal), "dv")
                C.wait("vector", "dv", b)
                c2_ = C.op("vector", lambda E, lt=lt: E.scalar_tensor_tensor(
                    EQ[:, 8:16], in0=EQ[:, 0:8], scalar=NEG, in1=lt,
                    op0=AluOpType.mult, op1=AluOpType.add), "dv")
                C.wait("vector", "dv", c2_)
                dd = C.op("vector", lambda E: E.tensor_reduce(
                    SM8[:, 5:6], EQ[:, 8:16], axis=mybir.AxisListType.X,
                    op=AluOpType.max), "dv")
                C.wait("vector", "dv", dd)
                ee = C.op("vector", lambda E: E.tensor_scalar(
                    EQ[:, 16:24], EQ[:, 8:16], SM8[:, 5:6], None,
                    op0=AluOpType.is_equal), "dv")
                C.wait("vector", "dv", ee)
                ff = C.op("vector", lambda E: E.tensor_tensor(
                    EQ[:, 24:32], EQ[:, 0:8], IOTA8ROW, op=AluOpType.mult), "dv")
                C.wait("vector", "dv", ff)
                gg = C.op("vector", lambda E, tt=tt: E.tensor_reduce(
                    PACK[:, tt:tt + 1], EQ[:, 24:32], axis=mybir.AxisListType.X,
                    op=AluOpType.add), "dv")
                C.wait("vector", "dv", gg)
                hh = C.op("vector", lambda E: E.tensor_tensor(
                    EQ[:, 24:32], EQ[:, 16:24], IOTA8ROW, op=AluOpType.mult), "dv")
                C.wait("vector", "dv", hh)
                ii = C.op("vector", lambda E, tt=tt: E.tensor_reduce(
                    PACK[:, 16 + tt:17 + tt], EQ[:, 24:32], axis=mybir.AxisListType.X,
                    op=AluOpType.add), "dv")
                C.wait("vector", "dv", ii)
                jj = C.op("vector", lambda E: E.tensor_tensor(
                    SM8[:, 6:7], SM8[:, 5:6], SM8[:, 4:5], op=AluOpType.subtract), "dv")
                C.wait("scalar", "dv", jj)
                kk = C.op("scalar", lambda E, tt=tt: E.activation(
                    PACK[:, 48 + tt:49 + tt], SM8[:, 6:7], AF.Sigmoid), "ac")
                C.wait("vector", "ac", kk)
                ll = C.op("vector", lambda E, tt=tt: E.tensor_scalar(
                    PACK[:, 32 + tt:33 + tt], PACK[:, 48 + tt:49 + tt], -1.0, 1.0,
                    op0=AluOpType.mult, op1=AluOpType.add), "dv")
                C.wait("tensor", "dv", ll)
                t1 = C.op("tensor", lambda E, tt=tt: E.transpose(
                    psL[0:8, 0:128], EQ[:, 0:8], IDF[:]), "pe")
                C.wait("vector", "pe", t1)
                mm1 = C.op("vector", lambda E, tt=tt: E.tensor_copy(
                    q("m1", tt * 128, 128), psL[0:8, 0:128]), "dv")
                C.wait("tensor", "dv", mm1)
                t2 = C.op("tensor", lambda E, tt=tt: E.transpose(
                    psL[0:8, 512:640], EQ[:, 16:24], IDF[:]), "pe")
                C.wait("vector", "pe", t2)
                last = C.op("vector", lambda E, tt=tt: E.tensor_copy(
                    q("m2", tt * 128, 128), psL[0:8, 512:640]), "dv")
            if os.environ.get("DBG_MOE") == "1":
                dbg_lt = nc.dram_tensor(C.name("dbg_lt"), [128, 16 * 8], F32,
                                        kind="ExternalOutput").ap()
                dbg_pk = nc.dram_tensor(C.name("dbg_pk"), [128, 64], F32,
                                        kind="ExternalOutput").ap()
                dbg_m = nc.dram_tensor(C.name("dbg_m"), [8, 2 * T], F32,
                                       kind="ExternalOutput").ap()
                C.wait("sync", "dv", last)
                C.dma("sync", dbg_lt, LT[:, 0:128], "l_dbg3")
                C.dma("sync", dbg_pk, PACK[:], "l_dbg3")
                C.dma("sync", dbg_m[:, 0:T], q("m1"), "l_dbg3")
                dmm_ = C.dma("sync", dbg_m[:, T:2 * T], q("m2"), "l_dbg3")
                C.wait("vector", dmm_)
            C.wait("vector", "dv", last)
            s1 = C.op("vector", lambda E: E.tensor_tensor_scan(
                q("c1"), q("ones8"), q("m1"), 0.0,
                op0=AluOpType.mult, op1=AluOpType.add), "dv")
            C.wait("vector", "dv", s1)
            s2 = C.op("vector", lambda E: E.tensor_tensor_scan(
                q("c2"), q("ones8"), q("m2"), 0.0,
                op0=AluOpType.mult, op1=AluOpType.add), "dv")
            C.wait("vector", "dv", s2)
            cn = C.op("vector", lambda E: E.tensor_tensor(
                SM8[0:8, 7:8], q("c1", T - 1, 1), q("c2", T - 1, 1),
                op=AluOpType.add), "dv")
            C.wait("gpsimd", "dv", cn)
            ci = C.dma("gpsimd", md["cci"].ap().rearrange("a b -> b a"),
                       SM8[0:8, 7:8], "g_cc")
            C.wait("gpsimd", ci)
            ccs = C.sem(f"cc{ap_i}")
            C.ctr[ccs] = 1
            def _cc(E, S):
                E.collective_compute(
                    "AllGather", AluOpType.bypass,
                    replica_groups=[list(range(NCORES))],
                    ins=[md["cci"].ap().opt()], outs=[md["cco"].ap().opt()],
                ).then_inc(S[ccs])
            C.q["gpsimd"].append(_cc)
            C.wait("gpsimd", ccs, 1)
            co = C.dma("gpsimd", COLS[0:8, 0:8], md["cco"].ap(), "g_cc")
            C.wait("tensor", co)
            of = C.op("tensor", lambda E: E.matmul(
                psC[0:8, 0:1], COLS[0:8, 0:8], RMASK[0:8, :],
                start=True, stop=True), "pe")
            C.wait("vector", "pe", of)
            ov = C.op("vector", lambda E: E.tensor_copy(SM8[0:8, 8:9], psC[0:8, 0:1]), "dv")
            OFF = SM8[0:8, 8:9]
            C.wait("vector", "dv", ov)
            p1 = C.op("vector", lambda E: E.tensor_tensor(
                q("pos1"), q("c1"), q("m1"), op=AluOpType.subtract), "dv")
            C.wait("vector", "dv", p1)
            p2 = C.op("vector", lambda E: E.tensor_tensor(
                q("c2"), q("c2"), q("m2"), op=AluOpType.subtract), "dv")
            C.wait("vector", "dv", p2)
            p3 = C.op("vector", lambda E: E.tensor_tensor(
                q("pos1"), q("pos1"), q("c2"), op=AluOpType.add), "dv")
            C.wait("vector", "dv", p3)
            p4 = C.op("vector", lambda E: E.tensor_tensor(
                q("pos2"), q("pos1"), q("m1"), op=AluOpType.add), "dv")
            C.wait("vector", "dv", p4)
            v1 = C.op("vector", lambda E: E.tensor_scalar(
                q("v1"), q("pos1"), OFF, 5120.0,
                op0=AluOpType.add, op1=AluOpType.is_lt), "dv")
            C.wait("vector", "dv", v1)
            v2 = C.op("vector", lambda E: E.tensor_scalar(
                q("v2"), q("pos2"), OFF, 5120.0,
                op0=AluOpType.add, op1=AluOpType.is_lt), "dv")
            C.wait("vector", "dv", v2)
            cl1 = C.op("vector", lambda E: E.tensor_scalar(
                q("pos1"), q("pos1"), float(ECAP - 1), None, op0=AluOpType.min), "dv")
            C.wait("vector", "dv", cl1)
            cl2 = C.op("vector", lambda E: E.tensor_scalar(
                q("pos2"), q("pos2"), float(ECAP - 1), None, op0=AluOpType.min), "dv")
            C.wait("vector", "dv", cl2)
            rowq = [("pos1", "m1", 0), ("pos2", "m2", 1), ("v1", "m1", 2), ("v2", "m2", 3)]
            bms = {}
            for nm, msk, ri in rowq:
                pr = C.op("vector", lambda E, nm=nm, msk=msk: E.tensor_tensor(
                    q("prod"), q(nm), q(msk), op=AluOpType.mult), "dv")
                C.wait("tensor", "dv", pr)
                for j in range(4):
                    rm2 = C.op("tensor", lambda E, j=j: E.matmul(
                        psL[0:1, j * 512:(j + 1) * 512],
                        K["ONESCF"][0:8, :], q("prod", j * 512, 512),
                        start=True, stop=True), "pe" if j == 3 else None)
                C.wait("vector", "pe", rm2)
                ev = C.op("vector", lambda E: E.tensor_copy(ROW1[0:1, :], psL[0:1, 0:T]), "dv")
                C.wait("sync", "dv", ev)
                bms[ri] = C.dma("sync", md["rowb"][ri:ri + 1, :], ROW1[0:1, :], "l_rbc")
                C.wait("vector", bms[ri])
            for ri in range(4):
                lb = C.dma("sync", LT[0:16, 0:128],
                           md["rowb"][ri:ri + 1, :].rearrange("o (a b) -> (o a) b", a=16),
                           "l_rbc")
                C.wait("tensor", lb)
                tp = C.op("tensor", lambda E: E.transpose(
                    psL[:, 0:16], LT[0:16, 0:128], IDF[0:16, 0:16]), "pe")
                C.wait("vector", "pe", tp)
                cv = C.op("vector", lambda E, ri=ri: E.tensor_copy(
                    COLS[:, 16 + ri * 16:32 + ri * 16], psL[:, 0:16]), "dv")
                C.wait("sync", "dv", cv)
            POS1C, POS2C = COLS[:, 16:32], COLS[:, 32:48]
            V1C, V2C = COLS[:, 48:64], COLS[:, 64:80]
            dv1 = C.op("vector", lambda E: E.tensor_scalar(
                COLS[:, 80:96], PACK[:, 0:16], float(ECAP), None,
                op0=AluOpType.mult), "dv")
            C.wait("vector", "dv", dv1)
            dv2 = C.op("vector", lambda E: E.tensor_tensor(
                COLS[:, 80:96], COLS[:, 80:96], POS1C, op=AluOpType.add), "dv")
            C.wait("vector", "dv", dv2)
            dv3 = C.op("vector", lambda E: E.tensor_scalar(
                COLS[:, 96:112], PACK[:, 16:32], float(ECAP), None,
                op0=AluOpType.mult), "dv")
            C.wait("vector", "dv", dv3)
            dv4 = C.op("vector", lambda E: E.tensor_tensor(
                COLS[:, 96:112], COLS[:, 96:112], POS2C, op=AluOpType.add), "dv")
            C.wait("vector", "dv", dv4)
            g1_ = C.op("vector", lambda E: E.tensor_tensor(
                PACK[:, 32:48], PACK[:, 32:48], V1C, op=AluOpType.mult), "dv")
            C.wait("vector", "dv", g1_)
            g2_ = C.op("vector", lambda E: E.tensor_tensor(
                PACK[:, 48:64], PACK[:, 48:64], V2C, op=AluOpType.mult), "dv")
            C.wait("gpsimd", "dv", dv4)
            du = C.dma("gpsimd", state["GDU"][:, 0:32], COLS[:, 80:112], "g_du")
            C.wait("sync", du)
            C.wait("vector", "dv", g2_)
            gd = C.op("vector", lambda E: E.tensor_copy(state["GD"][:, 0:32],
                      PACK[:, 32:64]), "dv")
            C.wait("sync", "dv", gd)
        C.phase()

        for _sn in ("g_sca0", "g_sca1", "g_scb0", "g_scb1", "g_ga0", "g_ga1"):
            C.sem(_sn)
        with nc.sbuf_tensor(C.name("xt"), [128, 2 * T], BF16) as XT:
            if os.environ.get("MOE_ZBUF") == "1":
                zf = C.op("gpsimd", lambda E: E.memset(XT[:, 0:T], 0.0), "gp")
                C.wait("sync", "gp", zf)
                zz = None
                for rr_ in range(41):
                    n_ = min(128, NSLOT - rr_ * 128)
                    zz = C.dma("sync", md["buf"][rr_ * 128:rr_ * 128 + n_, :],
                               XT[0:n_, 0:T], "l_zb")
                C.wait("sync", zz)
                C.wait("gpsimd", zz)
            sc_ms = {("a", 0): ("g_sca0", 0), ("a", 1): ("g_sca1", 0),
                     ("b", 0): ("g_scb0", 0), ("b", 1): ("g_scb1", 0)}
            for tt in range(16):
                sl = tt % 2
                if tt >= 2:
                    C.wait("sync", sc_ms[("a", sl)])
                    C.wait("sync", sc_ms[("b", sl)])
                xl = C.dmat("sync", XT[:, sl * T:sl * T + T],
                            md["xln"][:, tt * 128:(tt + 1) * 128], f"l_xt{sl}")
                C.wait("gpsimd", xl)
                def sc(E, S, sl=sl, tt=tt, which=0):
                    pass
                ia = C.inc(f"g_sca{sl}", 16)
                C.q["gpsimd"].append(lambda E, S, sl=sl, tt=tt, ia=ia: E.indirect_dma_start(
                    out=md["buf"], out_offset=bass.IndirectOffsetOnAxis(
                        ap=state["GDU"][:, tt:tt + 1], axis=0),
                    in_=XT[:, sl * T:sl * T + T], in_offset=None,
                ).then_inc(S[f"g_sca{sl}"], 16))
                sc_ms[("a", sl)] = (f"g_sca{sl}", ia)
                ib = C.inc(f"g_scb{sl}", 16)
                C.q["gpsimd"].append(lambda E, S, sl=sl, tt=tt, ib=ib: E.indirect_dma_start(
                    out=md["buf"], out_offset=bass.IndirectOffsetOnAxis(
                        ap=state["GDU"][:, 16 + tt:17 + tt], axis=0),
                    in_=XT[:, sl * T:sl * T + T], in_offset=None,
                ).then_inc(S[f"g_scb{sl}"], 16))
                sc_ms[("b", sl)] = (f"g_scb{sl}", ib)
            for k, v in sc_ms.items():
                C.wait("sync", v)
                C.wait("gpsimd", v)
            fl = C.dma("gpsimd", XT[:, 0:64], md["buf"][0:128, 0:64], "g_du")
            C.wait("sync", fl)
            C.wait("gpsimd", fl)
            C.wait("tensor", fl)
        C.phase()

        with (nc.sbuf_tensor(C.name("bt"), [128, 16 * EUSE], BF16) as BT,
              nc.sbuf_tensor(C.name("he"), [128, NI * EUSE], BF16) as HE,
              nc.sbuf_tensor(C.name("wep"), [128, 4 * 16 * 128], BF16) as WEP,
              nc.sbuf_tensor(C.name("sge"), [128, 2 * EUSE], F32) as SGE,
              nc.sbuf_tensor(C.name("obt"), [128, 2 * EUSE], BF16) as OBT,
              nc.psum_tensor(C.name("psga"), [128, 1024], F32) as psA,
              nc.psum_tensor(C.name("psgb"), [128, 1024], F32) as psB,
              nc.psum_tensor(C.name("psdd"), [128, 2048], F32) as psD2):
            wug = ext["L1_ffn_w_up_gate"]
            wdn = ext["L1_ffn_w_down"]
            state["m4_pan"] = {}
            prev_bt_pe = None
            for e in range(8):
                if prev_bt_pe is not None:
                    C.wait("sync", "pe", prev_bt_pe)
                bl = None
                for dc in range(NCH):
                    bl = C.dmat("sync", BT[:, dc * EUSE:(dc + 1) * EUSE],
                                md["buf"][e * ECAP:e * ECAP + EUSE,
                                          dc * 128:(dc + 1) * 128], "l_bt")
                C.wait("tensor", bl)
                silu_ms = None
                h_ms = None
                for i in range(NI):
                    for which, i2t in (("g", i), ("u", NI + i)):
                        slw = (0 if which == "g" else 1) * 2 + i % 2
                        wp = WEP[:, slw * 16 * 128:(slw + 1) * 16 * 128]
                        gk = (which, i % 2)
                        if gk in state["m4_pan"]:
                            C.wait("gpsimd", "pe", state["m4_pan"][gk])
                        lw = None
                        for c4 in range(4):
                            lw = C.dma("gpsimd",
                                       wp[:, c4 * 512:(c4 + 1) * 512]
                                       .rearrange("p (c n) -> p c n", c=4),
                                       wug[e, c4 * 512:(c4 + 1) * 512,
                                           i2t * 128:(i2t + 1) * 128]
                                       .rearrange("(c p) n -> p c n", p=128),
                                       f"g_we{slw}", )
                        C.wait("tensor", lw)
                        ps = psA if which == "g" else psB
                        if which == "g" and silu_ms is not None:
                            C.wait("tensor", "ac", silu_ms)
                        if which == "u" and h_ms is not None:
                            C.wait("tensor", "dv", h_ms)
                        for dc in range(NCH):
                            pe_l = C.op("tensor", lambda E, wp=wp, dc=dc, ps=ps: E.matmul(
                                ps[:, 0:512], wp[:, dc * 128:(dc + 1) * 128],
                                BT[:, dc * EUSE:dc * EUSE + 512],
                                start=(dc == 0), stop=(dc == NCH - 1)), None)
                            pe_l = C.op("tensor", lambda E, wp=wp, dc=dc, ps=ps: E.matmul(
                                ps[:, 512:512 + (EUSE - 512)],
                                wp[:, dc * 128:(dc + 1) * 128],
                                BT[:, dc * EUSE + 512:dc * EUSE + EUSE],
                                start=(dc == 0), stop=(dc == NCH - 1)),
                                "pe" if dc == NCH - 1 else None)
                        if which == "g":
                            gate_pe = pe_l
                        else:
                            up_pe = pe_l
                        state["m4_pan"][(which, i % 2)] = pe_l
                    C.wait("scalar", "pe", gate_pe)
                    if h_ms is not None:
                        C.wait("scalar", "dv", h_ms)
                    sg = SGE[:, (i % 2) * EUSE:(i % 2) * EUSE + EUSE]
                    silu_ms = C.op("scalar", lambda E, sg=sg: E.activation(
                        sg, psA[:, 0:EUSE], AF.Silu), "ac")
                    C.wait("vector", "ac", silu_ms)
                    C.wait("vector", "pe", up_pe)
                    h_ms = C.op("vector", lambda E, sg=sg, i=i: E.tensor_tensor(
                        HE[:, i * EUSE:(i + 1) * EUSE], sg, psB[:, 0:EUSE],
                        op=AluOpType.mult), "dv")
                C.wait("tensor", "dv", h_ms)
                ob_st = {}
                dmm = {}
                gu_done = C.val("pe")
                for dt in range(NCH):
                    cur = dt % 2
                    wp = WEP[:, cur * 16 * 128:cur * 16 * 128 + NI * 128]
                    if dt >= 2:
                        C.wait("gpsimd", "pe", dmm[dt - 2])
                    else:
                        C.wait("gpsimd", "pe", gu_done)
                    state["m4_pan"][("g", cur)] = gu_done
                    lw = None
                    for c4 in range(3):
                        nch4 = 4 if c4 < 2 else 3
                        lw = C.dma("gpsimd",
                                   wp[:, c4 * 512:c4 * 512 + nch4 * 128]
                                   .rearrange("p (c n) -> p c n", c=nch4),
                                   wdn[e, c4 * 512:c4 * 512 + nch4 * 128,
                                       dt * 128:(dt + 1) * 128]
                                   .rearrange("(c p) n -> p c n", p=128),
                                   f"g_wd{cur}")
                    C.wait("tensor", lw)
                    pd = psD2[:, cur * 1024:cur * 1024 + 1024]
                    if dt >= 2:
                        C.wait("tensor", "dv", dmm_ev[dt - 2])
                    for ic in range(NI):
                        dmm[dt] = C.op("tensor", lambda E, wp=wp, ic=ic, pd=pd: E.matmul(
                            pd[:, 0:512], wp[:, ic * 128:(ic + 1) * 128],
                            HE[:, ic * EUSE:ic * EUSE + 512],
                            start=(ic == 0), stop=(ic == NI - 1)), None)
                        dmm[dt] = C.op("tensor", lambda E, wp=wp, ic=ic, pd=pd: E.matmul(
                            pd[:, 512:EUSE], wp[:, ic * 128:(ic + 1) * 128],
                            HE[:, ic * EUSE + 512:(ic + 1) * EUSE],
                            start=(ic == 0), stop=(ic == NI - 1)),
                            "pe" if ic == NI - 1 else None)
                    C.wait("vector", "pe", dmm[dt])
                    if dt >= 2:
                        C.wait("vector", ob_st[dt - 2])
                    if dt == 0:
                        dmm_ev = {}
                    ob = OBT[:, cur * EUSE:(cur + 1) * EUSE]
                    dmm_ev[dt] = C.op("vector", lambda E, pd=pd, ob=ob: E.tensor_copy(
                        ob, pd[:, 0:EUSE]), "dv")
                    C.wait("sync", "dv", dmm_ev[dt])
                    ob_st[dt] = C.dma("sync", md["obt"][dt * 128:(dt + 1) * 128,
                                      e * ECAP:e * ECAP + EUSE], ob, f"l_ob{cur}")
                C.wait("sync", ob_st[NCH - 1])
                C.wait("sync", ob_st[NCH - 2])
                prev_bt_pe = dmm[NCH - 1]
                state["m4_pan"][("g", 0)] = dmm[NCH - 2]
                state["m4_pan"][("g", 1)] = dmm[NCH - 1]
        C.phase()

        with (nc.sbuf_tensor(C.name("ot"), [128, 2 * T], BF16) as OTB,
              nc.sbuf_tensor(C.name("zt"), [128, T], BF16) as ZT):
            zm = C.op("gpsimd", lambda E: E.memset(ZT[:], 0.0), "gp")
            C.wait("sync", "gp", zm)
            zs = None
            for e in range(8):
                zs = C.dma("sync", md["otm"][e * ECAP + EUSE:(e + 1) * ECAP, :],
                           ZT[0:ECAP - EUSE, :], "l_zs")
            ot_st = {}
            i = 0
            for e in range(8):
                for kk in range(EUSE // 128):
                    sl = i % 2
                    if i >= 2:
                        C.wait("sync", ot_st[i - 2])
                    tl = C.dmat("sync", OTB[:, sl * T:sl * T + T],
                                md["obt"][:, e * ECAP + kk * 128:e * ECAP + (kk + 1) * 128],
                                f"l_otl{sl}")
                    C.wait("sync", tl)
                    ot_st[i] = C.dma("sync",
                                     md["otm"][e * ECAP + kk * 128:e * ECAP + (kk + 1) * 128, :],
                                     OTB[:, sl * T:sl * T + T], f"l_ots{sl}")
                    i += 1
            C.wait("sync", ot_st[i - 1])
            C.wait("sync", ot_st[i - 2])
            C.wait("sync", zs)
        C.phase()

        with (nc.sbuf_tensor(C.name("yg"), [128, 2 * T], BF16) as YG,
              nc.sbuf_tensor(C.name("tmy"), [128, T], F32) as TMY,
              nc.sbuf_tensor(C.name("yb"), [128, 2 * T], BF16) as YB):
            yst = {}
            cm_ms = {}
            for tt in range(16):
                sl = tt % 2
                ga = C.inc(f"g_ga{sl}", 16)
                if tt >= 2:
                    C.wait("gpsimd", "dv", cm_ms[tt - 2])
                C.q["gpsimd"].append(lambda E, S, sl=sl, tt=tt: E.indirect_dma_start(
                    out=YG[:, sl * T:sl * T + T], out_offset=None,
                    in_=md["otm"], in_offset=bass.IndirectOffsetOnAxis(
                        ap=state["GDU"][:, tt:tt + 1], axis=0),
                ).then_inc(S[f"g_ga{sl}"], 16))
                C.wait("vector", (f"g_ga{sl}", ga))
                t1 = C.op("vector", lambda E, sl=sl, tt=tt: E.tensor_scalar(
                    TMY[:], YG[:, sl * T:sl * T + T], state["GD"][:, tt:tt + 1], None,
                    op0=AluOpType.mult), "dv")
                C.wait("gpsimd", "dv", t1)
                gb = C.inc(f"g_ga{sl}", 16)
                C.q["gpsimd"].append(lambda E, S, sl=sl, tt=tt: E.indirect_dma_start(
                    out=YG[:, sl * T:sl * T + T], out_offset=None,
                    in_=md["otm"], in_offset=bass.IndirectOffsetOnAxis(
                        ap=state["GDU"][:, 16 + tt:17 + tt], axis=0),
                ).then_inc(S[f"g_ga{sl}"], 16))
                C.wait("vector", (f"g_ga{sl}", gb))
                if tt >= 2:
                    C.wait("vector", yst[tt - 2])
                cm_ms[tt] = C.op("vector", lambda E, sl=sl, tt=tt: E.scalar_tensor_tensor(
                    YB[:, sl * T:sl * T + T], in0=YG[:, sl * T:sl * T + T],
                    scalar=state["GD"][:, 16 + tt:17 + tt], in1=TMY[:],
                    op0=AluOpType.mult, op1=AluOpType.add), "dv")
                C.wait("sync", "dv", cm_ms[tt])
                yst[tt] = C.dma("sync", md["y"][tt * 128:(tt + 1) * 128, :],
                                YB[:, sl * T:sl * T + T], f"l_y{sl}")
            C.wait("sync", yst[15])
            C.wait("sync", yst[14])
        C.phase()

        with (nc.sbuf_tensor(C.name("xln2"), [128, NCH * T], BF16) as XL2,
              nc.sbuf_tensor(C.name("wsp"), [128, 4 * 16 * 128], BF16) as WSP,
              nc.sbuf_tensor(C.name("sgs"), [128, 2 * T], F32) as SGS,
              nc.sbuf_tensor(C.name("hbs"), [128, 2 * T], BF16) as HBS,
              nc.psum_tensor(C.name("psg2"), [128, 2048], F32) as psG2,
              nc.psum_tensor(C.name("psu2"), [128, 2048], F32) as psU2):
            xl = None
            for c in range(NCH):
                xl = C.dma("sync", XL2[:, c * T:(c + 1) * T],
                           md["xln"][c * 128:(c + 1) * 128, :], "l_xst")
            C.wait("tensor", xl)
            swu = ext["L1_ffn_shared_w_up_gate"]
            silu_ms = {}
            h_ms = {}
            hst_ms = {}
            gate_pe = {}
            up_pe = {}
            for t in range(NJ):
                cur = t % 2
                for which, jt in (("g", t), ("u", NJ + t)):
                    slw = (0 if which == "g" else 1) * 2 + cur
                    wp = WSP[:, slw * 16 * 128:(slw + 1) * 16 * 128]
                    if t >= 2:
                        C.wait("gpsimd", "pe", (gate_pe if which == "g" else up_pe)[t - 2])
                    lw = None
                    for c4 in range(4):
                        lw = C.dma("gpsimd",
                                   wp[:, c4 * 512:(c4 + 1) * 512]
                                   .rearrange("p (c n) -> p c n", c=4),
                                   swu[c4 * 512:(c4 + 1) * 512, jt * 128:(jt + 1) * 128]
                                   .rearrange("(c p) n -> p c n", p=128),
                                   f"g_ws{slw}")
                    C.wait("tensor", lw)
                    ps = psG2 if which == "g" else psU2
                    if which == "g" and t >= 1:
                        C.wait("tensor", "ac", silu_ms[t - 1])
                    if which == "u" and t >= 1:
                        C.wait("tensor", "dv", h_ms[t - 1])
                    pe_l = None
                    for dc in range(NCH):
                        for j in range(4):
                            pe_l = C.op("tensor", lambda E, wp=wp, dc=dc, j=j, ps=ps: E.matmul(
                                ps[:, j * 512:(j + 1) * 512], wp[:, dc * 128:(dc + 1) * 128],
                                XL2[:, dc * T + j * 512:dc * T + (j + 1) * 512],
                                start=(dc == 0), stop=(dc == NCH - 1)),
                                "pe" if (dc == NCH - 1 and j == 3) else None)
                    if which == "g":
                        gate_pe[t] = pe_l
                    else:
                        up_pe[t] = pe_l
                C.wait("scalar", "pe", gate_pe[t])
                if t >= 1:
                    C.wait("scalar", "dv", h_ms[t - 1])
                sgb = SGS[:, cur * T:(cur + 1) * T]
                silu_ms[t] = C.op("scalar", lambda E, sgb=sgb: E.activation(
                    sgb, psG2[:], AF.Silu), "ac")
                C.wait("vector", "pe", up_pe[t])
                C.wait("vector", "ac", silu_ms[t])
                if t >= 2:
                    C.wait("vector", hst_ms[t - 2])
                hb = HBS[:, cur * T:(cur + 1) * T]
                h_ms[t] = C.op("vector", lambda E, hb=hb, sgb=sgb: E.tensor_tensor(
                    hb, sgb, psU2[:], op=AluOpType.mult), "dv")
                C.wait("sync", "dv", h_ms[t])
                hst_ms[t] = C.dma("sync", md["hsh"][t * 128:(t + 1) * 128, :], hb,
                                  f"l_hs{cur}")
            C.wait("sync", hst_ms[NJ - 1])
            C.wait("sync", hst_ms[NJ - 2])
        C.phase()
        with (nc.psum_tensor(C.name("psd3"), [128, 2048], F32) as psD3,
              nc.sbuf_tensor(C.name("epb3"), [128, 2 * 1024], F32) as EPB3,
              nc.sbuf_tensor(C.name("hhs"), [128, NJ * 1024], BF16) as HHS,
              nc.sbuf_tensor(C.name("wds"), [128, 2 * NJ * 128], BF16) as WDS):
            swd = ext["L1_ffn_shared_w_down"]
            prev_pe = None
            ep_ms = {}
            ep_st = {}
            dmm2 = {}
            for half in range(2):
                if prev_pe is not None:
                    C.wait("sync", "pe", prev_pe)
                lh = None
                for t in range(NJ):
                    lh = C.dma("sync", HHS[:, t * 1024:(t + 1) * 1024],
                               md["hsh"][t * 128:(t + 1) * 128,
                                         half * 1024:half * 1024 + 1024], "l_hh2")
                C.wait("tensor", lh)
                for dt0 in range(NCH):
                    dt = half * NCH + dt0
                    cur = dt % 2
                    wp = WDS[:, cur * NJ * 128:(cur + 1) * NJ * 128]
                    if dt >= 2:
                        C.wait("gpsimd", "pe", dmm2[dt - 2])
                    lw = None
                    for c4 in range(11):
                        lw = C.dma("gpsimd",
                                   wp[:, c4 * 512:(c4 + 1) * 512]
                                   .rearrange("p (c n) -> p c n", c=4),
                                   swd[c4 * 512:(c4 + 1) * 512, dt0 * 128:(dt0 + 1) * 128]
                                   .rearrange("(c p) n -> p c n", p=128),
                                   f"g_wd2{cur}")
                    C.wait("tensor", lw)
                    pd = psD3[:, cur * 1024:cur * 1024 + 1024]
                    ebuf = EPB3[:, cur * 1024:cur * 1024 + 1024]
                    if dt >= 2:
                        C.wait("tensor", "dv", ep_ms[dt - 2])
                        C.wait("sync", ep_st[dt - 2])
                    emit_ep_load(C, nc, state, dt0, half * 1024, 1024, ebuf, f"l_ep{cur}")
                    for jc in range(NJ):
                        for j in range(2):
                            dmm2[dt] = C.op("tensor", lambda E, wp=wp, jc=jc, j=j, pd=pd: E.matmul(
                                pd[:, j * 512:(j + 1) * 512], wp[:, jc * 128:(jc + 1) * 128],
                                HHS[:, jc * 1024 + j * 512:jc * 1024 + (j + 1) * 512],
                                start=(jc == 0), stop=(jc == NJ - 1)),
                                "pe" if (jc == NJ - 1 and j == 1) else None)
                    C.wait("vector", "pe", dmm2[dt])
                    ep_st[dt], ep_ms[dt] = emit_ep(C, nc, state, R, pd, 0.5, dt0,
                                                   half * 1024, 1024, ebuf, False,
                                                   f"l_eps{cur}")
                C.wait("tensor", "dv", ep_ms[half * NCH + NCH - 1])
                C.wait("tensor", "dv", ep_ms[half * NCH + NCH - 2])
                C.wait("sync", ep_st[half * NCH + NCH - 1])
                C.wait("sync", ep_st[half * NCH + NCH - 2])
                prev_pe = dmm2[half * NCH + NCH - 1]
        C.phase()
        state["r32_src"] = state["r32_dst"]
        state["r32_dst"] = (state["r32b"] if state["r32_dst"] is state["r32a"]
                            else state["r32a"])
        with (nc.sbuf_tensor(C.name("yt"), [128, 2 * T], BF16) as YT,
              nc.sbuf_tensor(C.name("epb5"), [128, 2 * T], F32) as EPB5):
            ep_ms = {}
            ep_st = {}
            for dc in range(NCH):
                cur = dc % 2
                if dc >= 2:
                    C.wait("sync", "dv", ep_ms[dc - 2])
                    C.wait("sync", ep_st[dc - 2])
                yl = C.dmat("sync", YT[:, cur * T:(cur + 1) * T],
                            md["y"][:, dc * 128:(dc + 1) * 128], f"l_yt{cur}")
                ebuf = EPB5[:, cur * T:(cur + 1) * T]
                emit_ep_load(C, nc, state, dc, 0, T, ebuf, f"l_ep{cur}")
                C.wait("vector", yl)
                ep_st[dc], ep_ms[dc] = emit_ep(C, nc, state, R,
                                               YT[:, cur * T:(cur + 1) * T], 0.5, dc,
                                               0, T, ebuf, True, f"l_eps{cur}")
            C.wait("tensor", "dv", ep_ms[NCH - 1])
            C.wait("tensor", "dv", ep_ms[NCH - 2])
            C.wait("sync", ep_st[NCH - 1])
            C.wait("sync", ep_st[NCH - 2])



def build_program(sublayers):
    nc = bass.Bass("TRN2", target_bir_lowering=False)
    ext = {}

    def ein(name, shape, dt):
        ext[name] = nc.dram_tensor(name, shape, dt, kind="ExternalInput").ap()

    ein("x_in", [D, T], F32)
    need = set(sublayers)
    if {"ffn0_a", "ffn0_b"} & need:
        ein("L0_ffn_lut", [1, 16], F32)
        ein("L0_ffn_gate_idx", [IFF, D], I32)
        ein("L0_ffn_up_idx", [IFF, D], I32)
        ein("L0_ffn_down_idx", [D, IFF], I32)
    if "attn0" in need:
        ein("L0_attn_Wq", [D, D], F32)
        ein("L0_attn_Wkv", [D, 256], F32)
        ein("L0_attn_Wout", [D, D], F32)
        for nm in ("rope0_cos_q", "rope0_sin_q", "rope0_cos_k", "rope0_sin_k"):
            ein(nm, [128, T], F32)
    if "attn1" in need:
        for nm in ("L1_attn_Wq", "L1_attn_Wk", "L1_attn_Wv", "L1_attn_Wo"):
            ein(nm, [D, D], F32)
        for nm in ("rope1_cos_q", "rope1_sin_q", "rope1_cos_k", "rope1_sin_k"):
            ein(nm, [128, T], F32)
    if {"moe_a", "moe_b"} & need:
        if "identf" not in ext:
            ein("identf", [128, 128], F32)
        ein("L1_ffn_router", [D, 8], F32)
        ein("L1_ffn_w_up_gate", [8, D, 2816], F32)
        ein("L1_ffn_w_down", [8, 1408, D], F32)
        ein("L1_ffn_shared_w_up_gate", [D, 11264], F32)
        ein("L1_ffn_shared_w_down", [5632, D], F32)
        ein("rank_mask", [8, 1], F32)
    if {"attn0", "attn1"} & need:
        ein("ident", [128, 128], BF16)
        ein("perm0", [128, 128], BF16)
        ein("perm1", [128, 128], BF16)
        ein("causal", [128, 2048], F32)
        if "identf" not in ext:
            ein("identf", [128, 128], F32)

    x_out = nc.dram_tensor("x_out", [D, T], F32, kind="ExternalOutput").ap()

    C = Ctx(nc)
    for s in ("pe", "dv", "ac", "gp"):
        C.sem(s)

    es = contextlib.ExitStack()
    with es:
        R = es.enter_context(nc.sbuf_tensor("R", [128, NCH * T], BF16))
        K = {}
        K["ONES1"] = es.enter_context(nc.sbuf_tensor("ones1", [1, 128], BF16))
        K["ONESC"] = es.enter_context(nc.sbuf_tensor("onesc", [128, 1], BF16))
        K["ONES1F"] = es.enter_context(nc.sbuf_tensor("ones1f", [1, 128], F32))
        K["ONESCF"] = es.enter_context(nc.sbuf_tensor("onescf", [128, 1], F32))
        K["EPS"] = es.enter_context(nc.sbuf_tensor("epsc", [128, 1], F32))
        state = {}
        if {"ffn0_a", "ffn0_b"} & need:
            state["LUTBC"] = es.enter_context(nc.sbuf_tensor("lutbc", [128, 16], F32))
        if {"moe_a", "moe_b"} & need:
            state["GD"] = es.enter_context(nc.sbuf_tensor("gdcols", [128, 32], F32))
            state["GDU"] = es.enter_context(nc.sbuf_tensor("gducols", [128, 32], U32))

        C.op("gpsimd", lambda E: E.memset(K["ONES1"][:], 1.0))
        C.op("gpsimd", lambda E: E.memset(K["ONESC"][:], 1.0))
        C.op("gpsimd", lambda E: E.memset(K["ONES1F"][:], 1.0))
        C.op("gpsimd", lambda E: E.memset(K["EPS"][:], 1e-5))
        g_ones = C.op("gpsimd", lambda E: E.memset(K["ONESCF"][:], 1.0), "gp")

        kr = "ExternalOutput" if os.environ.get("DBG_MOE") == "1" else "Internal"
        state["r32a"] = nc.dram_tensor("r32a", [D, T], F32, kind=kr).ap()
        state["r32b"] = nc.dram_tensor("r32b", [D, T], F32, kind=kr).ap()
        state["r32_src"] = ext["x_in"]
        state["r32_dst"] = state["r32a"]
        state["ep_pend"] = {}
        ld_x = None
        for c in range(NCH):
            ld_x = C.dma("gpsimd", R[:, c * T:(c + 1) * T],
                         ext["x_in"][c * 128:(c + 1) * 128, :], "g_xin")
        for e in ("tensor", "vector", "scalar"):
            C.wait(e, ld_x)
            C.wait(e, "gp", g_ones)
        C.wait("gpsimd", "gp", g_ones)
        C.barrier()

        for name in sublayers:
            if name in ("ffn0_a", "ffn0_b"):
                emit_ffn0(C, nc, ext, R, K, state)
            elif name == "attn0":
                emit_attn0(C, nc, ext, R, K, state)
            elif name == "attn1":
                emit_attn1(C, nc, ext, R, K, state)
            elif name in ("moe_a", "moe_b"):
                emit_moe(C, nc, ext, R, K, state)
            else:
                raise ValueError(name)
            if state.pop("r32_written", False):
                old_src = state["r32_src"]
                state["r32_src"] = state["r32_dst"]
                state["r32_dst"] = (state["r32b"] if state["r32_dst"] is state["r32a"]
                                    else state["r32a"])
            C.barrier()

        with nc.sbuf_tensor("xo", [128, 2 * T], F32) as XO:
            st_ms = {}
            rsrc = state["r32_src"]
            for c in range(NCH):
                buf = XO[:, (c % 2) * T:(c % 2) * T + T]
                if c >= 2:
                    C.wait("sync", st_ms[c - 2])
                lb = C.dma("sync", buf, rsrc[c * 128:(c + 1) * 128, :], f"xo{c % 2}")
                C.wait("sync", lb)
                st_ms[c] = C.dma("sync", x_out[c * 128:(c + 1) * 128, :], buf, f"st{c % 2}")
            C.wait("sync", st_ms[NCH - 1])
            C.wait("sync", st_ms[NCH - 2])

        S = {}
        for sn in C.sem_names:
            S[sn] = es.enter_context(nc.semaphore(sn))
        block = es.enter_context(nc.Block())
        for eng in ENGINES:
            ops = C.q[eng]
            def replay(E, ops=ops):
                for f in ops:
                    f(E, S)
            getattr(block, eng)(replay)
    return nc



def _rope_tables():
    t = np.arange(T, dtype=np.float64)
    out = {}
    inv = 1.0 / 10000.0 ** (np.arange(0, 128, 2, dtype=np.float64) / 128)
    fr = t[:, None] * inv[None, :]
    pos = np.concatenate([fr, fr], axis=1)
    sc = 128 ** -0.5
    cos, sin = np.cos(pos), np.sin(pos)
    out["rope0_cos_q"] = (cos.T * sc).astype(np.float32)
    out["rope0_sin_q"] = (sin.T * sc).astype(np.float32)
    out["rope0_cos_k"] = cos.T.astype(np.float32)
    out["rope0_sin_k"] = sin.T.astype(np.float32)
    th = 10000.0 ** (-2.0 * np.arange(64, dtype=np.float64) / 128)
    ang = t[:, None] * th[None, :]
    c1, s1 = np.cos(ang), np.sin(ang)
    cexp = np.repeat(c1, 2, axis=1)
    sexp = np.repeat(s1, 2, axis=1).copy()
    sexp[:, 0::2] *= -1.0
    sc1 = 128 ** -0.25
    out["rope1_cos_q"] = (cexp.T * (128 ** -0.5)).astype(np.float32)
    out["rope1_sin_q"] = (sexp.T * (128 ** -0.5)).astype(np.float32)
    out["rope1_cos_k"] = cexp.T.astype(np.float32)
    out["rope1_sin_k"] = sexp.T.astype(np.float32)
    return out


def _causal():
    m = np.full((128, 2048), NEG, np.float32)
    p = np.arange(128)[:, None]
    c = np.arange(512)[None, :]
    for v in range(4):
        m[:, v * 512:(v + 1) * 512] = np.where(c <= 128 * v + p, 0.0, NEG)
    return m


def _perms():
    p0 = np.zeros((128, 128), np.float32)
    for p in range(64):
        p0[p + 64, p] = -1.0
        p0[p, p + 64] = 1.0
    p1 = np.zeros((128, 128), np.float32)
    for i in range(64):
        p1[2 * i + 1, 2 * i] = 1.0
        p1[2 * i, 2 * i + 1] = 1.0
    return (p0.astype(ml_dtypes.bfloat16), p1.astype(ml_dtypes.bfloat16))


def kernel(**inputs):
    x = np.asarray(inputs["L0_attn_x"])
    consts = _rope_tables()
    perm0, perm1 = _perms()
    ident = np.eye(128).astype(ml_dtypes.bfloat16)
    causal = _causal()

    all_subs = [s for launch in LAUNCHES for s in launch]
    cur = [np.ascontiguousarray(x[c].T) for c in range(NCORES)]

    for launch in LAUNCHES:
        nc = build_program(launch)
        need = set(launch)
        in_maps = []
        for c in range(NCORES):
            im = {"x_in": cur[c]}
            if {"ffn0_a", "ffn0_b"} & need:
                im["L0_ffn_lut"] = np.asarray(inputs["L0_ffn_lut"]).reshape(1, 16).astype(np.float32)
                im["L0_ffn_gate_idx"] = np.asarray(inputs["L0_ffn_gate_idx"])
                im["L0_ffn_up_idx"] = np.asarray(inputs["L0_ffn_up_idx"])
                im["L0_ffn_down_idx"] = np.asarray(inputs["L0_ffn_down_idx"])
            if "attn0" in need:
                im["L0_attn_Wq"] = np.asarray(inputs["L0_attn_Wq"])
                im["L0_attn_Wkv"] = np.asarray(inputs["L0_attn_Wkv"])
                im["L0_attn_Wout"] = np.asarray(inputs["L0_attn_Wout"])
                for nm in ("rope0_cos_q", "rope0_sin_q", "rope0_cos_k", "rope0_sin_k"):
                    im[nm] = consts[nm]
            if "attn1" in need:
                for nm in ("L1_attn_Wq", "L1_attn_Wk", "L1_attn_Wv", "L1_attn_Wo"):
                    im[nm] = np.asarray(inputs[nm])
                for nm in ("rope1_cos_q", "rope1_sin_q", "rope1_cos_k", "rope1_sin_k"):
                    im[nm] = consts[nm]
            if {"moe_a", "moe_b"} & need:
                for nm in ("L1_ffn_router", "L1_ffn_w_up_gate", "L1_ffn_w_down",
                           "L1_ffn_shared_w_up_gate", "L1_ffn_shared_w_down"):
                    im[nm] = np.asarray(inputs[nm])
                im["rank_mask"] = (np.arange(8) < c).astype(np.float32).reshape(8, 1)
                im["identf"] = np.eye(128, dtype=np.float32)
            if {"attn0", "attn1"} & need:
                im["ident"] = ident
                im["perm0"] = perm0
                im["perm1"] = perm1
                im["causal"] = causal
                im["identf"] = np.eye(128, dtype=np.float32)
            in_maps.append(im)
        res = run_bass_kernel_spmd(nc, in_maps, core_ids=list(range(NCORES)))
        if any(os.environ.get(k) == "1" for k in ("DBG_ATTN", "DBG_GHOST", "DBG_LN", "DBG_MOE")):
            kernel.dbg = res.results
        cur = [np.asarray(r["x_out"]) for r in res.results]

    out = np.stack([c.T for c in cur], axis=0).astype(np.float32)
    return np.ascontiguousarray(out)
